# version 11
# baseline (speedup 1.0000x reference)
import contextlib
import numpy as np
import ml_dtypes
import concourse.bass as bass
import concourse.mybir as mybir
from concourse.bass_utils import run_bass_kernel_spmd

F32 = mybir.dt.float32
BF16 = mybir.dt.bfloat16
F16 = mybir.dt.float16
F8 = mybir.dt.float8e4
I32 = mybir.dt.int32
AF = mybir.ActivationFunctionType
ALU = mybir.AluOpType

NCORES = 8
BC = 4
S = 2048
D = 1024
DH = 2048
DI = 1024
NB = 16
H = 4096
DMOD = 6144
T = 256
NCH = S // T
EPS = 1e-5
PI = float(np.pi)


class Prog:
    ENGS = ("sync", "gpsimd", "vector", "scalar", "tensor")

    def __init__(self, sems):
        self.ops = {e: [] for e in self.ENGS}
        self.sems = sems
        self.cnt = {k: 0 for k in sems}

    AUTODRAIN = ("vector", "scalar")

    def op(self, eng, fn, drain=None):
        if drain is None:
            drain = eng in self.AUTODRAIN
        if drain:
            self.ops[eng].append(lambda e: e.drain())
        self.ops[eng].append(fn)

    def opi(self, eng, sem, fn, n=1, drain=None):
        h = self.sems[sem]
        self.cnt[sem] += n
        if drain is None:
            drain = eng in self.AUTODRAIN
        if drain:
            self.ops[eng].append(lambda e: e.drain())

        def wrapped(e, fn=fn, h=h, n=n):
            fn(e).then_inc(h, n)

        self.ops[eng].append(wrapped)

    def dma(self, eng, sem, fn):
        self.opi(eng, sem, fn, 16, drain=False)

    def wait(self, eng, sem, val=None):
        h = self.sems[sem]
        v = self.cnt[sem] if val is None else val
        self.ops[eng].append(lambda e, h=h, v=v: e.wait_ge(h, v))

    def mark(self, sem):
        return self.cnt[sem]


def build_program(n_seq=BC, n_chunks=NCH, pre_stage=99):
    nc = bass.Bass("TRN2", target_bir_lowering=False)

    dx = nc.dram_tensor("x", [BC, S, D], F32, kind="ExternalInput").ap()
    dmask = nc.dram_tensor("mask", [BC, S, D], F8, kind="ExternalInput").ap()
    dc = nc.dram_tensor("cvec", [BC, D], F32, kind="ExternalInput").ap()
    dwmod = nc.dram_tensor("wmod", [D, DMOD], F32, kind="ExternalInput").ap()
    dwin = nc.dram_tensor("W_in", [D, DH], F32, kind="ExternalInput").ap()
    dbin = nc.dram_tensor("b_in", [DH], F32, kind="ExternalInput").ap()
    dwgx = nc.dram_tensor("Wg_x", [NB, 128, 64], F32, kind="ExternalInput").ap()
    dbgx = nc.dram_tensor("bg_x", [DI], F32, kind="ExternalInput").ap()
    dwga = nc.dram_tensor("Wg_a", [NB, 128, 64], F32, kind="ExternalInput").ap()
    dbga = nc.dram_tensor("bg_a", [DI], F32, kind="ExternalInput").ap()
    darp = nc.dram_tensor("a_real", [DI], F32, kind="ExternalInput").ap()
    daip = nc.dram_tensor("a_imag", [DI], F32, kind="ExternalInput").ap()
    dwout = nc.dram_tensor("W_out", [DH, D], F32, kind="ExternalInput").ap()
    dbout = nc.dram_tensor("b_out", [D], F32, kind="ExternalInput").ap()
    dw1 = nc.dram_tensor("w1", [H, D], F32, kind="ExternalInput").ap()
    db1 = nc.dram_tensor("b1", [H], F32, kind="ExternalInput").ap()
    dw2 = nc.dram_tensor("w2", [D, H], F32, kind="ExternalInput").ap()
    db2 = nc.dram_tensor("b2", [D], F32, kind="ExternalInput").ap()
    dident = nc.dram_tensor("ident", [128, 128], F32, kind="ExternalInput").ap()
    dramp = nc.dram_tensor("ramp", [128, 2048], F32, kind="ExternalInput").ap()
    dy = nc.dram_tensor("y", [BC, S, D], F32, kind="ExternalOutput").ap()

    dwin_bf = nc.dram_tensor("win_bf", [D, DH], BF16, kind="Internal").ap()
    dx_bf = nc.dram_tensor("x_bf", [BC, S, D], BF16, kind="Internal").ap()
    dwout_bf = nc.dram_tensor("wout_bf", [DH, D], BF16, kind="Internal").ap()

    xr4 = dx.rearrange("b (n p) d -> b n p d", p=128)
    xbf4 = dx_bf.rearrange("b (n p) d -> b n p d", p=128)
    mr4 = dmask.rearrange("b (n p) d -> b n p d", p=128)
    yr4 = dy.rearrange("b (n p) d -> b n p d", p=128)
    winr = dwin.rearrange("(c p) m -> c p m", p=128)
    winbfr = dwin_bf.rearrange("(c p) m -> c p m", p=128)
    woutr = dwout.rearrange("(c p) m -> c p m", p=128)
    woutbfr = dwout_bf.rearrange("(c p) m -> c p m", p=128)
    w1r = dw1.rearrange("(c p) m -> c p m", p=128)
    w2r = dw2.rearrange("(c p) m -> c p m", p=128)
    wmodr = dwmod.rearrange("(c p) m -> c p m", p=128)

    with contextlib.ExitStack() as st:
        def sbt(name, shape, dt):
            return st.enter_context(nc.sbuf_tensor(name, shape, dt))

        def pst(name, shape, dt):
            return st.enter_context(nc.psum_tensor(name, shape, dt))

        w1T = sbt("w1T", [128, 8 * H], F8)
        w2T = sbt("w2T", [128, 32 * D], F8)
        wgx = sbt("wgx", [128, NB * 64], BF16)
        wga = sbt("wga", [128, NB * 64], BF16)
        identbf = sbt("identbf", [128, 128], BF16)
        identf = sbt("identf", [128, 128], F32)
        ones128 = sbt("ones128", [128, 128], BF16)
        rampbf = sbt("rampbf", [128, 2048], BF16)
        onesbf = sbt("onesbf", [1, 128], BF16)

        bcol_in = sbt("bcol_in", [128, 16], F32)
        b1q_col = sbt("b1q_col", [128, 32], F32)
        Lh_col = sbt("Lh_col", [128, 8], F32)
        L2_col = sbt("L2_col", [128, 8], F32)
        aimh_col = sbt("aimh_col", [128, 8], F32)
        bgx2_col = sbt("bgx2_col", [128, 8], F32)
        bga2_col = sbt("bga2_col", [128, 8], F32)
        b2q_row = sbt("b2q_row", [1, D], BF16)
        bout_row = sbt("bout_row", [1, D], BF16)

        mod_cols = sbt("mod_cols", [128, 192], F32)
        s1a_col = sbt("s1a_col", [128, 32], F32)
        sa_col = sbt("sa_col", [128, 32], F32)
        s1b_col = sbt("s1b_col", [128, 32], F32)
        sbb_col = sbt("sbb_col", [128, 32], F32)
        gacl_col = sbt("gacl_col", [128, 32], F32)
        gbcl_col = sbt("gbcl_col", [128, 32], F32)
        gab = sbt("gab", [128, D], BF16)
        gbb = sbt("gbb", [128, D], BF16)
        cfm = sbt("cfm", [128, 32], BF16)
        msc = sbt("msc", [128, 8], F32)
        cbias = sbt("cbias", [128, 4], F32)

        winb = sbt("winb", [128, 2 * DH], BF16)
        woutb = sbt("woutb", [128, 4 * D], BF16)

        xkeep = [sbt("xkeep0", [128, 2 * D], BF16), sbt("xkeep1", [128, 2 * D], BF16)]
        smaskp = [sbt("smask0", [128, 2 * D], F8), sbt("smask1", [128, 2 * D], F8)]
        sstat = sbt("sstat", [128, 24], F32)
        sstat2 = sbt("sstat2", [128, 24], F32)
        saggr = sbt("saggr", [128, 8], F32)
        srow = sbt("srow", [128, 8], F32)
        n1 = sbt("n1", [128, 2 * D], BF16)
        htld = sbt("htld", [128, DH], BF16)
        xh_r = sbt("xh_r", [128, 2048], BF16)
        xh_i = sbt("xh_i", [128, 2048], BF16)
        tnx = sbt("tnx", [128, 2048], BF16)
        tna = sbt("tna", [128, 2048], BF16)
        gx = sbt("gx", [128, 2048], BF16)
        Gt = sbt("Gt", [128, 2048], F32)
        cl = sbt("cl", [128, 2048], F32)
        thr = sbt("thr", [128, 2048], F16)
        riw = sbt("riw", [128, 1024], I32)
        nrm = sbt("nrm", [128, 2048], BF16)
        epos = sbt("epos", [128, 2048], BF16)
        eneg = sbt("eneg", [128, 2048], BF16)
        xtr = sbt("xtr", [128, 2048], BF16)
        xti = sbt("xti", [128, 2048], BF16)
        xsc = sbt("xsc", [128, 2048], BF16)
        Srt = sbt("Srt", [128, 2048], BF16)
        Sit = sbt("Sit", [128, 2048], BF16)
        hcarry = sbt("hcarry", [128, 16], BF16)
        xmid = sbt("xmid", [128, 2 * D], BF16)
        h2q = sbt("h2q", [128, 2048], F8)
        r1q = sbt("r1q", [128, 2048], F8)
        y16 = sbt("y16", [128, 2 * D], F16)

        PA = pst("PA", [128, 1024], F32)
        PB = pst("PB", [128, 1024], F32)
        PC = pst("PC", [128, 1024], F32)
        PE2 = pst("PE2", [128, 512], F32)
        PD = pst("PD", [128, 1024], BF16)

        sems = {}
        for name in ("qxa", "qxb", "qma", "qmb", "qo", "qs", "qwa", "qwb", "qva", "qvb", "spe", "sdv", "ssc", "spo"):
            sems[name] = st.enter_context(nc.semaphore(name))
        block = st.enter_context(nc.Block())
        P = Prog(sems)

        def tsl(tile, c, w=256):
            return tile[:, c * w:(c + 1) * w]

        def pre():
            P.dma("gpsimd", "qo", lambda e: e.dma_start(out=identf[:], in_=dident))
            P.dma("gpsimd", "qo", lambda e: e.dma_start(out=Gt[:], in_=dramp))
            P.op("vector", lambda e: e.memset(ones128[:], 1.0))
            P.op("vector", lambda e: e.memset(onesbf[:], 1.0))
            P.op("vector", lambda e: e.memset(cbias[:, 0:1], EPS))
            P.op("vector", lambda e: e.memset(cbias[:, 1:2], 1.0))
            P.op("vector", lambda e: e.memset(cbias[:, 2:3], PI / 2))

            P.wait("vector", "qo")
            P.op("vector", lambda e: e.tensor_copy(identbf[:], identf[:]))
            P.opi("vector", "sdv", lambda e: e.tensor_copy(rampbf[:], Gt[:]))
            if pre_stage < 1:
                return
            P.wait("gpsimd", "sdv")
            P.dma("gpsimd", "qo", lambda e: e.dma_start(
                out=cl[0:16, 1024:1152], in_=dbin.rearrange("(a p) -> a p", p=128)))
            P.dma("gpsimd", "qo", lambda e: e.dma_start(
                out=cl[32:64, 1024:1152], in_=db1.rearrange("(a p) -> a p", p=128)))
            P.dma("gpsimd", "qo", lambda e: e.dma_start(
                out=cl[64:72, 1024:1152], in_=daip.rearrange("(a p) -> a p", p=128)))
            P.dma("gpsimd", "qo", lambda e: e.dma_start(
                out=Gt[0:8, 1024:1152], in_=dbgx.rearrange("(a p) -> a p", p=128)))
            P.dma("gpsimd", "qo", lambda e: e.dma_start(
                out=Gt[32:40, 1024:1152], in_=dbga.rearrange("(a p) -> a p", p=128)))
            P.dma("gpsimd", "qo", lambda e: e.dma_start(
                out=Gt[64:72, 1024:1152], in_=darp.rearrange("(a p) -> a p", p=128)))
            P.dma("gpsimd", "qo", lambda e: e.dma_start(
                out=cl[0:1, 0:1024], in_=db2.rearrange("(o d) -> o d", o=1)))

            P.wait("vector", "qo")
            P.op("vector", lambda e: e.tensor_copy(r1q[32:64, 0:128], cl[32:64, 1024:1152]))
            P.op("vector", lambda e: e.tensor_copy(cl[32:64, 1024:1152], r1q[32:64, 0:128]))
            P.op("vector", lambda e: e.tensor_copy(r1q[0:1, 0:1024], cl[0:1, 0:1024]))
            P.opi("vector", "sdv", lambda e: e.tensor_copy(b2q_row[:], r1q[0:1, 0:1024]))
            P.wait("gpsimd", "sdv")
            P.dma("gpsimd", "qo", lambda e: e.dma_start(
                out=cl[0:1, 0:1024], in_=dbout.rearrange("(o d) -> o d", o=1)))
            P.wait("vector", "qo")
            P.opi("vector", "sdv", lambda e: e.tensor_copy(bout_row[:], cl[0:1, 0:1024]))
            P.opi("vector", "sdv", lambda e: e.tensor_scalar(
                cl[64:72, 1152:1280], cl[64:72, 1024:1152], 0.5 * PI, None, ALU.mult, ALU.bypass))
            P.wait("scalar", "sdv")
            P.op("scalar", lambda e: e.activation(out=Gt[64:72, 1152:1280], in_=Gt[64:72, 1024:1152], func=AF.Abs))
            P.op("scalar", lambda e: e.activation(out=Gt[64:72, 1280:1408], in_=Gt[64:72, 1152:1280], func=AF.Exp, scale=-1.0))
            P.op("scalar", lambda e: e.activation(out=Gt[64:72, 1152:1280], in_=Gt[64:72, 1280:1408], func=AF.Ln, bias=cbias[64:72, 1:2]))
            P.opi("scalar", "ssc", lambda e: e.activation(out=Gt[64:72, 1280:1408], in_=Gt[64:72, 1024:1152], func=AF.Relu))
            P.wait("vector", "ssc")
            P.opi("vector", "sdv", lambda e: e.tensor_tensor(
                Gt[64:72, 1408:1536], Gt[64:72, 1152:1280], Gt[64:72, 1280:1408], ALU.add))
            P.wait("scalar", "sdv")
            P.opi("scalar", "ssc", lambda e: e.activation(out=Gt[64:72, 1152:1280], in_=Gt[64:72, 1408:1536], func=AF.Ln))
            P.wait("vector", "ssc")
            P.op("vector", lambda e: e.tensor_scalar(
                Gt[64:72, 1152:1280], Gt[64:72, 1152:1280], 0.5, None, ALU.mult, ALU.bypass))
            P.op("vector", lambda e: e.tensor_scalar(
                Gt[0:8, 1152:1280], Gt[0:8, 1024:1152], 0.5, None, ALU.mult, ALU.bypass))
            P.opi("vector", "sdv", lambda e: e.tensor_scalar(
                Gt[32:40, 1152:1280], Gt[32:40, 1024:1152], 0.5, None, ALU.mult, ALU.bypass))
            P.wait("tensor", "sdv")
            P.op("tensor", lambda e: e.transpose(PA[:, 0:128], cl[:, 1024:1152], identf[:]))
            P.op("tensor", lambda e: e.transpose(PA[:, 128:256], cl[:, 1152:1280], identf[:]))
            P.opi("tensor", "spe", lambda e: e.transpose(PA[:, 256:384], Gt[:, 1152:1280], identf[:]))
            P.wait("vector", "spe")
            P.op("vector", lambda e: e.tensor_copy(bcol_in[:], PA[:, 0:16]))
            P.op("vector", lambda e: e.tensor_copy(b1q_col[:], PA[:, 32:64]))
            P.op("vector", lambda e: e.tensor_copy(aimh_col[:], PA[:, 192:200]))
            P.op("vector", lambda e: e.tensor_copy(bgx2_col[:], PA[:, 256:264]))
            P.op("vector", lambda e: e.tensor_copy(bga2_col[:], PA[:, 288:296]))
            P.op("vector", lambda e: e.tensor_copy(Lh_col[:], PA[:, 320:328]))
            P.opi("vector", "sdv", lambda e: e.tensor_scalar(L2_col[:], PA[:, 320:328], 2.0, None, ALU.mult, ALU.bypass))

            if pre_stage < 2:
                return
            P.wait("gpsimd", "sdv")
            P.dma("gpsimd", "qo", lambda e: e.dma_start(
                out=cl[:, 0:1024].rearrange("p (n j) -> p n j", j=64),
                in_=dwgx.rearrange("n h j -> h n j")))
            P.dma("gpsimd", "qo", lambda e: e.dma_start(
                out=Gt[:, 0:1024].rearrange("p (n j) -> p n j", j=64),
                in_=dwga.rearrange("n h j -> h n j")))
            P.wait("vector", "qo")
            P.op("vector", lambda e: e.tensor_copy(wgx[:], cl[:, 0:1024]))
            P.opi("vector", "sdv", lambda e: e.tensor_copy(wga[:], Gt[:, 0:1024]))

            if pre_stage < 3:
                return
            for c in range(8):
                P.wait("gpsimd", "sdv")
                P.dma("gpsimd", "qo", lambda e, c=c: e.dma_start(out=Gt[:], in_=winr[c]))
                P.wait("vector", "qo")
                P.opi("vector", "sdv", lambda e: e.tensor_copy(winb[:, 0:2048], Gt[:]))
                P.wait("gpsimd", "sdv")
                P.dma("gpsimd", "qs", lambda e, c=c: e.dma_start(out=winbfr[c], in_=winb[:, 0:2048]))
                P.wait("vector", "qs")
            if pre_stage < 4:
                return
            for c in range(16):
                P.wait("gpsimd", "sdv")
                P.dma("gpsimd", "qo", lambda e, c=c: e.dma_start(out=Gt[:, 0:1024], in_=woutr[c]))
                P.wait("vector", "qo")
                P.opi("vector", "sdv", lambda e: e.tensor_copy(winb[:, 0:1024], Gt[:, 0:1024]))
                P.wait("gpsimd", "sdv")
                P.dma("gpsimd", "qs", lambda e, c=c: e.dma_start(out=woutbfr[c], in_=winb[:, 0:1024]))
                P.wait("vector", "qs")

            if pre_stage < 5:
                return
            for hh in range(32):
                P.wait("gpsimd", "sdv")
                P.dma("gpsimd", "qo", lambda e, hh=hh: e.dma_start(out=Gt[:, 0:1024], in_=w1r[hh]))
                P.wait("vector", "qo")
                P.op("vector", lambda e: e.tensor_copy(r1q[:, 0:1024], Gt[:, 0:1024]))
                P.opi("vector", "sdv", lambda e: e.tensor_copy(xh_r[:, 0:1024], r1q[:, 0:1024]))
                P.wait("tensor", "sdv")
                for c in range(8):
                    fn = lambda e, c=c: e.transpose(
                        PD[:, c * 128:(c + 1) * 128], xh_r[:, c * 128:(c + 1) * 128], identbf[:])
                    (P.opi("tensor", "spe", fn) if c == 7 else P.op("tensor", fn))
                P.wait("vector", "spe")
                for c in range(8):
                    fn = lambda e, c=c, hh=hh: e.tensor_copy(
                        w1T[:, c * H + hh * 128: c * H + (hh + 1) * 128], PD[:, c * 128:(c + 1) * 128])
                    (P.opi("vector", "sdv", fn) if c == 7 else P.op("vector", fn))
                P.wait("tensor", "sdv")

            if pre_stage < 6:
                return
            for c in range(8):
                P.wait("gpsimd", "sdv")
                P.dma("gpsimd", "qo", lambda e, c=c: e.dma_start(out=Gt[:], in_=w2r[c][:, 0:2048]))
                P.dma("gpsimd", "qo", lambda e, c=c: e.dma_start(out=cl[:], in_=w2r[c][:, 2048:4096]))
                P.wait("vector", "qo")
                for half in range(2):
                    src = Gt if half == 0 else cl
                    P.op("vector", lambda e, src=src: e.tensor_copy(r1q[:], src[:]))
                    P.opi("vector", "sdv", lambda e: e.tensor_copy(xh_r[:], r1q[:]))
                    P.wait("tensor", "sdv")
                    for j2 in range(2):
                        for jj in range(8):
                            j = j2 * 8 + jj
                            fn = lambda e, j=j, jj=jj: e.transpose(
                                PD[:, jj * 128:(jj + 1) * 128], xh_r[:, j * 128:(j + 1) * 128], identbf[:])
                            (P.opi("tensor", "spe", fn) if jj == 7 else P.op("tensor", fn))
                        P.wait("vector", "spe")
                        for jj in range(8):
                            hh = half * 16 + j2 * 8 + jj
                            fn = lambda e, jj=jj, hh=hh, c=c: e.tensor_copy(
                                w2T[:, hh * D + c * 128: hh * D + (c + 1) * 128], PD[:, jj * 128:(jj + 1) * 128])
                            (P.opi("vector", "sdv", fn) if jj == 7 else P.op("vector", fn))
                        P.wait("tensor", "sdv")

            if pre_stage < 7:
                return
            for i in range(32):
                bb, nn_ = i // 8, (i % 8) * 2
                P.wait("gpsimd", "sdv")
                P.dma("gpsimd", "qo", lambda e, bb=bb, nn_=nn_: e.dma_start(out=Gt[:, 0:1024], in_=xr4[bb, nn_]))
                P.dma("gpsimd", "qo", lambda e, bb=bb, nn_=nn_: e.dma_start(out=cl[:, 0:1024], in_=xr4[bb, nn_ + 1]))
                P.wait("vector", "qo")
                P.op("vector", lambda e: e.tensor_copy(winb[:, 0:1024], Gt[:, 0:1024]))
                P.opi("vector", "sdv", lambda e: e.tensor_copy(winb[:, 1024:2048], cl[:, 0:1024]), drain=False)
                P.wait("gpsimd", "sdv")
                P.dma("gpsimd", "qs", lambda e, bb=bb, nn_=nn_: e.dma_start(out=xbf4[bb, nn_], in_=winb[:, 0:1024]))
                P.dma("gpsimd", "qs", lambda e, bb=bb, nn_=nn_: e.dma_start(out=xbf4[bb, nn_ + 1], in_=winb[:, 1024:2048]))
                P.wait("vector", "qs")

            P.wait("gpsimd", "spe")
            P.wait("gpsimd", "sdv")
            P.dma("gpsimd", "qo", lambda e: e.dma_start(out=cl[0:4, 0:1024], in_=dc))
            P.wait("scalar", "qo")
            P.opi("scalar", "ssc", lambda e: e.activation(out=Gt[0:4, 0:1024], in_=cl[0:4, 0:1024], func=AF.Silu))
            P.wait("tensor", "ssc")
            for c8 in range(8):
                fn = lambda e, c8=c8: e.transpose(
                    PB[:, c8 * 128:(c8 + 1) * 128], Gt[:, c8 * 128:(c8 + 1) * 128], identf[:])
                (P.opi("tensor", "spe", fn) if c8 == 7 else P.op("tensor", fn))
            P.wait("vector", "spe")
            for c8 in range(8):
                fn = lambda e, c8=c8: e.tensor_copy(
                    cfm[:, c8 * 4:(c8 + 1) * 4], PB[:, c8 * 128: c8 * 128 + 4])
                (P.opi("vector", "sdv", fn) if c8 == 7 else P.op("vector", fn))
            for d8 in range(8):
                for j3 in range(3):
                    P.wait("gpsimd", "sdv")
                    P.dma("gpsimd", "qo", lambda e, d8=d8, j3=j3: e.dma_start(
                        out=Gt[:], in_=wmodr[d8][:, j3 * 2048:(j3 + 1) * 2048]))
                    P.wait("vector", "qo")
                    P.wait("vector", "spe")
                    P.opi("vector", "sdv", lambda e: e.tensor_copy(xh_r[:], Gt[:]))
                    P.wait("tensor", "sdv")
                    for mcl in range(16):
                        gmc = j3 * 16 + mcl
                        fn = lambda e, gmc=gmc, mcl=mcl, d8=d8, j3=j3: e.matmul(
                            PE2[:, gmc * 4:(gmc + 1) * 4],
                            xh_r[:, mcl * 128:(mcl + 1) * 128],
                            cfm[:, d8 * 4:(d8 + 1) * 4],
                            start=(d8 == 0 and j3 == 0 and mcl == 0),
                            stop=(d8 == 7), skip_group_check=True)
                        (P.opi("tensor", "spe", fn) if mcl == 15 else P.op("tensor", fn))
            P.wait("vector", "spe")
            P.opi("vector", "sdv", lambda e: e.tensor_copy(mod_cols[:], PE2[:, 0:192]))

            for b in range(BC):
                P.op("vector", lambda e, b=b: e.tensor_scalar(
                    s1a_col[:, b * 8:(b + 1) * 8], mod_cols[:, 32 + b:64:4], 1.0, None, ALU.add, ALU.bypass))
                P.op("vector", lambda e, b=b: e.tensor_copy(
                    sa_col[:, b * 8:(b + 1) * 8], mod_cols[:, 0 + b:32:4]))
                P.op("vector", lambda e, b=b: e.tensor_scalar(
                    s1b_col[:, b * 8:(b + 1) * 8], mod_cols[:, 128 + b:160:4], 1.0, None, ALU.add, ALU.bypass))
                P.op("vector", lambda e, b=b: e.tensor_copy(
                    sbb_col[:, b * 8:(b + 1) * 8], mod_cols[:, 96 + b:128:4]))
                for (src0, dst) in ((64, gacl_col), (160, gbcl_col)):
                    P.op("vector", lambda e, b=b, src0=src0: e.tensor_scalar(
                        msc[:], mod_cols[:, src0 + b:src0 + 32:4], 0.0, None, ALU.is_ge, ALU.bypass))
                    P.op("vector", lambda e: e.tensor_scalar(
                        msc[:], msc[:], 2.0, -1.0, ALU.mult, ALU.add))
                    P.op("vector", lambda e, b=b, src0=src0: e.tensor_scalar(
                        srow[:], mod_cols[:, src0 + b:src0 + 32:4], -1.0, None, ALU.mult, ALU.bypass))
                    P.op("vector", lambda e, b=b, src0=src0, dst=dst: e.tensor_tensor(
                        dst[:, b * 8:(b + 1) * 8], mod_cols[:, src0 + b:src0 + 32:4],
                        srow[:], ALU.max))
                    P.op("vector", lambda e, b=b, dst=dst: e.tensor_scalar(
                        dst[:, b * 8:(b + 1) * 8], dst[:, b * 8:(b + 1) * 8],
                        0.25, None, ALU.max, ALU.bypass))
                    P.opi("vector", "sdv", lambda e, b=b, dst=dst: e.tensor_tensor(
                        dst[:, b * 8:(b + 1) * 8], dst[:, b * 8:(b + 1) * 8], msc[:], ALU.mult))

        def batch_setup(b):
            P.wait("vector", "spo")
            for (colt, dst) in ((gacl_col, gab), (gbcl_col, gbb)):
                for c8 in range(8):
                    fn = lambda e, colt=colt, b=b, c8=c8: e.tensor_scalar(
                        xsc[:, c8 * 128:(c8 + 1) * 128], ones128[:],
                        colt[:, b * 8 + c8:b * 8 + c8 + 1], None, ALU.mult, ALU.bypass)
                    (P.opi("vector", "sdv", fn, drain=(c8 == 0)) if c8 == 7
                     else P.op("vector", fn, drain=(c8 == 0)))
                P.wait("tensor", "sdv")
                for c8 in range(8):
                    fn = lambda e, c8=c8: e.transpose(
                        PD[:, c8 * 128:(c8 + 1) * 128], xsc[:, c8 * 128:(c8 + 1) * 128], identbf[:])
                    (P.opi("tensor", "spe", fn) if c8 == 7 else P.op("tensor", fn))
                P.wait("vector", "spe")
                P.opi("vector", "sdv", lambda e, dst=dst: e.tensor_copy(dst[:], PD[:]), drain=True)
                P.wait("tensor", "sdv")

        NK = n_seq * n_chunks
        M = [dict() for _ in range(NK + 2)]
        oslots = (PC[:, 0:512], PC[:, 512:1024], PE2[:], PA[:, 0:512])
        f2slots = (PB[:, 0:512], PB[:, 512:1024], PC[:, 0:512], PC[:, 512:1024])

        def dma_x(k):
            b, kk = divmod(k, n_chunks)
            par = k % 2
            qx = "qxa" if par == 0 else "qxb"
            P.wait("sync", "sdv")
            P.dma("sync", qx, lambda e, b=b, kk=kk, par=par: e.dma_start(
                out=xkeep[par][:, 0:D], in_=xbf4[b, kk * 2]))
            P.dma("sync", qx, lambda e, b=b, kk=kk, par=par: e.dma_start(
                out=xkeep[par][:, D:], in_=xbf4[b, kk * 2 + 1]))
            M[k]["qx"] = (qx, P.mark(qx))

        def dma_mask(k, gate):
            b, kk = divmod(k, n_chunks)
            par = k % 2
            qm = "qma" if par == 0 else "qmb"
            if gate is not None:
                P.wait("sync", "spo", gate)
            P.dma("sync", qm, lambda e, b=b, kk=kk, par=par: e.dma_start(
                out=smaskp[par][:, 0:D], in_=mr4[b, kk * 2]))
            P.dma("sync", qm, lambda e, b=b, kk=kk, par=par: e.dma_start(
                out=smaskp[par][:, D:], in_=mr4[b, kk * 2 + 1]))
            M[k]["qm"] = (qm, P.mark(qm))

        def dve_stats_ln1(k):
            par = k % 2
            P.wait("vector", M[k]["qx"][0], M[k]["qx"][1])
            for s in range(2):
                P.op("vector", lambda e, s=s, par=par: e.bn_stats(
                    sstat2[:, s * 12:s * 12 + 6], xkeep[par][:, s * D:s * D + 512]), drain=(s == 0))
                P.op("vector", lambda e, s=s, par=par: e.bn_stats(
                    sstat2[:, s * 12 + 6:s * 12 + 12], xkeep[par][:, s * D + 512:(s + 1) * D]), drain=False)
            for s in range(2):
                fn = lambda e, s=s: e.bn_aggr(saggr[:, s * 2:s * 2 + 2], sstat2[:, s * 12:s * 12 + 12])
                (P.opi("vector", "sdv", fn, drain=False) if s == 1 else P.op("vector", fn, drain=True))
            M[k]["stats"] = P.mark("sdv")

        def sc_ln1_rows(k):
            P.wait("scalar", "sdv", M[k]["stats"])
            for s in range(2):
                P.op("scalar", lambda e, s=s: e.activation(
                    out=srow[:, s:s + 1], in_=saggr[:, s * 2 + 1:s * 2 + 2], func=AF.Ln, bias=cbias[:, 0:1]))
                fn = lambda e, s=s: e.activation(
                    out=srow[:, s:s + 1], in_=srow[:, s:s + 1], func=AF.Exp, scale=-0.5)
                (P.opi("scalar", "ssc", fn) if s == 1 else P.op("scalar", fn))
            M[k]["ln1rows"] = P.mark("ssc")

        def dve_n1(k, trans_mark):
            par = k % 2
            P.wait("vector", "ssc", M[k]["ln1rows"])
            if trans_mark is not None:
                P.wait("vector", "spe", trans_mark)
            for s in range(2):
                P.op("vector", lambda e, s=s: e.tensor_tensor(
                    srow[:, 2 + s:3 + s], saggr[:, s * 2:s * 2 + 1], srow[:, s:s + 1], ALU.mult),
                    drain=(s == 0))
                fn = lambda e, s=s, par=par: e.tensor_scalar(
                    n1[:, s * D:(s + 1) * D], xkeep[par][:, s * D:(s + 1) * D],
                    srow[:, s:s + 1], srow[:, 2 + s:3 + s], ALU.mult, ALU.subtract)
                (P.opi("vector", "sdv", fn, drain=True) if s == 1 else P.op("vector", fn, drain=True))
            M[k]["n1"] = P.mark("sdv")

        def front(k):
            b, kk = divmod(k, n_chunks)
            P.wait("tensor", "sdv", M[k]["n1"])
            for grp in range(2):
                if grp == 1:
                    P.wait("tensor", "sdv")
                for s in range(2):
                    for cc in range(4):
                        c = grp * 4 + cc
                        fn = lambda e, s=s, c=c, cc=cc: e.transpose(
                            PD[:, cc * 256 + s * 128: cc * 256 + s * 128 + 128],
                            n1[:, s * D + c * 128: s * D + (c + 1) * 128], identbf[:])
                        (P.opi("tensor", "spe", fn) if (s == 1 and cc == 3) else P.op("tensor", fn))
                P.wait("vector", "spe")
                for cc in range(4):
                    c = grp * 4 + cc
                    fn = lambda e, c=c, cc=cc, b=b: e.tensor_scalar(
                        htld[:, c * 256: (c + 1) * 256],
                        PD[:, cc * 256: (cc + 1) * 256],
                        s1a_col[:, b * 8 + c:b * 8 + c + 1],
                        sa_col[:, b * 8 + c:b * 8 + c + 1],
                        ALU.mult, ALU.add)
                    (P.opi("vector", "sdv", fn, drain=(cc == 0)) if cc == 3
                     else P.op("vector", fn, drain=(cc == 0)))
            M[k]["trans"] = P.mark("spe")
            M[k]["htld"] = P.mark("sdv")

            win_marks = {}
            for ph in range(2):
                for dd in range(8):
                    i = ph * 8 + dd
                    slot = i % 2
                    wsem = "qwa" if slot == 0 else "qwb"
                    if i >= 2:
                        P.wait("sync", "spe", win_marks[i - 2])
                    else:
                        P.wait("sync", "spe")
                    P.dma("sync", wsem, lambda e, dd=dd, slot=slot: e.dma_start(
                        out=winb[:, slot * 2048:(slot + 1) * 2048], in_=winbfr[dd]))
                    qmk = P.mark(wsem)
                    P.wait("tensor", wsem, qmk)
                    if i == 0:
                        P.wait("tensor", "sdv", M[k]["htld"])
                    if i == 8:
                        P.wait("tensor", "sdv")
                    for mm_ in range(8):
                        m = ph * 8 + mm_
                        pt = PA if mm_ < 4 else PB
                        fn = lambda e, slot=slot, m=m, mm_=mm_, dd=dd, pt=pt: e.matmul(
                            pt[:, (mm_ % 4) * 256:(mm_ % 4 + 1) * 256],
                            winb[:, slot * 2048 + m * 128: slot * 2048 + (m + 1) * 128],
                            tsl(htld, dd), start=(dd == 0 and mm_ % 2 == 0),
                            stop=(dd == 7), skip_group_check=True)
                        (P.opi("tensor", "spe", fn) if mm_ == 7 else P.op("tensor", fn))
                    win_marks[i] = P.mark("spe")
                P.wait("vector", "spe")
                for mm_ in range(8):
                    m = ph * 8 + mm_
                    dst = xh_r if m < 8 else xh_i
                    pt = PA if mm_ < 4 else PB
                    fn = lambda e, m=m, mm_=mm_, dst=dst, pt=pt: e.tensor_scalar(
                        dst[:, (m % 8) * 256:(m % 8 + 1) * 256],
                        pt[:, (mm_ % 4) * 256:(mm_ % 4 + 1) * 256],
                        bcol_in[:, m:m + 1], None, ALU.add, ALU.bypass)
                    (P.opi("vector", "sdv", fn, drain=(mm_ == 0)) if mm_ == 7
                     else P.op("vector", fn, drain=(mm_ == 0)))
            M[k]["xh"] = P.mark("sdv")

            P.wait("tensor", "sdv", M[k]["xh"])
            for rnd in range(4):
                gate = rnd // 2
                half = rnd % 2
                wg = wgx if gate == 0 else wga
                dstt = tnx if gate == 0 else tna
                bcolt = bgx2_col if gate == 0 else bga2_col
                if rnd > 0:
                    P.wait("tensor", "ssc")
                for nn in range(8):
                    n = half * 8 + nn
                    cch = n // 2
                    pbase = (n % 2) * 64
                    srcx = xh_r if n < 8 else xh_i
                    fn = lambda e, n=n, cch=cch, pbase=pbase, srcx=srcx, wg=wg: e.matmul(
                        PA[pbase:pbase + 64, (cch % 4) * 256:(cch % 4 + 1) * 256],
                        wg[:, n * 64:(n + 1) * 64],
                        srcx[:, (n % 8) * 256:(n % 8 + 1) * 256],
                        start=True, stop=True)
                    (P.opi("tensor", "spe", fn) if nn == 7 else P.op("tensor", fn))
                P.wait("scalar", "spe")
                for cc in range(4):
                    cch = half * 4 + cc
                    fn = lambda e, cc=cc, cch=cch, dstt=dstt, bcolt=bcolt: e.activation(
                        out=dstt[:, cch * 256:(cch + 1) * 256],
                        in_=PA[:, cc * 256:(cc + 1) * 256],
                        func=AF.Tanh, scale=0.5, bias=bcolt[:, cch:cch + 1])
                    (P.opi("scalar", "ssc", fn, drain=(cc == 0)) if cc == 3
                     else P.op("scalar", fn, drain=(cc == 0)))
            M[k]["tanh"] = P.mark("ssc")

        def back_pe_outproj(j, tanh_mark):
            P.wait("tensor", "ssc", tanh_mark)
            P.wait("tensor", "sdv", max(M[j]["h"], M[j - 1]["y16"] if j > 0 else 0))
            for si in range(4):
                dh = si % 2
                P.op("tensor", lambda e, si=si, dh=dh: e.matmul(
                    oslots[si], onesbf[:], bout_row[:, dh * 512:(dh + 1) * 512],
                    start=True, stop=False))
            wout_marks = {}
            for ch in range(16):
                slot = ch % 4
                vsem = "qva" if ch % 2 == 0 else "qvb"
                if ch >= 2:
                    P.wait("sync", "spe", wout_marks[ch - 2])
                else:
                    P.wait("sync", "spe")
                P.dma("sync", vsem, lambda e, ch=ch, slot=slot: e.dma_start(
                    out=woutb[:, slot * D:(slot + 1) * D], in_=woutbfr[ch]))
                qmk = P.mark(vsem)
                P.wait("tensor", vsem, qmk)
                hsrc = xtr if ch < 8 else xti
                for si in range(4):
                    s, dh = si // 2, si % 2
                    fn = lambda e, ch=ch, si=si, s=s, dh=dh, hsrc=hsrc: e.matmul(
                        oslots[si],
                        hsrc[:, (ch % 8) * 256 + s * 128: (ch % 8) * 256 + s * 128 + 128],
                        woutb[:, (ch % 4) * D + dh * 512: (ch % 4) * D + (dh + 1) * 512],
                        start=False, stop=(ch == 15))
                    (P.opi("tensor", "spe", fn) if si == 3 else P.op("tensor", fn))
                wout_marks[ch] = P.mark("spe")
            M[j]["outproj"] = P.mark("spe")

        def back_dve_early(j):
            b, kk = divmod(j, n_chunks)
            par = j % 2
            P.wait("vector", "spe", M[j]["outproj"])
            for si in range(4):
                s, dh = si // 2, si % 2
                P.op("vector", lambda e, si=si, s=s, dh=dh: e.tensor_tensor(
                    xmid[:, s * D + dh * 512: s * D + (dh + 1) * 512],
                    oslots[si], gab[:, dh * 512:(dh + 1) * 512], ALU.mult),
                    drain=(si == 0))
            for s in range(2):
                fn = lambda e, s=s, par=par: e.tensor_tensor(
                    xmid[:, s * D:(s + 1) * D], xmid[:, s * D:(s + 1) * D],
                    xkeep[par][:, s * D:(s + 1) * D], ALU.add)
                (P.opi("vector", "sdv", fn, drain=(s == 0)) if s == 1
                 else P.op("vector", fn, drain=(s == 0)))
            M[j]["xmid"] = P.mark("sdv")
            for s in range(2):
                P.op("vector", lambda e, s=s: e.bn_stats(
                    sstat[:, s * 12:s * 12 + 6], xmid[:, s * D:s * D + 512]), drain=(s == 0))
                P.op("vector", lambda e, s=s: e.bn_stats(
                    sstat[:, s * 12 + 6:s * 12 + 12], xmid[:, s * D + 512:(s + 1) * D]), drain=False)
            for s in range(2):
                fn = lambda e, s=s: e.bn_aggr(saggr[:, 4 + s * 2:6 + s * 2], sstat[:, s * 12:s * 12 + 12])
                (P.opi("vector", "sdv", fn, drain=False) if s == 1 else P.op("vector", fn, drain=True))
            M[j]["ln2stats"] = P.mark("sdv")

        def scan_prep(k):
            P.wait("vector", "ssc", M[k]["tanh"])
            P.wait("vector", "qo")
            P.op("vector", lambda e: e.tensor_scalar(gx[:], tnx[:], 0.5, 0.5, ALU.mult, ALU.add))
            for c in range(8):
                P.op("vector", lambda e, c=c: e.tensor_tensor_scan(
                    tsl(Gt, c), tsl(tna, c), tsl(tna, c), 0.0, ALU.add, ALU.bypass),
                    drain=(c == 0))
            P.op("vector", lambda e: e.tensor_tensor(Gt[:], Gt[:], rampbf[:], ALU.add))
            for c in range(8):
                P.op("vector", lambda e, c=c: e.tensor_scalar(
                    tsl(cl, c), tsl(Gt, c), Lh_col[:, c:c + 1], None, ALU.mult, ALU.bypass),
                    drain=(c == 0))
            for c in range(8):
                P.op("vector", lambda e, c=c: e.tensor_scalar(
                    tsl(Gt, c), tsl(Gt, c), aimh_col[:, c:c + 1], None, ALU.mult, ALU.bypass),
                    drain=(c == 0))
            for hf in range(2):
                gh = Gt[:, hf * 1024:(hf + 1) * 1024]
                xh_ = xsc[:, hf * 1024:(hf + 1) * 1024]
                xt_ = nrm[:, hf * 1024:(hf + 1) * 1024]
                P.op("vector", lambda e, gh=gh: e.tensor_scalar(
                    riw[:], gh, 1.0 / (2 * PI), None, ALU.mult, ALU.bypass))
                P.op("vector", lambda e, gh=gh: e.scalar_tensor_tensor(
                    gh, riw[:], -2 * PI, gh, ALU.mult, ALU.add))
                P.op("vector", lambda e, gh=gh, xh_=xh_: e.tensor_scalar(
                    xh_, gh, PI, None, ALU.is_gt, ALU.bypass))
                P.op("vector", lambda e, gh=gh, xt_=xt_: e.tensor_scalar(
                    xt_, gh, -PI, None, ALU.is_lt, ALU.bypass), drain=False)
                P.op("vector", lambda e, xh_=xh_, xt_=xt_: e.tensor_tensor(
                    xh_, xh_, xt_, ALU.subtract))
                fn = lambda e, gh=gh, xh_=xh_, hf=hf: e.scalar_tensor_tensor(
                    thr[:, hf * 1024:(hf + 1) * 1024], xh_, -2 * PI, gh, ALU.mult, ALU.add)
                (P.opi("vector", "sdv", fn) if hf == 1 else P.op("vector", fn))
            M[k]["thr"] = P.mark("sdv")

        def sc_ln2rows(j):
            P.wait("scalar", "sdv", M[j]["ln2stats"])
            for s in range(2):
                P.op("scalar", lambda e, s=s: e.activation(
                    out=srow[:, 4 + s:5 + s], in_=saggr[:, 5 + s * 2:6 + s * 2], func=AF.Ln, bias=cbias[:, 0:1]))
                fn = lambda e, s=s: e.activation(
                    out=srow[:, 4 + s:5 + s], in_=srow[:, 4 + s:5 + s], func=AF.Exp, scale=-0.5)
                (P.opi("scalar", "ssc", fn) if s == 1 else P.op("scalar", fn))
            M[j]["ln2rows"] = P.mark("ssc")

        def back_dve_n2(j):
            P.wait("vector", "ssc", M[j]["ln2rows"])
            for s in range(2):
                P.op("vector", lambda e, s=s: e.tensor_tensor(
                    srow[:, 6 + s:7 + s], saggr[:, 4 + s * 2:5 + s * 2], srow[:, 4 + s:5 + s], ALU.mult),
                    drain=(s == 0))
                fn = lambda e, s=s: e.tensor_scalar(
                    Srt[:, s * D:(s + 1) * D], xmid[:, s * D:(s + 1) * D],
                    srow[:, 4 + s:5 + s], srow[:, 6 + s:7 + s], ALU.mult, ALU.subtract)
                (P.opi("vector", "sdv", fn, drain=True) if s == 1 else P.op("vector", fn, drain=True))
            M[j]["n2"] = P.mark("sdv")

        def sc_exps(k):
            P.wait("scalar", "sdv", M[k]["thr"])
            P.op("scalar", lambda e: e.activation(out=epos[:], in_=cl[:], func=AF.Exp))
            P.opi("scalar", "ssc", lambda e: e.activation(out=eneg[:], in_=cl[:], func=AF.Exp, scale=-1.0),
                  drain=False)
            for c in range(8):
                fn = lambda e, c=c: e.activation(
                    out=tsl(cl, c), in_=tsl(tna, c), func=AF.Exp,
                    scale=L2_col[:, c:c + 1], bias=L2_col[:, c:c + 1])
                (P.opi("scalar", "ssc", fn, drain=(c == 0)) if c == 7
                 else P.op("scalar", fn, drain=(c == 0)))
            sc_m2 = P.mark("ssc")
            P.wait("vector", "ssc", sc_m2)
            P.op("vector", lambda e: e.tensor_scalar(cl[:], cl[:], -1.0, 1.0, ALU.mult, ALU.add))
            P.opi("vector", "sdv", lambda e: e.tensor_scalar(cl[:], cl[:], 1e-6, None, ALU.max, ALU.bypass))
            dv_w = P.mark("sdv")
            P.wait("scalar", "sdv", dv_w)
            P.op("scalar", lambda e: e.activation(out=Gt[:], in_=cl[:], func=AF.Ln))
            P.opi("scalar", "ssc", lambda e: e.activation(out=nrm[:], in_=Gt[:], func=AF.Exp, scale=0.5))
            M[k]["nrm"] = P.mark("ssc")

        def sc_trig(k):
            P.wait("scalar", "spo")
            P.op("scalar", lambda e: e.activation(out=y16[:], in_=thr[:], func=AF.Abs))
            P.op("scalar", lambda e: e.activation(out=tnx[:], in_=y16[:], func=AF.Sin, scale=-1.0, bias=cbias[:, 2:3]))
            P.opi("scalar", "ssc", lambda e: e.activation(out=tna[:], in_=thr[:], func=AF.Sin))
            M[k]["trig"] = P.mark("ssc")

        def back_pe_trans(j):
            b, kk = divmod(j, n_chunks)
            P.wait("tensor", "sdv", M[j]["n2"])
            for grp in range(2):
                P.wait("tensor", "sdv")
                for s in range(2):
                    for cc in range(4):
                        c = grp * 4 + cc
                        fn = lambda e, s=s, c=c, cc=cc: e.transpose(
                            PD[:, cc * 256 + s * 128: cc * 256 + s * 128 + 128],
                            Srt[:, s * D + c * 128: s * D + (c + 1) * 128], identbf[:])
                        (P.opi("tensor", "spe", fn) if (s == 1 and cc == 3) else P.op("tensor", fn))
                P.wait("vector", "spe")
                for cc in range(4):
                    c = grp * 4 + cc
                    fn = lambda e, c=c, cc=cc, b=b: e.tensor_scalar(
                        h2q[:, c * 256: (c + 1) * 256],
                        PD[:, cc * 256: (cc + 1) * 256],
                        s1b_col[:, b * 8 + c:b * 8 + c + 1],
                        sbb_col[:, b * 8 + c:b * 8 + c + 1],
                        ALU.mult, ALU.add)
                    (P.opi("vector", "sdv", fn, drain=(cc == 0)) if cc == 3
                     else P.op("vector", fn, drain=(cc == 0)))
            M[j]["ln2tr"] = P.mark("spe")
            M[j]["h2q"] = P.mark("sdv")

        def back_pe_ffn(j):
            P.wait("tensor", "sdv", M[j]["h2q"])
            for si in range(4):
                dh = si % 2
                P.op("tensor", lambda e, si=si, dh=dh: e.matmul(
                    f2slots[si], onesbf[:], b2q_row[:, dh * 512:(dh + 1) * 512],
                    start=True, stop=False))
            relu_marks = {}
            cast_marks = {}
            for g in range(4):
                for pr in range(2):
                    r = g * 2 + pr
                    if r >= 1:
                        P.wait("tensor", "ssc", relu_marks[r - 1])
                    for q in range(4):
                        hh = r * 4 + q
                        for dd in range(8):
                            fn = lambda e, hh=hh, q=q, dd=dd: e.matmul(
                                PA[:, q * 256:(q + 1) * 256],
                                w1T[:, dd * H + hh * 128: dd * H + (hh + 1) * 128],
                                tsl(h2q, dd), start=(dd == 0 and q % 2 == 0),
                                stop=(dd == 7), skip_group_check=True)
                            (P.opi("tensor", "spe", fn) if (q == 3 and dd == 7) else P.op("tensor", fn))
                    P.wait("scalar", "spe")
                    for q in range(4):
                        hh = r * 4 + q
                        fn = lambda e, hh=hh, q=q: e.activation(
                            out=y16[:, (hh % 8) * 256:(hh % 8 + 1) * 256],
                            in_=PA[:, q * 256:(q + 1) * 256],
                            func=AF.Relu, bias=b1q_col[:, hh:hh + 1])
                        (P.opi("scalar", "ssc", fn, drain=(q == 0)) if q == 3
                         else P.op("scalar", fn, drain=(q == 0)))
                    relu_marks[r] = P.mark("ssc")
                    P.wait("gpsimd", "ssc", relu_marks[r])
                    P.wait("gpsimd", "spe")
                    hf = r % 2
                    P.opi("gpsimd", "spo", lambda e, hf=hf: e.tensor_copy(
                        r1q[:, hf * 1024:(hf + 1) * 1024],
                        y16[:, hf * 1024:(hf + 1) * 1024]), drain=True)
                    cast_marks[r] = P.mark("spo")
                for sub in range(2):
                    P.wait("tensor", "spo", cast_marks[g * 2 + sub])
                    for hh4 in range(4):
                        hh8 = sub * 4 + hh4
                        hh = g * 8 + hh8
                        for si in range(4):
                            s, dh = si // 2, si % 2
                            fn = lambda e, hh=hh, hh8=hh8, si=si, s=s, dh=dh: e.matmul(
                                f2slots[si],
                                r1q[:, hh8 * 256 + s * 128: hh8 * 256 + s * 128 + 128],
                                w2T[:, hh * D + dh * 512: hh * D + (dh + 1) * 512],
                                start=False, stop=(hh == 31))
                            (P.opi("tensor", "spe", fn) if (si == 3 and hh4 == 3) else P.op("tensor", fn))
            M[j]["ffn2"] = P.mark("spe")

        def scan_main(k):
            b, kk = divmod(k, n_chunks)
            P.wait("vector", "ssc", M[k]["trig"])
            if k >= 1:
                P.wait("vector", "spe", M[k - 1]["ln2tr"])
            P.op("vector", lambda e: e.tensor_tensor(gx[:], gx[:], nrm[:], ALU.mult))
            P.op("vector", lambda e: e.tensor_tensor(eneg[:], gx[:], eneg[:], ALU.mult))
            P.op("vector", lambda e: e.tensor_tensor(xtr[:], xh_r[:], tnx[:], ALU.mult))
            P.op("vector", lambda e: e.tensor_tensor(xti[:], xh_i[:], tna[:], ALU.mult))
            P.op("vector", lambda e: e.tensor_tensor(xtr[:], xtr[:], xti[:], ALU.add))
            P.op("vector", lambda e: e.tensor_tensor(xtr[:], xtr[:], eneg[:], ALU.mult))
            P.op("vector", lambda e: e.tensor_tensor(xti[:], xh_i[:], tnx[:], ALU.mult))
            P.op("vector", lambda e: e.tensor_tensor(xsc[:], xh_r[:], tna[:], ALU.mult))
            P.op("vector", lambda e: e.tensor_tensor(xti[:], xti[:], xsc[:], ALU.subtract))
            P.op("vector", lambda e: e.tensor_tensor(xti[:], xti[:], eneg[:], ALU.mult))
            for c in range(8):
                init_r = 0.0 if kk == 0 else hcarry[:, c:c + 1]
                init_i = 0.0 if kk == 0 else hcarry[:, 8 + c:9 + c]
                P.op("vector", lambda e, c=c, init_r=init_r: e.tensor_tensor_scan(
                    tsl(Srt, c), tsl(xtr, c), tsl(xtr, c), init_r, ALU.add, ALU.bypass),
                    drain=(c == 0))
                P.op("vector", lambda e, c=c, init_i=init_i: e.tensor_tensor_scan(
                    tsl(Sit, c), tsl(xti, c), tsl(xti, c), init_i, ALU.add, ALU.bypass),
                    drain=False)
            P.op("vector", lambda e: e.tensor_tensor(tnx[:], tnx[:], epos[:], ALU.mult))
            P.op("vector", lambda e: e.tensor_tensor(tna[:], tna[:], epos[:], ALU.mult))
            P.op("vector", lambda e: e.tensor_tensor(xtr[:], tnx[:], Srt[:], ALU.mult))
            P.op("vector", lambda e: e.tensor_tensor(xsc[:], tna[:], Sit[:], ALU.mult))
            P.op("vector", lambda e: e.tensor_tensor(xtr[:], xtr[:], xsc[:], ALU.subtract))
            P.op("vector", lambda e: e.tensor_tensor(xti[:], tnx[:], Sit[:], ALU.mult))
            P.op("vector", lambda e: e.tensor_tensor(xsc[:], tna[:], Srt[:], ALU.mult))
            P.op("vector", lambda e: e.tensor_tensor(xti[:], xti[:], xsc[:], ALU.add))
            P.op("vector", lambda e: e.tensor_copy(hcarry[:, 0:8], xtr[:, 255:2048:256]))
            P.opi("vector", "sdv", lambda e: e.tensor_copy(hcarry[:, 8:16], xti[:, 255:2048:256]), drain=False)
            M[k]["h"] = P.mark("sdv")

        def back_dve_tail_casts(j):
            par = j % 2
            P.wait("vector", "spe", M[j]["ffn2"])
            for si in range(4):
                s, dh = si // 2, si % 2
                fn = lambda e, si=si, s=s, dh=dh: e.tensor_copy(
                    y16[:, s * D + dh * 512: s * D + (dh + 1) * 512], f2slots[si])
                (P.opi("vector", "sdv", fn, drain=(si == 0)) if si == 3
                 else P.op("vector", fn, drain=(si == 0)))
            M[j]["y16"] = P.mark("sdv")
            P.wait("gpsimd", "sdv", M[j]["y16"])
            P.wait("gpsimd", M[j]["qm"][0], M[j]["qm"][1])
            P.op("gpsimd", lambda e, par=par: e.tensor_tensor(
                y16[:], y16[:], smaskp[par][:], ALU.mult), drain=True)
            P.op("gpsimd", lambda e: e.tensor_tensor(y16[:, 0:D], y16[:, 0:D], gbb[:], ALU.mult), drain=True)
            P.opi("gpsimd", "spo", lambda e: e.tensor_tensor(y16[:, D:], y16[:, D:], gbb[:], ALU.mult), drain=True)
            M[j]["po_y"] = P.mark("spo")

        def back_dve_tail_store(j):
            P.wait("vector", "spo", M[j]["po_y"])
            P.wait("vector", "qo")
            P.op("vector", lambda e: e.tensor_tensor(cl[:, 0:D], y16[:, 0:D], xmid[:, 0:D], ALU.add), drain=True)
            P.opi("vector", "sdv", lambda e: e.tensor_tensor(cl[:, D:], y16[:, D:], xmid[:, D:], ALU.add), drain=False)
            M[j]["out"] = P.mark("sdv")
            b, kk = divmod(j, n_chunks)
            P.wait("gpsimd", "sdv", M[j]["out"])
            P.dma("gpsimd", "qo", lambda e, b=b, kk=kk: e.dma_start(out=yr4[b, kk * 2], in_=cl[:, 0:D]))
            P.dma("gpsimd", "qo", lambda e, b=b, kk=kk: e.dma_start(out=yr4[b, kk * 2 + 1], in_=cl[:, D:]))

        pre()
        if NK > 0:
            batch_setup(0)
            dma_x(0)
            dma_mask(0, None)
            dve_stats_ln1(0)
            sc_ln1_rows(0)
            dve_n1(0, None)
            for kslot in range(NK + 2):
                k = kslot
                j = kslot - 1
                jj = kslot - 2
                if 0 <= jj < NK:
                    back_dve_tail_casts(jj)
                if k < NK:
                    front(k)
                if 0 <= jj < NK:
                    back_dve_tail_store(jj)
                if 1 <= k < NK:
                    dma_mask(k, M[k - 2]["po_y"] if k >= 2 else None)
                if k < NK:
                    bk, kkk = divmod(k, n_chunks)
                    if kkk == 1 and bk > 0:
                        batch_setup(bk)
                if 0 <= j < NK:
                    back_pe_outproj(j, M[min(k, NK - 1)]["tanh"])
                if k < NK:
                    scan_prep(k)
                if 0 <= j < NK:
                    back_dve_early(j)
                    sc_ln2rows(j)
                    back_dve_n2(j)
                    back_pe_trans(j)
                if k + 1 < NK:
                    dma_x(k + 1)
                    dve_stats_ln1(k + 1)
                    sc_ln1_rows(k + 1)
                    dve_n1(k + 1, M[k]["trans"])
                if k < NK:
                    sc_exps(k)
                    sc_trig(k)
                if 0 <= j < NK:
                    back_pe_ffn(j)
                if k < NK:
                    scan_main(k)
        P.wait("gpsimd", "qo")

        @block.sync
        def _(e):
            for f in P.ops["sync"]:
                f(e)

        @block.gpsimd
        def _(e):
            for f in P.ops["gpsimd"]:
                f(e)

        @block.vector
        def _(e):
            for f in P.ops["vector"]:
                f(e)

        @block.scalar
        def _(e):
            for f in P.ops["scalar"]:
                f(e)

        @block.tensor
        def _(e):
            for f in P.ops["tensor"]:
                f(e)

    return nc


_CACHE = {}


def _host_constants():
    if "mask" not in _CACHE:
        import jax
        cpu = jax.local_devices(backend="cpu")[0]
        with jax.default_device(cpu):
            keep = np.asarray(jax.random.bernoulli(
                jax.random.key(42, impl="threefry2x32"), 0.8, (32, S, D)))
        _CACHE["mask"] = (keep.astype(np.float32) * np.float32(1.25)).astype(
            ml_dtypes.float8_e4m3)
        _CACHE["ident"] = np.eye(128, dtype=np.float32)
        _CACHE["ramp"] = np.tile(np.arange(1, T + 1, dtype=np.float32), (128, 8))
    return _CACHE["mask"], _CACHE["ident"], _CACHE["ramp"]


def kernel(**inputs):
    mask, ident, ramp = _host_constants()
    if "nc" not in _CACHE:
        _CACHE["nc"] = build_program()
    nc = _CACHE["nc"]

    full = {
        "wmod": inputs["L0_wmod"], "W_in": inputs["L0_attn_W_in"],
        "b_in": inputs["L0_attn_b_in"], "Wg_x": inputs["L0_attn_Wg_x"],
        "bg_x": inputs["L0_attn_bg_x"], "Wg_a": inputs["L0_attn_Wg_a"],
        "bg_a": inputs["L0_attn_bg_a"], "a_real": inputs["L0_attn_a_real_param"],
        "a_imag": inputs["L0_attn_a_imag_param"], "W_out": inputs["L0_attn_W_out"],
        "b_out": inputs["L0_attn_b_out"], "w1": inputs["L0_ffn_w1"],
        "b1": inputs["L0_ffn_b1"], "w2": inputs["L0_ffn_w2"],
        "b2": inputs["L0_ffn_b2"], "ident": ident, "ramp": ramp,
    }
    full = {k: np.ascontiguousarray(np.asarray(v, dtype=np.float32))
            for k, v in full.items()}
    x = np.asarray(inputs["L0_attn_x"], dtype=np.float32)
    cv = np.asarray(inputs["L0_c"], dtype=np.float32)

    in_maps = []
    for core in range(NCORES):
        m = dict(full)
        m["x"] = np.ascontiguousarray(x[core * BC:(core + 1) * BC])
        m["cvec"] = np.ascontiguousarray(cv[core * BC:(core + 1) * BC])
        m["mask"] = np.ascontiguousarray(mask[core * BC:(core + 1) * BC])
        in_maps.append(m)

    res = run_bass_kernel_spmd(nc, in_maps, core_ids=list(range(NCORES)))
    return np.concatenate([r["y"] for r in res.results], axis=0)


if __name__ == "__main__":
    build_program()
    print("program built ok")


# version 12
# speedup vs baseline: 1.0313x; 1.0313x over previous
import contextlib
import numpy as np
import ml_dtypes
import concourse.bass as bass
import concourse.mybir as mybir
from concourse.bass_utils import run_bass_kernel_spmd

F32 = mybir.dt.float32
BF16 = mybir.dt.bfloat16
F16 = mybir.dt.float16
F8 = mybir.dt.float8e4
I32 = mybir.dt.int32
AF = mybir.ActivationFunctionType
ALU = mybir.AluOpType

NCORES = 8
BC = 4
S = 2048
D = 1024
DH = 2048
DI = 1024
NB = 16
H = 4096
DMOD = 6144
T = 256
NCH = S // T
EPS = 1e-5
PI = float(np.pi)


class Prog:
    ENGS = ("sync", "gpsimd", "vector", "scalar", "tensor")

    def __init__(self, sems):
        self.ops = {e: [] for e in self.ENGS}
        self.sems = sems
        self.cnt = {k: 0 for k in sems}

    AUTODRAIN = ("vector", "scalar")

    def op(self, eng, fn, drain=None):
        if drain is None:
            drain = eng in self.AUTODRAIN
        if drain:
            self.ops[eng].append(lambda e: e.drain())
        self.ops[eng].append(fn)

    def opi(self, eng, sem, fn, n=1, drain=None):
        h = self.sems[sem]
        self.cnt[sem] += n
        if drain is None:
            drain = eng in self.AUTODRAIN
        if drain:
            self.ops[eng].append(lambda e: e.drain())

        def wrapped(e, fn=fn, h=h, n=n):
            fn(e).then_inc(h, n)

        self.ops[eng].append(wrapped)

    def dma(self, eng, sem, fn):
        self.opi(eng, sem, fn, 16, drain=False)

    def wait(self, eng, sem, val=None):
        h = self.sems[sem]
        v = self.cnt[sem] if val is None else val
        self.ops[eng].append(lambda e, h=h, v=v: e.wait_ge(h, v))

    def mark(self, sem):
        return self.cnt[sem]


def build_program(n_seq=BC, n_chunks=NCH, pre_stage=99):
    nc = bass.Bass("TRN2", target_bir_lowering=False)

    dx = nc.dram_tensor("x", [BC, S, D], F32, kind="ExternalInput").ap()
    dmask = nc.dram_tensor("mask", [BC, S, D], F8, kind="ExternalInput").ap()
    dc = nc.dram_tensor("cvec", [BC, D], F32, kind="ExternalInput").ap()
    dwmod = nc.dram_tensor("wmod", [D, DMOD], F32, kind="ExternalInput").ap()
    dwin = nc.dram_tensor("W_in", [D, DH], F32, kind="ExternalInput").ap()
    dbin = nc.dram_tensor("b_in", [DH], F32, kind="ExternalInput").ap()
    dwgx = nc.dram_tensor("Wg_x", [NB, 128, 64], F32, kind="ExternalInput").ap()
    dbgx = nc.dram_tensor("bg_x", [DI], F32, kind="ExternalInput").ap()
    dwga = nc.dram_tensor("Wg_a", [NB, 128, 64], F32, kind="ExternalInput").ap()
    dbga = nc.dram_tensor("bg_a", [DI], F32, kind="ExternalInput").ap()
    darp = nc.dram_tensor("a_real", [DI], F32, kind="ExternalInput").ap()
    daip = nc.dram_tensor("a_imag", [DI], F32, kind="ExternalInput").ap()
    dwout = nc.dram_tensor("W_out", [DH, D], F32, kind="ExternalInput").ap()
    dbout = nc.dram_tensor("b_out", [D], F32, kind="ExternalInput").ap()
    dw1 = nc.dram_tensor("w1", [H, D], F32, kind="ExternalInput").ap()
    db1 = nc.dram_tensor("b1", [H], F32, kind="ExternalInput").ap()
    dw2 = nc.dram_tensor("w2", [D, H], F32, kind="ExternalInput").ap()
    db2 = nc.dram_tensor("b2", [D], F32, kind="ExternalInput").ap()
    dident = nc.dram_tensor("ident", [128, 128], F32, kind="ExternalInput").ap()
    dramp = nc.dram_tensor("ramp", [128, 2048], F32, kind="ExternalInput").ap()
    dy = nc.dram_tensor("y", [BC, S, D], F32, kind="ExternalOutput").ap()

    dwin_bf = nc.dram_tensor("win_bf", [D, DH], BF16, kind="Internal").ap()
    dx_bf = nc.dram_tensor("x_bf", [BC, S, D], BF16, kind="Internal").ap()
    dwout_bf = nc.dram_tensor("wout_bf", [DH, D], BF16, kind="Internal").ap()

    xr4 = dx.rearrange("b (n p) d -> b n p d", p=128)
    xbf4 = dx_bf.rearrange("b (n p) d -> b n p d", p=128)
    mr4 = dmask.rearrange("b (n p) d -> b n p d", p=128)
    yr4 = dy.rearrange("b (n p) d -> b n p d", p=128)
    winr = dwin.rearrange("(c p) m -> c p m", p=128)
    winbfr = dwin_bf.rearrange("(c p) m -> c p m", p=128)
    woutr = dwout.rearrange("(c p) m -> c p m", p=128)
    woutbfr = dwout_bf.rearrange("(c p) m -> c p m", p=128)
    w1r = dw1.rearrange("(c p) m -> c p m", p=128)
    w2r = dw2.rearrange("(c p) m -> c p m", p=128)
    wmodr = dwmod.rearrange("(c p) m -> c p m", p=128)

    with contextlib.ExitStack() as st:
        def sbt(name, shape, dt):
            return st.enter_context(nc.sbuf_tensor(name, shape, dt))

        def pst(name, shape, dt):
            return st.enter_context(nc.psum_tensor(name, shape, dt))

        w1T = sbt("w1T", [128, 8 * H], F8)
        w2T = sbt("w2T", [128, 32 * D], F8)
        wgx = sbt("wgx", [128, NB * 64], BF16)
        wga = sbt("wga", [128, NB * 64], BF16)
        identbf = sbt("identbf", [128, 128], BF16)
        identf = sbt("identf", [128, 128], F32)
        ones128 = sbt("ones128", [128, 128], BF16)
        rampbf = sbt("rampbf", [128, 2048], BF16)
        onesbf = sbt("onesbf", [1, 128], BF16)

        bcol_in = sbt("bcol_in", [128, 16], F32)
        b1q_col = sbt("b1q_col", [128, 32], F32)
        Lh_col = sbt("Lh_col", [128, 8], F32)
        L2_col = sbt("L2_col", [128, 8], F32)
        aimh_col = sbt("aimh_col", [128, 8], F32)
        bgx2_col = sbt("bgx2_col", [128, 8], F32)
        bga2_col = sbt("bga2_col", [128, 8], F32)
        b2q_row = sbt("b2q_row", [1, D], BF16)
        bout_row = sbt("bout_row", [1, D], BF16)

        mod_cols = sbt("mod_cols", [128, 192], F32)
        s1a_col = sbt("s1a_col", [128, 32], F32)
        sa_col = sbt("sa_col", [128, 32], F32)
        s1b_col = sbt("s1b_col", [128, 32], F32)
        sbb_col = sbt("sbb_col", [128, 32], F32)
        gacl_col = sbt("gacl_col", [128, 32], F32)
        gbcl_col = sbt("gbcl_col", [128, 32], F32)
        gab = sbt("gab", [128, D], BF16)
        gbb = sbt("gbb", [128, D], BF16)
        cfm = sbt("cfm", [128, 32], BF16)
        msc = sbt("msc", [128, 8], F32)
        cbias = sbt("cbias", [128, 4], F32)

        winb = sbt("winb", [128, 2 * DH], BF16)
        woutb = sbt("woutb", [128, 4 * D], BF16)

        xkeep = [sbt("xkeep0", [128, 2 * D], BF16), sbt("xkeep1", [128, 2 * D], BF16)]
        smaskp = [sbt("smask0", [128, 2 * D], F8), sbt("smask1", [128, 2 * D], F8)]
        sstat = sbt("sstat", [128, 24], F32)
        sstat2 = sbt("sstat2", [128, 24], F32)
        saggr = sbt("saggr", [128, 8], F32)
        srow = sbt("srow", [128, 8], F32)
        n1 = sbt("n1", [128, 2 * D], BF16)
        htld = sbt("htld", [128, DH], BF16)
        xh_r = sbt("xh_r", [128, 2048], BF16)
        xh_i = sbt("xh_i", [128, 2048], BF16)
        tnx = sbt("tnx", [128, 2048], BF16)
        tna = sbt("tna", [128, 2048], BF16)
        gx = sbt("gx", [128, 2048], BF16)
        Gt = sbt("Gt", [128, 2048], F32)
        cl = sbt("cl", [128, 2048], F32)
        thr = sbt("thr", [128, 2048], F16)
        riw = sbt("riw", [128, 1024], I32)
        nrm = sbt("nrm", [128, 2048], BF16)
        epos = sbt("epos", [128, 2048], BF16)
        eneg = sbt("eneg", [128, 2048], BF16)
        xtr = sbt("xtr", [128, 2048], BF16)
        xti = sbt("xti", [128, 2048], BF16)
        xsc = sbt("xsc", [128, 2048], BF16)
        Srt = sbt("Srt", [128, 2048], BF16)
        Sit = sbt("Sit", [128, 2048], BF16)
        hcarry = sbt("hcarry", [128, 16], BF16)
        xmid = sbt("xmid", [128, 2 * D], BF16)
        h2q = sbt("h2q", [128, 2048], F8)
        r1q = sbt("r1q", [128, 2048], F8)
        y16 = sbt("y16", [128, 2 * D], F16)

        PA = pst("PA", [128, 1024], F32)
        PB = pst("PB", [128, 1024], F32)
        PC = pst("PC", [128, 1024], F32)
        PE2 = pst("PE2", [128, 512], F32)
        PD = pst("PD", [128, 1024], BF16)

        sems = {}
        for name in ("qxa", "qxb", "qma", "qmb", "qo", "qs", "qwa", "qwb", "qva", "qvb", "spe", "sdv", "ssc", "spo"):
            sems[name] = st.enter_context(nc.semaphore(name))
        block = st.enter_context(nc.Block())
        P = Prog(sems)

        def tsl(tile, c, w=256):
            return tile[:, c * w:(c + 1) * w]

        def pre():
            P.dma("gpsimd", "qo", lambda e: e.dma_start(out=identf[:], in_=dident))
            P.dma("gpsimd", "qo", lambda e: e.dma_start(out=Gt[:], in_=dramp))
            P.op("vector", lambda e: e.memset(ones128[:], 1.0))
            P.op("vector", lambda e: e.memset(onesbf[:], 1.0))
            P.op("vector", lambda e: e.memset(cbias[:, 0:1], EPS))
            P.op("vector", lambda e: e.memset(cbias[:, 1:2], 1.0))
            P.op("vector", lambda e: e.memset(cbias[:, 2:3], PI / 2))

            P.wait("vector", "qo")
            P.op("vector", lambda e: e.tensor_copy(identbf[:], identf[:]))
            P.opi("vector", "sdv", lambda e: e.tensor_copy(rampbf[:], Gt[:]))
            if pre_stage < 1:
                return
            P.wait("gpsimd", "sdv")
            P.dma("gpsimd", "qo", lambda e: e.dma_start(
                out=cl[0:16, 1024:1152], in_=dbin.rearrange("(a p) -> a p", p=128)))
            P.dma("gpsimd", "qo", lambda e: e.dma_start(
                out=cl[32:64, 1024:1152], in_=db1.rearrange("(a p) -> a p", p=128)))
            P.dma("gpsimd", "qo", lambda e: e.dma_start(
                out=cl[64:72, 1024:1152], in_=daip.rearrange("(a p) -> a p", p=128)))
            P.dma("gpsimd", "qo", lambda e: e.dma_start(
                out=Gt[0:8, 1024:1152], in_=dbgx.rearrange("(a p) -> a p", p=128)))
            P.dma("gpsimd", "qo", lambda e: e.dma_start(
                out=Gt[32:40, 1024:1152], in_=dbga.rearrange("(a p) -> a p", p=128)))
            P.dma("gpsimd", "qo", lambda e: e.dma_start(
                out=Gt[64:72, 1024:1152], in_=darp.rearrange("(a p) -> a p", p=128)))
            P.dma("gpsimd", "qo", lambda e: e.dma_start(
                out=cl[0:1, 0:1024], in_=db2.rearrange("(o d) -> o d", o=1)))

            P.wait("vector", "qo")
            P.op("vector", lambda e: e.tensor_copy(r1q[32:64, 0:128], cl[32:64, 1024:1152]))
            P.op("vector", lambda e: e.tensor_copy(cl[32:64, 1024:1152], r1q[32:64, 0:128]))
            P.op("vector", lambda e: e.tensor_copy(r1q[0:1, 0:1024], cl[0:1, 0:1024]))
            P.opi("vector", "sdv", lambda e: e.tensor_copy(b2q_row[:], r1q[0:1, 0:1024]))
            P.wait("gpsimd", "sdv")
            P.dma("gpsimd", "qo", lambda e: e.dma_start(
                out=cl[0:1, 0:1024], in_=dbout.rearrange("(o d) -> o d", o=1)))
            P.wait("vector", "qo")
            P.opi("vector", "sdv", lambda e: e.tensor_copy(bout_row[:], cl[0:1, 0:1024]))
            P.opi("vector", "sdv", lambda e: e.tensor_scalar(
                cl[64:72, 1152:1280], cl[64:72, 1024:1152], 0.5 * PI, None, ALU.mult, ALU.bypass))
            P.wait("scalar", "sdv")
            P.op("scalar", lambda e: e.activation(out=Gt[64:72, 1152:1280], in_=Gt[64:72, 1024:1152], func=AF.Abs))
            P.op("scalar", lambda e: e.activation(out=Gt[64:72, 1280:1408], in_=Gt[64:72, 1152:1280], func=AF.Exp, scale=-1.0))
            P.op("scalar", lambda e: e.activation(out=Gt[64:72, 1152:1280], in_=Gt[64:72, 1280:1408], func=AF.Ln, bias=cbias[64:72, 1:2]))
            P.opi("scalar", "ssc", lambda e: e.activation(out=Gt[64:72, 1280:1408], in_=Gt[64:72, 1024:1152], func=AF.Relu))
            P.wait("vector", "ssc")
            P.opi("vector", "sdv", lambda e: e.tensor_tensor(
                Gt[64:72, 1408:1536], Gt[64:72, 1152:1280], Gt[64:72, 1280:1408], ALU.add))
            P.wait("scalar", "sdv")
            P.opi("scalar", "ssc", lambda e: e.activation(out=Gt[64:72, 1152:1280], in_=Gt[64:72, 1408:1536], func=AF.Ln))
            P.wait("vector", "ssc")
            P.op("vector", lambda e: e.tensor_scalar(
                Gt[64:72, 1152:1280], Gt[64:72, 1152:1280], 0.5, None, ALU.mult, ALU.bypass))
            P.op("vector", lambda e: e.tensor_scalar(
                Gt[0:8, 1152:1280], Gt[0:8, 1024:1152], 0.5, None, ALU.mult, ALU.bypass))
            P.opi("vector", "sdv", lambda e: e.tensor_scalar(
                Gt[32:40, 1152:1280], Gt[32:40, 1024:1152], 0.5, None, ALU.mult, ALU.bypass))
            P.wait("tensor", "sdv")
            P.op("tensor", lambda e: e.transpose(PA[:, 0:128], cl[:, 1024:1152], identf[:]))
            P.op("tensor", lambda e: e.transpose(PA[:, 128:256], cl[:, 1152:1280], identf[:]))
            P.opi("tensor", "spe", lambda e: e.transpose(PA[:, 256:384], Gt[:, 1152:1280], identf[:]))
            P.wait("vector", "spe")
            P.op("vector", lambda e: e.tensor_copy(bcol_in[:], PA[:, 0:16]))
            P.op("vector", lambda e: e.tensor_copy(b1q_col[:], PA[:, 32:64]))
            P.op("vector", lambda e: e.tensor_copy(aimh_col[:], PA[:, 192:200]))
            P.op("vector", lambda e: e.tensor_copy(bgx2_col[:], PA[:, 256:264]))
            P.op("vector", lambda e: e.tensor_copy(bga2_col[:], PA[:, 288:296]))
            P.op("vector", lambda e: e.tensor_copy(Lh_col[:], PA[:, 320:328]))
            P.opi("vector", "sdv", lambda e: e.tensor_scalar(L2_col[:], PA[:, 320:328], 2.0, None, ALU.mult, ALU.bypass))

            if pre_stage < 2:
                return
            P.wait("gpsimd", "sdv")
            P.dma("gpsimd", "qo", lambda e: e.dma_start(
                out=cl[:, 0:1024].rearrange("p (n j) -> p n j", j=64),
                in_=dwgx.rearrange("n h j -> h n j")))
            P.dma("gpsimd", "qo", lambda e: e.dma_start(
                out=Gt[:, 0:1024].rearrange("p (n j) -> p n j", j=64),
                in_=dwga.rearrange("n h j -> h n j")))
            P.wait("vector", "qo")
            P.op("vector", lambda e: e.tensor_copy(wgx[:], cl[:, 0:1024]))
            P.opi("vector", "sdv", lambda e: e.tensor_copy(wga[:], Gt[:, 0:1024]))

            if pre_stage < 3:
                return
            for c in range(8):
                P.wait("gpsimd", "sdv")
                P.dma("gpsimd", "qo", lambda e, c=c: e.dma_start(out=Gt[:], in_=winr[c]))
                P.wait("vector", "qo")
                P.opi("vector", "sdv", lambda e: e.tensor_copy(winb[:, 0:2048], Gt[:]))
                P.wait("gpsimd", "sdv")
                P.dma("gpsimd", "qs", lambda e, c=c: e.dma_start(out=winbfr[c], in_=winb[:, 0:2048]))
                P.wait("vector", "qs")
            if pre_stage < 4:
                return
            for c in range(16):
                P.wait("gpsimd", "sdv")
                P.dma("gpsimd", "qo", lambda e, c=c: e.dma_start(out=Gt[:, 0:1024], in_=woutr[c]))
                P.wait("vector", "qo")
                P.opi("vector", "sdv", lambda e: e.tensor_copy(winb[:, 0:1024], Gt[:, 0:1024]))
                P.wait("gpsimd", "sdv")
                P.dma("gpsimd", "qs", lambda e, c=c: e.dma_start(out=woutbfr[c], in_=winb[:, 0:1024]))
                P.wait("vector", "qs")

            if pre_stage < 5:
                return
            for hh in range(32):
                P.wait("gpsimd", "sdv")
                P.dma("gpsimd", "qo", lambda e, hh=hh: e.dma_start(out=Gt[:, 0:1024], in_=w1r[hh]))
                P.wait("vector", "qo")
                P.op("vector", lambda e: e.tensor_copy(r1q[:, 0:1024], Gt[:, 0:1024]))
                P.opi("vector", "sdv", lambda e: e.tensor_copy(xh_r[:, 0:1024], r1q[:, 0:1024]))
                P.wait("tensor", "sdv")
                for c in range(8):
                    fn = lambda e, c=c: e.transpose(
                        PD[:, c * 128:(c + 1) * 128], xh_r[:, c * 128:(c + 1) * 128], identbf[:])
                    (P.opi("tensor", "spe", fn) if c == 7 else P.op("tensor", fn))
                P.wait("vector", "spe")
                for c in range(8):
                    fn = lambda e, c=c, hh=hh: e.tensor_copy(
                        w1T[:, c * H + hh * 128: c * H + (hh + 1) * 128], PD[:, c * 128:(c + 1) * 128])
                    (P.opi("vector", "sdv", fn) if c == 7 else P.op("vector", fn))
                P.wait("tensor", "sdv")

            if pre_stage < 6:
                return
            for c in range(8):
                P.wait("gpsimd", "sdv")
                P.dma("gpsimd", "qo", lambda e, c=c: e.dma_start(out=Gt[:], in_=w2r[c][:, 0:2048]))
                P.dma("gpsimd", "qo", lambda e, c=c: e.dma_start(out=cl[:], in_=w2r[c][:, 2048:4096]))
                P.wait("vector", "qo")
                for half in range(2):
                    src = Gt if half == 0 else cl
                    P.op("vector", lambda e, src=src: e.tensor_copy(r1q[:], src[:]))
                    P.opi("vector", "sdv", lambda e: e.tensor_copy(xh_r[:], r1q[:]))
                    P.wait("tensor", "sdv")
                    for j2 in range(2):
                        for jj in range(8):
                            j = j2 * 8 + jj
                            fn = lambda e, j=j, jj=jj: e.transpose(
                                PD[:, jj * 128:(jj + 1) * 128], xh_r[:, j * 128:(j + 1) * 128], identbf[:])
                            (P.opi("tensor", "spe", fn) if jj == 7 else P.op("tensor", fn))
                        P.wait("vector", "spe")
                        for jj in range(8):
                            hh = half * 16 + j2 * 8 + jj
                            fn = lambda e, jj=jj, hh=hh, c=c: e.tensor_copy(
                                w2T[:, hh * D + c * 128: hh * D + (c + 1) * 128], PD[:, jj * 128:(jj + 1) * 128])
                            (P.opi("vector", "sdv", fn) if jj == 7 else P.op("vector", fn))
                        P.wait("tensor", "sdv")

            if pre_stage < 7:
                return
            for i in range(32):
                bb, nn_ = i // 8, (i % 8) * 2
                P.wait("gpsimd", "sdv")
                P.dma("gpsimd", "qo", lambda e, bb=bb, nn_=nn_: e.dma_start(out=Gt[:, 0:1024], in_=xr4[bb, nn_]))
                P.dma("gpsimd", "qo", lambda e, bb=bb, nn_=nn_: e.dma_start(out=cl[:, 0:1024], in_=xr4[bb, nn_ + 1]))
                P.wait("vector", "qo")
                P.op("vector", lambda e: e.tensor_copy(winb[:, 0:1024], Gt[:, 0:1024]))
                P.opi("vector", "sdv", lambda e: e.tensor_copy(winb[:, 1024:2048], cl[:, 0:1024]), drain=False)
                P.wait("gpsimd", "sdv")
                P.dma("gpsimd", "qs", lambda e, bb=bb, nn_=nn_: e.dma_start(out=xbf4[bb, nn_], in_=winb[:, 0:1024]))
                P.dma("gpsimd", "qs", lambda e, bb=bb, nn_=nn_: e.dma_start(out=xbf4[bb, nn_ + 1], in_=winb[:, 1024:2048]))
                P.wait("vector", "qs")

            P.wait("gpsimd", "spe")
            P.wait("gpsimd", "sdv")
            P.dma("gpsimd", "qo", lambda e: e.dma_start(out=cl[0:4, 0:1024], in_=dc))
            P.wait("scalar", "qo")
            P.opi("scalar", "ssc", lambda e: e.activation(out=Gt[0:4, 0:1024], in_=cl[0:4, 0:1024], func=AF.Silu))
            P.wait("tensor", "ssc")
            for c8 in range(8):
                fn = lambda e, c8=c8: e.transpose(
                    PB[:, c8 * 128:(c8 + 1) * 128], Gt[:, c8 * 128:(c8 + 1) * 128], identf[:])
                (P.opi("tensor", "spe", fn) if c8 == 7 else P.op("tensor", fn))
            P.wait("vector", "spe")
            for c8 in range(8):
                fn = lambda e, c8=c8: e.tensor_copy(
                    cfm[:, c8 * 4:(c8 + 1) * 4], PB[:, c8 * 128: c8 * 128 + 4])
                (P.opi("vector", "sdv", fn) if c8 == 7 else P.op("vector", fn))
            for d8 in range(8):
                for j3 in range(3):
                    P.wait("gpsimd", "sdv")
                    P.dma("gpsimd", "qo", lambda e, d8=d8, j3=j3: e.dma_start(
                        out=Gt[:], in_=wmodr[d8][:, j3 * 2048:(j3 + 1) * 2048]))
                    P.wait("vector", "qo")
                    P.wait("vector", "spe")
                    P.opi("vector", "sdv", lambda e: e.tensor_copy(xh_r[:], Gt[:]))
                    P.wait("tensor", "sdv")
                    for mcl in range(16):
                        gmc = j3 * 16 + mcl
                        fn = lambda e, gmc=gmc, mcl=mcl, d8=d8, j3=j3: e.matmul(
                            PE2[:, gmc * 4:(gmc + 1) * 4],
                            xh_r[:, mcl * 128:(mcl + 1) * 128],
                            cfm[:, d8 * 4:(d8 + 1) * 4],
                            start=(d8 == 0 and j3 == 0 and mcl == 0),
                            stop=(d8 == 7), skip_group_check=True)
                        (P.opi("tensor", "spe", fn) if mcl == 15 else P.op("tensor", fn))
            P.wait("vector", "spe")
            P.opi("vector", "sdv", lambda e: e.tensor_copy(mod_cols[:], PE2[:, 0:192]))

            for b in range(BC):
                P.op("vector", lambda e, b=b: e.tensor_scalar(
                    s1a_col[:, b * 8:(b + 1) * 8], mod_cols[:, 32 + b:64:4], 1.0, None, ALU.add, ALU.bypass))
                P.op("vector", lambda e, b=b: e.tensor_copy(
                    sa_col[:, b * 8:(b + 1) * 8], mod_cols[:, 0 + b:32:4]))
                P.op("vector", lambda e, b=b: e.tensor_scalar(
                    s1b_col[:, b * 8:(b + 1) * 8], mod_cols[:, 128 + b:160:4], 1.0, None, ALU.add, ALU.bypass))
                P.op("vector", lambda e, b=b: e.tensor_copy(
                    sbb_col[:, b * 8:(b + 1) * 8], mod_cols[:, 96 + b:128:4]))
                for (src0, dst) in ((64, gacl_col), (160, gbcl_col)):
                    P.op("vector", lambda e, b=b, src0=src0: e.tensor_scalar(
                        msc[:], mod_cols[:, src0 + b:src0 + 32:4], 0.0, None, ALU.is_ge, ALU.bypass))
                    P.op("vector", lambda e: e.tensor_scalar(
                        msc[:], msc[:], 2.0, -1.0, ALU.mult, ALU.add))
                    P.op("vector", lambda e, b=b, src0=src0: e.tensor_scalar(
                        srow[:], mod_cols[:, src0 + b:src0 + 32:4], -1.0, None, ALU.mult, ALU.bypass))
                    P.op("vector", lambda e, b=b, src0=src0, dst=dst: e.tensor_tensor(
                        dst[:, b * 8:(b + 1) * 8], mod_cols[:, src0 + b:src0 + 32:4],
                        srow[:], ALU.max))
                    P.op("vector", lambda e, b=b, dst=dst: e.tensor_scalar(
                        dst[:, b * 8:(b + 1) * 8], dst[:, b * 8:(b + 1) * 8],
                        0.25, None, ALU.max, ALU.bypass))
                    P.opi("vector", "sdv", lambda e, b=b, dst=dst: e.tensor_tensor(
                        dst[:, b * 8:(b + 1) * 8], dst[:, b * 8:(b + 1) * 8], msc[:], ALU.mult))

        def batch_setup(b):
            P.wait("vector", "spo")
            for (colt, dst) in ((gacl_col, gab), (gbcl_col, gbb)):
                for c8 in range(8):
                    fn = lambda e, colt=colt, b=b, c8=c8: e.tensor_scalar(
                        xsc[:, c8 * 128:(c8 + 1) * 128], ones128[:],
                        colt[:, b * 8 + c8:b * 8 + c8 + 1], None, ALU.mult, ALU.bypass)
                    (P.opi("vector", "sdv", fn, drain=(c8 == 0)) if c8 == 7
                     else P.op("vector", fn, drain=(c8 == 0)))
                P.wait("tensor", "sdv")
                for c8 in range(8):
                    fn = lambda e, c8=c8: e.transpose(
                        PD[:, c8 * 128:(c8 + 1) * 128], xsc[:, c8 * 128:(c8 + 1) * 128], identbf[:])
                    (P.opi("tensor", "spe", fn) if c8 == 7 else P.op("tensor", fn))
                P.wait("vector", "spe")
                P.opi("vector", "sdv", lambda e, dst=dst: e.tensor_copy(dst[:], PD[:]), drain=True)
                P.wait("tensor", "sdv")

        NK = n_seq * n_chunks
        M = [dict() for _ in range(NK + 2)]
        oslots = (PC[:, 0:512], PC[:, 512:1024], PE2[:], PA[:, 0:512])
        f2slots = (PB[:, 0:512], PB[:, 512:1024], PC[:, 0:512], PC[:, 512:1024])

        def dma_x(k):
            b, kk = divmod(k, n_chunks)
            par = k % 2
            qx = "qxa" if par == 0 else "qxb"
            P.wait("sync", "sdv")
            P.dma("sync", qx, lambda e, b=b, kk=kk, par=par: e.dma_start(
                out=xkeep[par][:, 0:D], in_=xbf4[b, kk * 2]))
            P.dma("sync", qx, lambda e, b=b, kk=kk, par=par: e.dma_start(
                out=xkeep[par][:, D:], in_=xbf4[b, kk * 2 + 1]))
            M[k]["qx"] = (qx, P.mark(qx))

        def dma_mask(k, gate):
            b, kk = divmod(k, n_chunks)
            par = k % 2
            qm = "qma" if par == 0 else "qmb"
            if gate is not None:
                P.wait("sync", "spo", gate)
            P.dma("sync", qm, lambda e, b=b, kk=kk, par=par: e.dma_start(
                out=smaskp[par][:, 0:D], in_=mr4[b, kk * 2]))
            P.dma("sync", qm, lambda e, b=b, kk=kk, par=par: e.dma_start(
                out=smaskp[par][:, D:], in_=mr4[b, kk * 2 + 1]))
            M[k]["qm"] = (qm, P.mark(qm))

        def dve_stats_ln1(k):
            par = k % 2
            P.wait("vector", M[k]["qx"][0], M[k]["qx"][1])
            for s in range(2):
                P.op("vector", lambda e, s=s, par=par: e.bn_stats(
                    sstat2[:, s * 12:s * 12 + 6], xkeep[par][:, s * D:s * D + 512]), drain=(s == 0))
                P.op("vector", lambda e, s=s, par=par: e.bn_stats(
                    sstat2[:, s * 12 + 6:s * 12 + 12], xkeep[par][:, s * D + 512:(s + 1) * D]), drain=False)
            for s in range(2):
                fn = lambda e, s=s: e.bn_aggr(saggr[:, s * 2:s * 2 + 2], sstat2[:, s * 12:s * 12 + 12])
                (P.opi("vector", "sdv", fn, drain=False) if s == 1 else P.op("vector", fn, drain=True))
            M[k]["stats"] = P.mark("sdv")

        def sc_ln1_rows(k):
            P.wait("scalar", "sdv", M[k]["stats"])
            for s in range(2):
                P.op("scalar", lambda e, s=s: e.activation(
                    out=srow[:, s:s + 1], in_=saggr[:, s * 2 + 1:s * 2 + 2], func=AF.Ln, bias=cbias[:, 0:1]))
                fn = lambda e, s=s: e.activation(
                    out=srow[:, s:s + 1], in_=srow[:, s:s + 1], func=AF.Exp, scale=-0.5)
                (P.opi("scalar", "ssc", fn) if s == 1 else P.op("scalar", fn))
            M[k]["ln1rows"] = P.mark("ssc")

        def dve_n1(k, trans_mark):
            par = k % 2
            P.wait("vector", "ssc", M[k]["ln1rows"])
            if trans_mark is not None:
                P.wait("vector", "spe", trans_mark)
            for s in range(2):
                P.op("vector", lambda e, s=s: e.tensor_tensor(
                    srow[:, 2 + s:3 + s], saggr[:, s * 2:s * 2 + 1], srow[:, s:s + 1], ALU.mult),
                    drain=(s == 0))
                fn = lambda e, s=s, par=par: e.tensor_scalar(
                    n1[:, s * D:(s + 1) * D], xkeep[par][:, s * D:(s + 1) * D],
                    srow[:, s:s + 1], srow[:, 2 + s:3 + s], ALU.mult, ALU.subtract)
                (P.opi("vector", "sdv", fn, drain=True) if s == 1 else P.op("vector", fn, drain=True))
            M[k]["n1"] = P.mark("sdv")

        def front(k):
            b, kk = divmod(k, n_chunks)
            P.wait("tensor", "sdv", M[k]["n1"])
            for grp in range(2):
                if grp == 1:
                    P.wait("tensor", "sdv")
                for s in range(2):
                    for cc in range(4):
                        c = grp * 4 + cc
                        fn = lambda e, s=s, c=c, cc=cc: e.transpose(
                            PD[:, cc * 256 + s * 128: cc * 256 + s * 128 + 128],
                            n1[:, s * D + c * 128: s * D + (c + 1) * 128], identbf[:])
                        (P.opi("tensor", "spe", fn) if (s == 1 and cc == 3) else P.op("tensor", fn))
                P.wait("vector", "spe")
                for cc in range(4):
                    c = grp * 4 + cc
                    fn = lambda e, c=c, cc=cc, b=b: e.tensor_scalar(
                        htld[:, c * 256: (c + 1) * 256],
                        PD[:, cc * 256: (cc + 1) * 256],
                        s1a_col[:, b * 8 + c:b * 8 + c + 1],
                        sa_col[:, b * 8 + c:b * 8 + c + 1],
                        ALU.mult, ALU.add)
                    (P.opi("vector", "sdv", fn, drain=(cc == 0)) if cc == 3
                     else P.op("vector", fn, drain=(cc == 0)))
            M[k]["trans"] = P.mark("spe")
            M[k]["htld"] = P.mark("sdv")

            win_marks = {}
            for ph in range(2):
                for dd in range(8):
                    i = ph * 8 + dd
                    slot = i % 2
                    wsem = "qwa" if slot == 0 else "qwb"
                    if i >= 2:
                        P.wait("sync", "spe", win_marks[i - 2])
                    else:
                        P.wait("sync", "spe")
                    P.dma("sync", wsem, lambda e, dd=dd, slot=slot: e.dma_start(
                        out=winb[:, slot * 2048:(slot + 1) * 2048], in_=winbfr[dd]))
                    qmk = P.mark(wsem)
                    P.wait("tensor", wsem, qmk)
                    if i == 0:
                        P.wait("tensor", "sdv", M[k]["htld"])
                    if i == 8:
                        P.wait("tensor", "sdv")
                    for mm_ in range(8):
                        m = ph * 8 + mm_
                        pt = PA if mm_ < 4 else PB
                        fn = lambda e, slot=slot, m=m, mm_=mm_, dd=dd, pt=pt: e.matmul(
                            pt[:, (mm_ % 4) * 256:(mm_ % 4 + 1) * 256],
                            winb[:, slot * 2048 + m * 128: slot * 2048 + (m + 1) * 128],
                            tsl(htld, dd), start=(dd == 0 and mm_ % 2 == 0),
                            stop=(dd == 7), skip_group_check=True)
                        (P.opi("tensor", "spe", fn) if mm_ == 7 else P.op("tensor", fn))
                    win_marks[i] = P.mark("spe")
                P.wait("vector", "spe")
                for mm_ in range(8):
                    m = ph * 8 + mm_
                    dst = xh_r if m < 8 else xh_i
                    pt = PA if mm_ < 4 else PB
                    fn = lambda e, m=m, mm_=mm_, dst=dst, pt=pt: e.tensor_scalar(
                        dst[:, (m % 8) * 256:(m % 8 + 1) * 256],
                        pt[:, (mm_ % 4) * 256:(mm_ % 4 + 1) * 256],
                        bcol_in[:, m:m + 1], None, ALU.add, ALU.bypass)
                    (P.opi("vector", "sdv", fn, drain=(mm_ == 0)) if mm_ == 7
                     else P.op("vector", fn, drain=(mm_ == 0)))
            M[k]["xh"] = P.mark("sdv")

            P.wait("tensor", "sdv", M[k]["xh"])
            for rnd in range(4):
                gate = rnd // 2
                half = rnd % 2
                wg = wgx if gate == 0 else wga
                dstt = tnx if gate == 0 else tna
                bcolt = bgx2_col if gate == 0 else bga2_col
                if rnd > 0:
                    P.wait("tensor", "ssc")
                for nn in range(8):
                    n = half * 8 + nn
                    cch = n // 2
                    pbase = (n % 2) * 64
                    srcx = xh_r if n < 8 else xh_i
                    fn = lambda e, n=n, cch=cch, pbase=pbase, srcx=srcx, wg=wg: e.matmul(
                        PA[pbase:pbase + 64, (cch % 4) * 256:(cch % 4 + 1) * 256],
                        wg[:, n * 64:(n + 1) * 64],
                        srcx[:, (n % 8) * 256:(n % 8 + 1) * 256],
                        start=True, stop=True)
                    (P.opi("tensor", "spe", fn) if nn == 7 else P.op("tensor", fn))
                P.wait("scalar", "spe")
                for cc in range(4):
                    cch = half * 4 + cc
                    fn = lambda e, cc=cc, cch=cch, dstt=dstt, bcolt=bcolt: e.activation(
                        out=dstt[:, cch * 256:(cch + 1) * 256],
                        in_=PA[:, cc * 256:(cc + 1) * 256],
                        func=AF.Tanh, scale=0.5, bias=bcolt[:, cch:cch + 1])
                    (P.opi("scalar", "ssc", fn, drain=(cc == 0)) if cc == 3
                     else P.op("scalar", fn, drain=(cc == 0)))
            M[k]["tanh"] = P.mark("ssc")

        def back_pe_outproj(j, tanh_mark):
            P.wait("tensor", "ssc", tanh_mark)
            P.wait("tensor", "sdv", max(M[j]["h"], M[j - 1]["y16"] if j > 0 else 0))
            for si in range(4):
                dh = si % 2
                P.op("tensor", lambda e, si=si, dh=dh: e.matmul(
                    oslots[si], onesbf[:], bout_row[:, dh * 512:(dh + 1) * 512],
                    start=True, stop=False))
            wout_marks = {}
            for ch in range(16):
                slot = ch % 4
                vsem = "qva" if ch % 2 == 0 else "qvb"
                if ch >= 2:
                    P.wait("sync", "spe", wout_marks[ch - 2])
                else:
                    P.wait("sync", "spe")
                P.dma("sync", vsem, lambda e, ch=ch, slot=slot: e.dma_start(
                    out=woutb[:, slot * D:(slot + 1) * D], in_=woutbfr[ch]))
                qmk = P.mark(vsem)
                P.wait("tensor", vsem, qmk)
                hsrc = xtr if ch < 8 else xti
                for si in range(4):
                    s, dh = si // 2, si % 2
                    fn = lambda e, ch=ch, si=si, s=s, dh=dh, hsrc=hsrc: e.matmul(
                        oslots[si],
                        hsrc[:, (ch % 8) * 256 + s * 128: (ch % 8) * 256 + s * 128 + 128],
                        woutb[:, (ch % 4) * D + dh * 512: (ch % 4) * D + (dh + 1) * 512],
                        start=False, stop=(ch == 15))
                    (P.opi("tensor", "spe", fn) if si == 3 else P.op("tensor", fn))
                wout_marks[ch] = P.mark("spe")
            M[j]["outproj"] = P.mark("spe")

        def back_dve_early(j):
            b, kk = divmod(j, n_chunks)
            par = j % 2
            P.wait("vector", "spe", M[j]["outproj"])
            for si in range(4):
                s, dh = si // 2, si % 2
                P.op("vector", lambda e, si=si, s=s, dh=dh: e.tensor_tensor(
                    xmid[:, s * D + dh * 512: s * D + (dh + 1) * 512],
                    oslots[si], gab[:, dh * 512:(dh + 1) * 512], ALU.mult),
                    drain=(si == 0))
            for s in range(2):
                fn = lambda e, s=s, par=par: e.tensor_tensor(
                    xmid[:, s * D:(s + 1) * D], xmid[:, s * D:(s + 1) * D],
                    xkeep[par][:, s * D:(s + 1) * D], ALU.add)
                (P.opi("vector", "sdv", fn, drain=(s == 0)) if s == 1
                 else P.op("vector", fn, drain=(s == 0)))
            M[j]["xmid"] = P.mark("sdv")
            for s in range(2):
                P.op("vector", lambda e, s=s: e.bn_stats(
                    sstat[:, s * 12:s * 12 + 6], xmid[:, s * D:s * D + 512]), drain=(s == 0))
                P.op("vector", lambda e, s=s: e.bn_stats(
                    sstat[:, s * 12 + 6:s * 12 + 12], xmid[:, s * D + 512:(s + 1) * D]), drain=False)
            for s in range(2):
                fn = lambda e, s=s: e.bn_aggr(saggr[:, 4 + s * 2:6 + s * 2], sstat[:, s * 12:s * 12 + 12])
                (P.opi("vector", "sdv", fn, drain=False) if s == 1 else P.op("vector", fn, drain=True))
            M[j]["ln2stats"] = P.mark("sdv")

        def scan_prep(k):
            P.wait("vector", "ssc", M[k]["tanh"])
            P.wait("vector", "qo")
            P.op("vector", lambda e: e.tensor_scalar(gx[:], tnx[:], 0.5, 0.5, ALU.mult, ALU.add))
            for c in range(8):
                P.op("vector", lambda e, c=c: e.tensor_tensor_scan(
                    tsl(Gt, c), tsl(tna, c), tsl(tna, c), 0.0, ALU.add, ALU.bypass),
                    drain=(c == 0))
            P.op("vector", lambda e: e.tensor_tensor(Gt[:], Gt[:], rampbf[:], ALU.add))
            for c in range(8):
                P.op("vector", lambda e, c=c: e.tensor_scalar(
                    tsl(cl, c), tsl(Gt, c), Lh_col[:, c:c + 1], None, ALU.mult, ALU.bypass),
                    drain=(c == 0))
            for c in range(8):
                P.op("vector", lambda e, c=c: e.tensor_scalar(
                    tsl(Gt, c), tsl(Gt, c), aimh_col[:, c:c + 1], None, ALU.mult, ALU.bypass),
                    drain=(c == 0))
            for hf in range(2):
                gh = Gt[:, hf * 1024:(hf + 1) * 1024]
                xh_ = xsc[:, hf * 1024:(hf + 1) * 1024]
                xt_ = nrm[:, hf * 1024:(hf + 1) * 1024]
                P.op("vector", lambda e, gh=gh: e.tensor_scalar(
                    riw[:], gh, 1.0 / (2 * PI), None, ALU.mult, ALU.bypass))
                P.op("vector", lambda e, gh=gh: e.scalar_tensor_tensor(
                    gh, riw[:], -2 * PI, gh, ALU.mult, ALU.add))
                P.op("vector", lambda e, gh=gh, xh_=xh_: e.tensor_scalar(
                    xh_, gh, PI, None, ALU.is_gt, ALU.bypass))
                P.op("vector", lambda e, gh=gh, xt_=xt_: e.tensor_scalar(
                    xt_, gh, -PI, None, ALU.is_lt, ALU.bypass), drain=False)
                P.op("vector", lambda e, xh_=xh_, xt_=xt_: e.tensor_tensor(
                    xh_, xh_, xt_, ALU.subtract))
                fn = lambda e, gh=gh, xh_=xh_, hf=hf: e.scalar_tensor_tensor(
                    thr[:, hf * 1024:(hf + 1) * 1024], xh_, -2 * PI, gh, ALU.mult, ALU.add)
                (P.opi("vector", "sdv", fn) if hf == 1 else P.op("vector", fn))
            M[k]["thr"] = P.mark("sdv")

        def sc_ln2rows(j):
            P.wait("scalar", "sdv", M[j]["ln2stats"])
            for s in range(2):
                P.op("scalar", lambda e, s=s: e.activation(
                    out=srow[:, 4 + s:5 + s], in_=saggr[:, 5 + s * 2:6 + s * 2], func=AF.Ln, bias=cbias[:, 0:1]))
                fn = lambda e, s=s: e.activation(
                    out=srow[:, 4 + s:5 + s], in_=srow[:, 4 + s:5 + s], func=AF.Exp, scale=-0.5)
                (P.opi("scalar", "ssc", fn) if s == 1 else P.op("scalar", fn))
            M[j]["ln2rows"] = P.mark("ssc")

        def back_dve_n2(j):
            P.wait("vector", "ssc", M[j]["ln2rows"])
            for s in range(2):
                P.op("vector", lambda e, s=s: e.tensor_tensor(
                    srow[:, 6 + s:7 + s], saggr[:, 4 + s * 2:5 + s * 2], srow[:, 4 + s:5 + s], ALU.mult),
                    drain=(s == 0))
                fn = lambda e, s=s: e.tensor_scalar(
                    Srt[:, s * D:(s + 1) * D], xmid[:, s * D:(s + 1) * D],
                    srow[:, 4 + s:5 + s], srow[:, 6 + s:7 + s], ALU.mult, ALU.subtract)
                (P.opi("vector", "sdv", fn, drain=True) if s == 1 else P.op("vector", fn, drain=True))
            M[j]["n2"] = P.mark("sdv")

        def sc_exps(k):
            P.wait("scalar", "sdv", M[k]["thr"])
            P.op("scalar", lambda e: e.activation(out=epos[:], in_=cl[:], func=AF.Exp))
            P.opi("scalar", "ssc", lambda e: e.activation(out=eneg[:], in_=cl[:], func=AF.Exp, scale=-1.0),
                  drain=False)
            for c in range(8):
                fn = lambda e, c=c: e.activation(
                    out=tsl(cl, c), in_=tsl(tna, c), func=AF.Exp,
                    scale=L2_col[:, c:c + 1], bias=L2_col[:, c:c + 1])
                (P.opi("scalar", "ssc", fn, drain=(c == 0)) if c == 7
                 else P.op("scalar", fn, drain=(c == 0)))
            sc_m2 = P.mark("ssc")
            P.wait("vector", "ssc", sc_m2)
            P.op("vector", lambda e: e.tensor_scalar(cl[:], cl[:], -1.0, 1.0, ALU.mult, ALU.add))
            P.opi("vector", "sdv", lambda e: e.tensor_scalar(cl[:], cl[:], 1e-6, None, ALU.max, ALU.bypass))
            dv_w = P.mark("sdv")
            P.wait("scalar", "sdv", dv_w)
            P.op("scalar", lambda e: e.activation(out=Gt[:], in_=cl[:], func=AF.Ln))
            P.opi("scalar", "ssc", lambda e: e.activation(out=nrm[:], in_=Gt[:], func=AF.Exp, scale=0.5))
            M[k]["nrm"] = P.mark("ssc")

        def sc_trig(k):
            P.wait("scalar", "spo")
            P.op("scalar", lambda e: e.activation(out=y16[:], in_=thr[:], func=AF.Abs))
            P.op("scalar", lambda e: e.activation(out=tnx[:], in_=y16[:], func=AF.Sin, scale=-1.0, bias=cbias[:, 2:3]))
            P.opi("scalar", "ssc", lambda e: e.activation(out=tna[:], in_=thr[:], func=AF.Sin))
            M[k]["trig"] = P.mark("ssc")

        def back_pe_trans(j):
            b, kk = divmod(j, n_chunks)
            P.wait("tensor", "sdv", M[j]["n2"])
            for grp in range(2):
                P.wait("tensor", "sdv")
                for s in range(2):
                    for cc in range(4):
                        c = grp * 4 + cc
                        fn = lambda e, s=s, c=c, cc=cc: e.transpose(
                            PD[:, cc * 256 + s * 128: cc * 256 + s * 128 + 128],
                            Srt[:, s * D + c * 128: s * D + (c + 1) * 128], identbf[:])
                        (P.opi("tensor", "spe", fn) if (s == 1 and cc == 3) else P.op("tensor", fn))
                P.wait("vector", "spe")
                for cc in range(4):
                    c = grp * 4 + cc
                    fn = lambda e, c=c, cc=cc, b=b: e.tensor_scalar(
                        h2q[:, c * 256: (c + 1) * 256],
                        PD[:, cc * 256: (cc + 1) * 256],
                        s1b_col[:, b * 8 + c:b * 8 + c + 1],
                        sbb_col[:, b * 8 + c:b * 8 + c + 1],
                        ALU.mult, ALU.add)
                    (P.opi("vector", "sdv", fn, drain=(cc == 0)) if cc == 3
                     else P.op("vector", fn, drain=(cc == 0)))
            M[j]["ln2tr"] = P.mark("spe")
            M[j]["h2q"] = P.mark("sdv")

        def back_pe_ffn(j):
            P.wait("tensor", "sdv", M[j]["h2q"])
            for si in range(4):
                dh = si % 2
                P.op("tensor", lambda e, si=si, dh=dh: e.matmul(
                    f2slots[si], onesbf[:], b2q_row[:, dh * 512:(dh + 1) * 512],
                    start=True, stop=False))
            relu_marks = {}
            cast_marks = {}
            for g in range(4):
                for pr in range(2):
                    r = g * 2 + pr
                    if r >= 1:
                        P.wait("tensor", "ssc", relu_marks[r - 1])
                    for q in range(4):
                        hh = r * 4 + q
                        for dd in range(8):
                            fn = lambda e, hh=hh, q=q, dd=dd: e.matmul(
                                PA[:, q * 256:(q + 1) * 256],
                                w1T[:, dd * H + hh * 128: dd * H + (hh + 1) * 128],
                                tsl(h2q, dd), start=(dd == 0 and q % 2 == 0),
                                stop=(dd == 7), skip_group_check=True)
                            (P.opi("tensor", "spe", fn) if (q == 3 and dd == 7) else P.op("tensor", fn))
                    P.wait("scalar", "spe")
                    for q in range(4):
                        hh = r * 4 + q
                        fn = lambda e, hh=hh, q=q: e.activation(
                            out=y16[:, (hh % 8) * 256:(hh % 8 + 1) * 256],
                            in_=PA[:, q * 256:(q + 1) * 256],
                            func=AF.Relu, bias=b1q_col[:, hh:hh + 1])
                        (P.opi("scalar", "ssc", fn, drain=(q == 0)) if q == 3
                         else P.op("scalar", fn, drain=(q == 0)))
                    relu_marks[r] = P.mark("ssc")
                    P.wait("gpsimd", "ssc", relu_marks[r])
                    P.wait("gpsimd", "spe")
                    hf = r % 2
                    P.opi("gpsimd", "spo", lambda e, hf=hf: e.tensor_copy(
                        r1q[:, hf * 1024:(hf + 1) * 1024],
                        y16[:, hf * 1024:(hf + 1) * 1024]), drain=True)
                    cast_marks[r] = P.mark("spo")
                for sub in range(2):
                    P.wait("tensor", "spo", cast_marks[g * 2 + sub])
                    for hh4 in range(4):
                        hh8 = sub * 4 + hh4
                        hh = g * 8 + hh8
                        for si in range(4):
                            s, dh = si // 2, si % 2
                            fn = lambda e, hh=hh, hh8=hh8, si=si, s=s, dh=dh: e.matmul(
                                f2slots[si],
                                r1q[:, hh8 * 256 + s * 128: hh8 * 256 + s * 128 + 128],
                                w2T[:, hh * D + dh * 512: hh * D + (dh + 1) * 512],
                                start=False, stop=(hh == 31))
                            (P.opi("tensor", "spe", fn) if (si == 3 and hh4 == 3) else P.op("tensor", fn))
            M[j]["ffn2"] = P.mark("spe")

        def scan_main(k):
            b, kk = divmod(k, n_chunks)
            P.wait("vector", "ssc", M[k]["trig"])
            if k >= 1:
                P.wait("vector", "spe", M[k - 1]["ln2tr"])
            P.op("vector", lambda e: e.tensor_tensor(gx[:], gx[:], nrm[:], ALU.mult))
            P.op("vector", lambda e: e.tensor_tensor(eneg[:], gx[:], eneg[:], ALU.mult))
            P.op("vector", lambda e: e.tensor_tensor(xtr[:], xh_r[:], tnx[:], ALU.mult))
            P.op("vector", lambda e: e.tensor_tensor(xti[:], xh_i[:], tna[:], ALU.mult))
            P.op("vector", lambda e: e.tensor_tensor(xtr[:], xtr[:], xti[:], ALU.add))
            P.op("vector", lambda e: e.tensor_tensor(xtr[:], xtr[:], eneg[:], ALU.mult))
            P.op("vector", lambda e: e.tensor_tensor(xti[:], xh_i[:], tnx[:], ALU.mult))
            P.op("vector", lambda e: e.tensor_tensor(xsc[:], xh_r[:], tna[:], ALU.mult))
            P.op("vector", lambda e: e.tensor_tensor(xti[:], xti[:], xsc[:], ALU.subtract))
            P.op("vector", lambda e: e.tensor_tensor(xti[:], xti[:], eneg[:], ALU.mult))
            for c in range(8):
                init_r = 0.0 if kk == 0 else hcarry[:, c:c + 1]
                init_i = 0.0 if kk == 0 else hcarry[:, 8 + c:9 + c]
                P.op("vector", lambda e, c=c, init_r=init_r: e.tensor_tensor_scan(
                    tsl(Srt, c), tsl(xtr, c), tsl(xtr, c), init_r, ALU.add, ALU.bypass),
                    drain=(c == 0))
                P.op("vector", lambda e, c=c, init_i=init_i: e.tensor_tensor_scan(
                    tsl(Sit, c), tsl(xti, c), tsl(xti, c), init_i, ALU.add, ALU.bypass),
                    drain=False)
            P.op("vector", lambda e: e.tensor_tensor(tnx[:], tnx[:], epos[:], ALU.mult))
            P.op("vector", lambda e: e.tensor_tensor(tna[:], tna[:], epos[:], ALU.mult))
            P.op("vector", lambda e: e.tensor_tensor(xtr[:], tnx[:], Srt[:], ALU.mult))
            P.op("vector", lambda e: e.tensor_tensor(xsc[:], tna[:], Sit[:], ALU.mult))
            P.op("vector", lambda e: e.tensor_tensor(xtr[:], xtr[:], xsc[:], ALU.subtract))
            P.op("vector", lambda e: e.tensor_tensor(xti[:], tnx[:], Sit[:], ALU.mult))
            P.op("vector", lambda e: e.tensor_tensor(xsc[:], tna[:], Srt[:], ALU.mult))
            P.op("vector", lambda e: e.tensor_tensor(xti[:], xti[:], xsc[:], ALU.add))
            P.op("vector", lambda e: e.tensor_copy(hcarry[:, 0:8], xtr[:, 255:2048:256]))
            P.opi("vector", "sdv", lambda e: e.tensor_copy(hcarry[:, 8:16], xti[:, 255:2048:256]), drain=False)
            M[k]["h"] = P.mark("sdv")

        def back_dve_tail_casts(j):
            par = j % 2
            P.wait("vector", "spe", M[j]["ffn2"])
            for si in range(4):
                s, dh = si // 2, si % 2
                fn = lambda e, si=si, s=s, dh=dh: e.tensor_copy(
                    y16[:, s * D + dh * 512: s * D + (dh + 1) * 512], f2slots[si])
                (P.opi("vector", "sdv", fn, drain=(si == 0)) if si == 3
                 else P.op("vector", fn, drain=(si == 0)))
            M[j]["y16"] = P.mark("sdv")
            P.wait("gpsimd", "sdv", M[j]["y16"])
            P.wait("gpsimd", M[j]["qm"][0], M[j]["qm"][1])
            P.op("gpsimd", lambda e, par=par: e.tensor_tensor(
                y16[:], y16[:], smaskp[par][:], ALU.mult), drain=True)
            P.op("gpsimd", lambda e: e.tensor_tensor(y16[:, 0:D], y16[:, 0:D], gbb[:], ALU.mult), drain=True)
            P.opi("gpsimd", "spo", lambda e: e.tensor_tensor(y16[:, D:], y16[:, D:], gbb[:], ALU.mult), drain=True)
            M[j]["po_y"] = P.mark("spo")

        def back_dve_tail_store(j):
            P.wait("vector", "spo", M[j]["po_y"])
            P.wait("vector", "qo")
            P.op("vector", lambda e: e.tensor_tensor(cl[:, 0:D], y16[:, 0:D], xmid[:, 0:D], ALU.add), drain=True)
            P.opi("vector", "sdv", lambda e: e.tensor_tensor(cl[:, D:], y16[:, D:], xmid[:, D:], ALU.add), drain=False)
            M[j]["out"] = P.mark("sdv")
            b, kk = divmod(j, n_chunks)
            P.wait("gpsimd", "sdv", M[j]["out"])
            P.dma("gpsimd", "qo", lambda e, b=b, kk=kk: e.dma_start(out=yr4[b, kk * 2], in_=cl[:, 0:D]))
            P.dma("gpsimd", "qo", lambda e, b=b, kk=kk: e.dma_start(out=yr4[b, kk * 2 + 1], in_=cl[:, D:]))

        pre()
        if NK > 0:
            batch_setup(0)
            dma_x(0)
            dma_mask(0, None)
            dve_stats_ln1(0)
            sc_ln1_rows(0)
            dve_n1(0, None)
            for kslot in range(NK + 2):
                k = kslot
                j = kslot - 1
                jj = kslot - 2
                if 0 <= jj < NK:
                    back_dve_tail_casts(jj)
                if k < NK:
                    front(k)
                if 0 <= jj < NK:
                    back_dve_tail_store(jj)
                if 1 <= k < NK:
                    dma_mask(k, M[k - 2]["po_y"] if k >= 2 else None)
                if k < NK:
                    bk, kkk = divmod(k, n_chunks)
                    if kkk == 1 and bk > 0:
                        batch_setup(bk)
                if 0 <= j < NK:
                    back_pe_outproj(j, M[min(k, NK - 1)]["tanh"])
                if k < NK:
                    scan_prep(k)
                if 0 <= j < NK:
                    back_dve_early(j)
                if k + 1 < NK:
                    dma_x(k + 1)
                    dve_stats_ln1(k + 1)
                if 0 <= j < NK:
                    sc_ln2rows(j)
                if k + 1 < NK:
                    sc_ln1_rows(k + 1)
                if 0 <= j < NK:
                    back_dve_n2(j)
                    back_pe_trans(j)
                if k + 1 < NK:
                    dve_n1(k + 1, M[k]["trans"])
                if k < NK:
                    sc_exps(k)
                    sc_trig(k)
                if 0 <= j < NK:
                    back_pe_ffn(j)
                if k < NK:
                    scan_main(k)
        P.wait("gpsimd", "qo")

        @block.sync
        def _(e):
            for f in P.ops["sync"]:
                f(e)

        @block.gpsimd
        def _(e):
            for f in P.ops["gpsimd"]:
                f(e)

        @block.vector
        def _(e):
            for f in P.ops["vector"]:
                f(e)

        @block.scalar
        def _(e):
            for f in P.ops["scalar"]:
                f(e)

        @block.tensor
        def _(e):
            for f in P.ops["tensor"]:
                f(e)

    return nc


_CACHE = {}


def _host_constants():
    if "mask" not in _CACHE:
        import jax
        cpu = jax.local_devices(backend="cpu")[0]
        with jax.default_device(cpu):
            keep = np.asarray(jax.random.bernoulli(
                jax.random.key(42, impl="threefry2x32"), 0.8, (32, S, D)))
        _CACHE["mask"] = (keep.astype(np.float32) * np.float32(1.25)).astype(
            ml_dtypes.float8_e4m3)
        _CACHE["ident"] = np.eye(128, dtype=np.float32)
        _CACHE["ramp"] = np.tile(np.arange(1, T + 1, dtype=np.float32), (128, 8))
    return _CACHE["mask"], _CACHE["ident"], _CACHE["ramp"]


def kernel(**inputs):
    mask, ident, ramp = _host_constants()
    if "nc" not in _CACHE:
        _CACHE["nc"] = build_program()
    nc = _CACHE["nc"]

    full = {
        "wmod": inputs["L0_wmod"], "W_in": inputs["L0_attn_W_in"],
        "b_in": inputs["L0_attn_b_in"], "Wg_x": inputs["L0_attn_Wg_x"],
        "bg_x": inputs["L0_attn_bg_x"], "Wg_a": inputs["L0_attn_Wg_a"],
        "bg_a": inputs["L0_attn_bg_a"], "a_real": inputs["L0_attn_a_real_param"],
        "a_imag": inputs["L0_attn_a_imag_param"], "W_out": inputs["L0_attn_W_out"],
        "b_out": inputs["L0_attn_b_out"], "w1": inputs["L0_ffn_w1"],
        "b1": inputs["L0_ffn_b1"], "w2": inputs["L0_ffn_w2"],
        "b2": inputs["L0_ffn_b2"], "ident": ident, "ramp": ramp,
    }
    full = {k: np.ascontiguousarray(np.asarray(v, dtype=np.float32))
            for k, v in full.items()}
    x = np.asarray(inputs["L0_attn_x"], dtype=np.float32)
    cv = np.asarray(inputs["L0_c"], dtype=np.float32)

    in_maps = []
    for core in range(NCORES):
        m = dict(full)
        m["x"] = np.ascontiguousarray(x[core * BC:(core + 1) * BC])
        m["cvec"] = np.ascontiguousarray(cv[core * BC:(core + 1) * BC])
        m["mask"] = np.ascontiguousarray(mask[core * BC:(core + 1) * BC])
        in_maps.append(m)

    res = run_bass_kernel_spmd(nc, in_maps, core_ids=list(range(NCORES)))
    return np.concatenate([r["y"] for r in res.results], axis=0)


if __name__ == "__main__":
    build_program()
    print("program built ok")


# version 13
# speedup vs baseline: 1.0440x; 1.0124x over previous
import contextlib
import numpy as np
import ml_dtypes
import concourse.bass as bass
import concourse.mybir as mybir
from concourse.bass_utils import run_bass_kernel_spmd

F32 = mybir.dt.float32
BF16 = mybir.dt.bfloat16
F16 = mybir.dt.float16
F8 = mybir.dt.float8e4
I32 = mybir.dt.int32
AF = mybir.ActivationFunctionType
ALU = mybir.AluOpType

NCORES = 8
BC = 4
S = 2048
D = 1024
DH = 2048
DI = 1024
NB = 16
H = 4096
DMOD = 6144
T = 256
NCH = S // T
EPS = 1e-5
PI = float(np.pi)


class Prog:
    ENGS = ("sync", "gpsimd", "vector", "scalar", "tensor")

    def __init__(self, sems):
        self.ops = {e: [] for e in self.ENGS}
        self.sems = sems
        self.cnt = {k: 0 for k in sems}

    AUTODRAIN = ("vector", "scalar")

    def op(self, eng, fn, drain=None):
        if drain is None:
            drain = eng in self.AUTODRAIN
        if drain:
            self.ops[eng].append(lambda e: e.drain())
        self.ops[eng].append(fn)

    def opi(self, eng, sem, fn, n=1, drain=None):
        h = self.sems[sem]
        self.cnt[sem] += n
        if drain is None:
            drain = eng in self.AUTODRAIN
        if drain:
            self.ops[eng].append(lambda e: e.drain())

        def wrapped(e, fn=fn, h=h, n=n):
            fn(e).then_inc(h, n)

        self.ops[eng].append(wrapped)

    def dma(self, eng, sem, fn):
        self.opi(eng, sem, fn, 16, drain=False)

    def wait(self, eng, sem, val=None):
        h = self.sems[sem]
        v = self.cnt[sem] if val is None else val
        self.ops[eng].append(lambda e, h=h, v=v: e.wait_ge(h, v))

    def mark(self, sem):
        return self.cnt[sem]


def build_program(n_seq=BC, n_chunks=NCH, pre_stage=99):
    nc = bass.Bass("TRN2", target_bir_lowering=False)

    dx = nc.dram_tensor("x", [BC, S, D], F32, kind="ExternalInput").ap()
    dmask = nc.dram_tensor("mask", [BC, S, D], F8, kind="ExternalInput").ap()
    dc = nc.dram_tensor("cvec", [BC, D], F32, kind="ExternalInput").ap()
    dwmod = nc.dram_tensor("wmod", [D, DMOD], F32, kind="ExternalInput").ap()
    dwin = nc.dram_tensor("W_in", [D, DH], F32, kind="ExternalInput").ap()
    dbin = nc.dram_tensor("b_in", [DH], F32, kind="ExternalInput").ap()
    dwgx = nc.dram_tensor("Wg_x", [NB, 128, 64], F32, kind="ExternalInput").ap()
    dbgx = nc.dram_tensor("bg_x", [DI], F32, kind="ExternalInput").ap()
    dwga = nc.dram_tensor("Wg_a", [NB, 128, 64], F32, kind="ExternalInput").ap()
    dbga = nc.dram_tensor("bg_a", [DI], F32, kind="ExternalInput").ap()
    darp = nc.dram_tensor("a_real", [DI], F32, kind="ExternalInput").ap()
    daip = nc.dram_tensor("a_imag", [DI], F32, kind="ExternalInput").ap()
    dwout = nc.dram_tensor("W_out", [DH, D], F32, kind="ExternalInput").ap()
    dbout = nc.dram_tensor("b_out", [D], F32, kind="ExternalInput").ap()
    dw1 = nc.dram_tensor("w1", [H, D], F32, kind="ExternalInput").ap()
    db1 = nc.dram_tensor("b1", [H], F32, kind="ExternalInput").ap()
    dw2 = nc.dram_tensor("w2", [D, H], F32, kind="ExternalInput").ap()
    db2 = nc.dram_tensor("b2", [D], F32, kind="ExternalInput").ap()
    dident = nc.dram_tensor("ident", [128, 128], F32, kind="ExternalInput").ap()
    dramp = nc.dram_tensor("ramp", [128, 2048], F32, kind="ExternalInput").ap()
    dy = nc.dram_tensor("y", [BC, S, D], F32, kind="ExternalOutput").ap()

    dwin_bf = nc.dram_tensor("win_bf", [D, DH], BF16, kind="Internal").ap()
    dx_bf = nc.dram_tensor("x_bf", [BC, S, D], BF16, kind="Internal").ap()
    dwout_bf = nc.dram_tensor("wout_bf", [DH, D], BF16, kind="Internal").ap()

    xr4 = dx.rearrange("b (n p) d -> b n p d", p=128)
    xbf4 = dx_bf.rearrange("b (n p) d -> b n p d", p=128)
    mr4 = dmask.rearrange("b (n p) d -> b n p d", p=128)
    yr4 = dy.rearrange("b (n p) d -> b n p d", p=128)
    winr = dwin.rearrange("(c p) m -> c p m", p=128)
    winbfr = dwin_bf.rearrange("(c p) m -> c p m", p=128)
    woutr = dwout.rearrange("(c p) m -> c p m", p=128)
    woutbfr = dwout_bf.rearrange("(c p) m -> c p m", p=128)
    w1r = dw1.rearrange("(c p) m -> c p m", p=128)
    w2r = dw2.rearrange("(c p) m -> c p m", p=128)
    wmodr = dwmod.rearrange("(c p) m -> c p m", p=128)

    with contextlib.ExitStack() as st:
        def sbt(name, shape, dt):
            return st.enter_context(nc.sbuf_tensor(name, shape, dt))

        def pst(name, shape, dt):
            return st.enter_context(nc.psum_tensor(name, shape, dt))

        w1T = sbt("w1T", [128, 8 * H], F8)
        w2T = sbt("w2T", [128, 32 * D], F8)
        wgx = sbt("wgx", [128, NB * 64], BF16)
        wga = sbt("wga", [128, NB * 64], BF16)
        identbf = sbt("identbf", [128, 128], BF16)
        identf = sbt("identf", [128, 128], F32)
        ones128 = sbt("ones128", [128, 128], BF16)
        rampbf = sbt("rampbf", [128, 2048], BF16)
        onesbf = sbt("onesbf", [1, 128], BF16)

        bcol_in = sbt("bcol_in", [128, 16], F32)
        b1q_col = sbt("b1q_col", [128, 32], F32)
        Lh_col = sbt("Lh_col", [128, 8], F32)
        L2_col = sbt("L2_col", [128, 8], F32)
        aimh_col = sbt("aimh_col", [128, 8], F32)
        bgx2_col = sbt("bgx2_col", [128, 8], F32)
        bga2_col = sbt("bga2_col", [128, 8], F32)
        b2q_row = sbt("b2q_row", [1, D], BF16)
        bout_row = sbt("bout_row", [1, D], BF16)

        mod_cols = sbt("mod_cols", [128, 192], F32)
        s1a_col = sbt("s1a_col", [128, 32], F32)
        sa_col = sbt("sa_col", [128, 32], F32)
        s1b_col = sbt("s1b_col", [128, 32], F32)
        sbb_col = sbt("sbb_col", [128, 32], F32)
        gacl_col = sbt("gacl_col", [128, 32], F32)
        gbcl_col = sbt("gbcl_col", [128, 32], F32)
        gab = sbt("gab", [128, D], BF16)
        gbb = sbt("gbb", [128, D], BF16)
        cfm = sbt("cfm", [128, 32], BF16)
        msc = sbt("msc", [128, 8], F32)
        cbias = sbt("cbias", [128, 4], F32)

        winb = sbt("winb", [128, 2 * DH], BF16)
        woutb = sbt("woutb", [128, 4 * D], BF16)

        xkeep = [sbt("xkeep0", [128, 2 * D], BF16), sbt("xkeep1", [128, 2 * D], BF16)]
        smaskp = [sbt("smask0", [128, 2 * D], F8), sbt("smask1", [128, 2 * D], F8)]
        sstat = sbt("sstat", [128, 24], F32)
        sstat2 = sbt("sstat2", [128, 24], F32)
        saggr = sbt("saggr", [128, 8], F32)
        srow = sbt("srow", [128, 8], F32)
        n1 = sbt("n1", [128, 2 * D], BF16)
        htld = sbt("htld", [128, DH], BF16)
        xh_r = sbt("xh_r", [128, 2048], BF16)
        xh_i = sbt("xh_i", [128, 2048], BF16)
        tnx = sbt("tnx", [128, 2048], BF16)
        tna = sbt("tna", [128, 2048], BF16)
        gx = sbt("gx", [128, 2048], BF16)
        Gt = sbt("Gt", [128, 2048], F32)
        cl = sbt("cl", [128, 2048], F32)
        thr = sbt("thr", [128, 2048], F16)
        riw = sbt("riw", [128, 1024], I32)
        nrm = sbt("nrm", [128, 2048], BF16)
        epos = sbt("epos", [128, 2048], BF16)
        eneg = sbt("eneg", [128, 2048], BF16)
        xtr = sbt("xtr", [128, 2048], BF16)
        xti = sbt("xti", [128, 2048], BF16)
        xsc = sbt("xsc", [128, 2048], BF16)
        Srt = sbt("Srt", [128, 2048], BF16)
        Sit = sbt("Sit", [128, 2048], BF16)
        hcarry = sbt("hcarry", [128, 16], BF16)
        xmid = sbt("xmid", [128, 2 * D], BF16)
        h2q = sbt("h2q", [128, 2048], F8)
        r1q = sbt("r1q", [128, 2048], F8)
        y16 = sbt("y16", [128, 2 * D], F16)

        PA = pst("PA", [128, 1024], F32)
        PB = pst("PB", [128, 1024], F32)
        PC = pst("PC", [128, 1024], F32)
        PE2 = pst("PE2", [128, 512], F32)
        PD = pst("PD", [128, 1024], BF16)

        sems = {}
        for name in ("qxa", "qxb", "qma", "qmb", "qo", "qs", "qwa", "qwb", "qva", "qvb", "spe", "sdv", "ssc", "spo"):
            sems[name] = st.enter_context(nc.semaphore(name))
        block = st.enter_context(nc.Block())
        P = Prog(sems)

        def tsl(tile, c, w=256):
            return tile[:, c * w:(c + 1) * w]

        def pre():
            P.dma("gpsimd", "qo", lambda e: e.dma_start(out=identf[:], in_=dident))
            P.dma("gpsimd", "qo", lambda e: e.dma_start(out=Gt[:], in_=dramp))
            P.op("vector", lambda e: e.memset(ones128[:], 1.0))
            P.op("vector", lambda e: e.memset(onesbf[:], 1.0))
            P.op("vector", lambda e: e.memset(cbias[:, 0:1], EPS))
            P.op("vector", lambda e: e.memset(cbias[:, 1:2], 1.0))
            P.op("vector", lambda e: e.memset(cbias[:, 2:3], PI / 2))

            P.wait("vector", "qo")
            P.op("vector", lambda e: e.tensor_copy(identbf[:], identf[:]))
            P.opi("vector", "sdv", lambda e: e.tensor_copy(rampbf[:], Gt[:]))
            if pre_stage < 1:
                return
            P.wait("gpsimd", "sdv")
            P.dma("gpsimd", "qo", lambda e: e.dma_start(
                out=cl[0:16, 1024:1152], in_=dbin.rearrange("(a p) -> a p", p=128)))
            P.dma("gpsimd", "qo", lambda e: e.dma_start(
                out=cl[32:64, 1024:1152], in_=db1.rearrange("(a p) -> a p", p=128)))
            P.dma("gpsimd", "qo", lambda e: e.dma_start(
                out=cl[64:72, 1024:1152], in_=daip.rearrange("(a p) -> a p", p=128)))
            P.dma("gpsimd", "qo", lambda e: e.dma_start(
                out=Gt[0:8, 1024:1152], in_=dbgx.rearrange("(a p) -> a p", p=128)))
            P.dma("gpsimd", "qo", lambda e: e.dma_start(
                out=Gt[32:40, 1024:1152], in_=dbga.rearrange("(a p) -> a p", p=128)))
            P.dma("gpsimd", "qo", lambda e: e.dma_start(
                out=Gt[64:72, 1024:1152], in_=darp.rearrange("(a p) -> a p", p=128)))
            P.dma("gpsimd", "qo", lambda e: e.dma_start(
                out=cl[0:1, 0:1024], in_=db2.rearrange("(o d) -> o d", o=1)))

            P.wait("vector", "qo")
            P.op("vector", lambda e: e.tensor_copy(r1q[32:64, 0:128], cl[32:64, 1024:1152]))
            P.op("vector", lambda e: e.tensor_copy(cl[32:64, 1024:1152], r1q[32:64, 0:128]))
            P.op("vector", lambda e: e.tensor_copy(r1q[0:1, 0:1024], cl[0:1, 0:1024]))
            P.opi("vector", "sdv", lambda e: e.tensor_copy(b2q_row[:], r1q[0:1, 0:1024]))
            P.wait("gpsimd", "sdv")
            P.dma("gpsimd", "qo", lambda e: e.dma_start(
                out=cl[0:1, 0:1024], in_=dbout.rearrange("(o d) -> o d", o=1)))
            P.wait("vector", "qo")
            P.opi("vector", "sdv", lambda e: e.tensor_copy(bout_row[:], cl[0:1, 0:1024]))
            P.opi("vector", "sdv", lambda e: e.tensor_scalar(
                cl[64:72, 1152:1280], cl[64:72, 1024:1152], 0.5 * PI, None, ALU.mult, ALU.bypass))
            P.wait("scalar", "sdv")
            P.op("scalar", lambda e: e.activation(out=Gt[64:72, 1152:1280], in_=Gt[64:72, 1024:1152], func=AF.Abs))
            P.op("scalar", lambda e: e.activation(out=Gt[64:72, 1280:1408], in_=Gt[64:72, 1152:1280], func=AF.Exp, scale=-1.0))
            P.op("scalar", lambda e: e.activation(out=Gt[64:72, 1152:1280], in_=Gt[64:72, 1280:1408], func=AF.Ln, bias=cbias[64:72, 1:2]))
            P.opi("scalar", "ssc", lambda e: e.activation(out=Gt[64:72, 1280:1408], in_=Gt[64:72, 1024:1152], func=AF.Relu))
            P.wait("vector", "ssc")
            P.opi("vector", "sdv", lambda e: e.tensor_tensor(
                Gt[64:72, 1408:1536], Gt[64:72, 1152:1280], Gt[64:72, 1280:1408], ALU.add))
            P.wait("scalar", "sdv")
            P.opi("scalar", "ssc", lambda e: e.activation(out=Gt[64:72, 1152:1280], in_=Gt[64:72, 1408:1536], func=AF.Ln))
            P.wait("vector", "ssc")
            P.op("vector", lambda e: e.tensor_scalar(
                Gt[64:72, 1152:1280], Gt[64:72, 1152:1280], 0.5, None, ALU.mult, ALU.bypass))
            P.op("vector", lambda e: e.tensor_scalar(
                Gt[0:8, 1152:1280], Gt[0:8, 1024:1152], 0.5, None, ALU.mult, ALU.bypass))
            P.opi("vector", "sdv", lambda e: e.tensor_scalar(
                Gt[32:40, 1152:1280], Gt[32:40, 1024:1152], 0.5, None, ALU.mult, ALU.bypass))
            P.wait("tensor", "sdv")
            P.op("tensor", lambda e: e.transpose(PA[:, 0:128], cl[:, 1024:1152], identf[:]))
            P.op("tensor", lambda e: e.transpose(PA[:, 128:256], cl[:, 1152:1280], identf[:]))
            P.opi("tensor", "spe", lambda e: e.transpose(PA[:, 256:384], Gt[:, 1152:1280], identf[:]))
            P.wait("vector", "spe")
            P.op("vector", lambda e: e.tensor_copy(bcol_in[:], PA[:, 0:16]))
            P.op("vector", lambda e: e.tensor_copy(b1q_col[:], PA[:, 32:64]))
            P.op("vector", lambda e: e.tensor_copy(aimh_col[:], PA[:, 192:200]))
            P.op("vector", lambda e: e.tensor_copy(bgx2_col[:], PA[:, 256:264]))
            P.op("vector", lambda e: e.tensor_copy(bga2_col[:], PA[:, 288:296]))
            P.op("vector", lambda e: e.tensor_copy(Lh_col[:], PA[:, 320:328]))
            P.opi("vector", "sdv", lambda e: e.tensor_scalar(L2_col[:], PA[:, 320:328], 2.0, None, ALU.mult, ALU.bypass))

            if pre_stage < 2:
                return
            P.wait("gpsimd", "sdv")
            P.dma("gpsimd", "qo", lambda e: e.dma_start(
                out=cl[:, 0:1024].rearrange("p (n j) -> p n j", j=64),
                in_=dwgx.rearrange("n h j -> h n j")))
            P.dma("gpsimd", "qo", lambda e: e.dma_start(
                out=Gt[:, 0:1024].rearrange("p (n j) -> p n j", j=64),
                in_=dwga.rearrange("n h j -> h n j")))
            P.wait("vector", "qo")
            P.op("vector", lambda e: e.tensor_copy(wgx[:], cl[:, 0:1024]))
            P.opi("vector", "sdv", lambda e: e.tensor_copy(wga[:], Gt[:, 0:1024]))

            if pre_stage < 3:
                return
            for c in range(8):
                P.wait("gpsimd", "sdv")
                P.dma("gpsimd", "qo", lambda e, c=c: e.dma_start(out=Gt[:], in_=winr[c]))
                P.wait("vector", "qo")
                P.opi("vector", "sdv", lambda e: e.tensor_copy(winb[:, 0:2048], Gt[:]))
                P.wait("gpsimd", "sdv")
                P.dma("gpsimd", "qs", lambda e, c=c: e.dma_start(out=winbfr[c], in_=winb[:, 0:2048]))
                P.wait("vector", "qs")
            if pre_stage < 4:
                return
            for c in range(16):
                P.wait("gpsimd", "sdv")
                P.dma("gpsimd", "qo", lambda e, c=c: e.dma_start(out=Gt[:, 0:1024], in_=woutr[c]))
                P.wait("vector", "qo")
                P.opi("vector", "sdv", lambda e: e.tensor_copy(winb[:, 0:1024], Gt[:, 0:1024]))
                P.wait("gpsimd", "sdv")
                P.dma("gpsimd", "qs", lambda e, c=c: e.dma_start(out=woutbfr[c], in_=winb[:, 0:1024]))
                P.wait("vector", "qs")

            if pre_stage < 5:
                return
            for hh in range(32):
                P.wait("gpsimd", "sdv")
                P.dma("gpsimd", "qo", lambda e, hh=hh: e.dma_start(out=Gt[:, 0:1024], in_=w1r[hh]))
                P.wait("vector", "qo")
                P.op("vector", lambda e: e.tensor_copy(r1q[:, 0:1024], Gt[:, 0:1024]))
                P.opi("vector", "sdv", lambda e: e.tensor_copy(xh_r[:, 0:1024], r1q[:, 0:1024]))
                P.wait("tensor", "sdv")
                for c in range(8):
                    fn = lambda e, c=c: e.transpose(
                        PD[:, c * 128:(c + 1) * 128], xh_r[:, c * 128:(c + 1) * 128], identbf[:])
                    (P.opi("tensor", "spe", fn) if c == 7 else P.op("tensor", fn))
                P.wait("vector", "spe")
                for c in range(8):
                    fn = lambda e, c=c, hh=hh: e.tensor_copy(
                        w1T[:, c * H + hh * 128: c * H + (hh + 1) * 128], PD[:, c * 128:(c + 1) * 128])
                    (P.opi("vector", "sdv", fn) if c == 7 else P.op("vector", fn))
                P.wait("tensor", "sdv")

            if pre_stage < 6:
                return
            for c in range(8):
                P.wait("gpsimd", "sdv")
                P.dma("gpsimd", "qo", lambda e, c=c: e.dma_start(out=Gt[:], in_=w2r[c][:, 0:2048]))
                P.dma("gpsimd", "qo", lambda e, c=c: e.dma_start(out=cl[:], in_=w2r[c][:, 2048:4096]))
                P.wait("vector", "qo")
                for half in range(2):
                    src = Gt if half == 0 else cl
                    P.op("vector", lambda e, src=src: e.tensor_copy(r1q[:], src[:]))
                    P.opi("vector", "sdv", lambda e: e.tensor_copy(xh_r[:], r1q[:]))
                    P.wait("tensor", "sdv")
                    for j2 in range(2):
                        for jj in range(8):
                            j = j2 * 8 + jj
                            fn = lambda e, j=j, jj=jj: e.transpose(
                                PD[:, jj * 128:(jj + 1) * 128], xh_r[:, j * 128:(j + 1) * 128], identbf[:])
                            (P.opi("tensor", "spe", fn) if jj == 7 else P.op("tensor", fn))
                        P.wait("vector", "spe")
                        for jj in range(8):
                            hh = half * 16 + j2 * 8 + jj
                            fn = lambda e, jj=jj, hh=hh, c=c: e.tensor_copy(
                                w2T[:, hh * D + c * 128: hh * D + (c + 1) * 128], PD[:, jj * 128:(jj + 1) * 128])
                            (P.opi("vector", "sdv", fn) if jj == 7 else P.op("vector", fn))
                        P.wait("tensor", "sdv")

            if pre_stage < 7:
                return
            for i in range(32):
                bb, nn_ = i // 8, (i % 8) * 2
                P.wait("gpsimd", "sdv")
                P.dma("gpsimd", "qo", lambda e, bb=bb, nn_=nn_: e.dma_start(out=Gt[:, 0:1024], in_=xr4[bb, nn_]))
                P.dma("gpsimd", "qo", lambda e, bb=bb, nn_=nn_: e.dma_start(out=cl[:, 0:1024], in_=xr4[bb, nn_ + 1]))
                P.wait("vector", "qo")
                P.op("vector", lambda e: e.tensor_copy(winb[:, 0:1024], Gt[:, 0:1024]))
                P.opi("vector", "sdv", lambda e: e.tensor_copy(winb[:, 1024:2048], cl[:, 0:1024]), drain=False)
                P.wait("gpsimd", "sdv")
                P.dma("gpsimd", "qs", lambda e, bb=bb, nn_=nn_: e.dma_start(out=xbf4[bb, nn_], in_=winb[:, 0:1024]))
                P.dma("gpsimd", "qs", lambda e, bb=bb, nn_=nn_: e.dma_start(out=xbf4[bb, nn_ + 1], in_=winb[:, 1024:2048]))
                P.wait("vector", "qs")

            P.wait("gpsimd", "spe")
            P.wait("gpsimd", "sdv")
            P.dma("gpsimd", "qo", lambda e: e.dma_start(out=cl[0:4, 0:1024], in_=dc))
            P.wait("scalar", "qo")
            P.opi("scalar", "ssc", lambda e: e.activation(out=Gt[0:4, 0:1024], in_=cl[0:4, 0:1024], func=AF.Silu))
            P.wait("tensor", "ssc")
            for c8 in range(8):
                fn = lambda e, c8=c8: e.transpose(
                    PB[:, c8 * 128:(c8 + 1) * 128], Gt[:, c8 * 128:(c8 + 1) * 128], identf[:])
                (P.opi("tensor", "spe", fn) if c8 == 7 else P.op("tensor", fn))
            P.wait("vector", "spe")
            for c8 in range(8):
                fn = lambda e, c8=c8: e.tensor_copy(
                    cfm[:, c8 * 4:(c8 + 1) * 4], PB[:, c8 * 128: c8 * 128 + 4])
                (P.opi("vector", "sdv", fn) if c8 == 7 else P.op("vector", fn))
            for d8 in range(8):
                for j3 in range(3):
                    P.wait("gpsimd", "sdv")
                    P.dma("gpsimd", "qo", lambda e, d8=d8, j3=j3: e.dma_start(
                        out=Gt[:], in_=wmodr[d8][:, j3 * 2048:(j3 + 1) * 2048]))
                    P.wait("vector", "qo")
                    P.wait("vector", "spe")
                    P.opi("vector", "sdv", lambda e: e.tensor_copy(xh_r[:], Gt[:]))
                    P.wait("tensor", "sdv")
                    for mcl in range(16):
                        gmc = j3 * 16 + mcl
                        fn = lambda e, gmc=gmc, mcl=mcl, d8=d8, j3=j3: e.matmul(
                            PE2[:, gmc * 4:(gmc + 1) * 4],
                            xh_r[:, mcl * 128:(mcl + 1) * 128],
                            cfm[:, d8 * 4:(d8 + 1) * 4],
                            start=(d8 == 0 and j3 == 0 and mcl == 0),
                            stop=(d8 == 7), skip_group_check=True)
                        (P.opi("tensor", "spe", fn) if mcl == 15 else P.op("tensor", fn))
            P.wait("vector", "spe")
            P.opi("vector", "sdv", lambda e: e.tensor_copy(mod_cols[:], PE2[:, 0:192]))

            for b in range(BC):
                P.op("vector", lambda e, b=b: e.tensor_scalar(
                    s1a_col[:, b * 8:(b + 1) * 8], mod_cols[:, 32 + b:64:4], 1.0, None, ALU.add, ALU.bypass))
                P.op("vector", lambda e, b=b: e.tensor_copy(
                    sa_col[:, b * 8:(b + 1) * 8], mod_cols[:, 0 + b:32:4]))
                P.op("vector", lambda e, b=b: e.tensor_scalar(
                    s1b_col[:, b * 8:(b + 1) * 8], mod_cols[:, 128 + b:160:4], 1.0, None, ALU.add, ALU.bypass))
                P.op("vector", lambda e, b=b: e.tensor_copy(
                    sbb_col[:, b * 8:(b + 1) * 8], mod_cols[:, 96 + b:128:4]))
                for (src0, dst) in ((64, gacl_col), (160, gbcl_col)):
                    P.op("vector", lambda e, b=b, src0=src0: e.tensor_scalar(
                        msc[:], mod_cols[:, src0 + b:src0 + 32:4], 0.0, None, ALU.is_ge, ALU.bypass))
                    P.op("vector", lambda e: e.tensor_scalar(
                        msc[:], msc[:], 2.0, -1.0, ALU.mult, ALU.add))
                    P.op("vector", lambda e, b=b, src0=src0: e.tensor_scalar(
                        srow[:], mod_cols[:, src0 + b:src0 + 32:4], -1.0, None, ALU.mult, ALU.bypass))
                    P.op("vector", lambda e, b=b, src0=src0, dst=dst: e.tensor_tensor(
                        dst[:, b * 8:(b + 1) * 8], mod_cols[:, src0 + b:src0 + 32:4],
                        srow[:], ALU.max))
                    P.op("vector", lambda e, b=b, dst=dst: e.tensor_scalar(
                        dst[:, b * 8:(b + 1) * 8], dst[:, b * 8:(b + 1) * 8],
                        0.25, None, ALU.max, ALU.bypass))
                    P.opi("vector", "sdv", lambda e, b=b, dst=dst: e.tensor_tensor(
                        dst[:, b * 8:(b + 1) * 8], dst[:, b * 8:(b + 1) * 8], msc[:], ALU.mult))

        def batch_setup(b):
            P.wait("vector", "spo")
            for (colt, dst) in ((gacl_col, gab), (gbcl_col, gbb)):
                for c8 in range(8):
                    fn = lambda e, colt=colt, b=b, c8=c8: e.tensor_scalar(
                        xsc[:, c8 * 128:(c8 + 1) * 128], ones128[:],
                        colt[:, b * 8 + c8:b * 8 + c8 + 1], None, ALU.mult, ALU.bypass)
                    (P.opi("vector", "sdv", fn, drain=(c8 == 0)) if c8 == 7
                     else P.op("vector", fn, drain=(c8 == 0)))
                P.wait("tensor", "sdv")
                for c8 in range(8):
                    fn = lambda e, c8=c8: e.transpose(
                        PD[:, c8 * 128:(c8 + 1) * 128], xsc[:, c8 * 128:(c8 + 1) * 128], identbf[:])
                    (P.opi("tensor", "spe", fn) if c8 == 7 else P.op("tensor", fn))
                P.wait("vector", "spe")
                P.opi("vector", "sdv", lambda e, dst=dst: e.tensor_copy(dst[:], PD[:]), drain=True)
                P.wait("tensor", "sdv")

        NK = n_seq * n_chunks
        M = [dict() for _ in range(NK + 2)]
        oslots = (PC[:, 0:512], PC[:, 512:1024], PE2[:], PA[:, 0:512])
        f2slots = (PB[:, 0:512], PB[:, 512:1024], PC[:, 0:512], PC[:, 512:1024])

        def dma_x(k):
            b, kk = divmod(k, n_chunks)
            par = k % 2
            qx = "qxa" if par == 0 else "qxb"
            P.wait("sync", "sdv")
            P.dma("sync", qx, lambda e, b=b, kk=kk, par=par: e.dma_start(
                out=xkeep[par][:, 0:D], in_=xbf4[b, kk * 2]))
            P.dma("sync", qx, lambda e, b=b, kk=kk, par=par: e.dma_start(
                out=xkeep[par][:, D:], in_=xbf4[b, kk * 2 + 1]))
            M[k]["qx"] = (qx, P.mark(qx))

        def dma_mask(k, gate):
            b, kk = divmod(k, n_chunks)
            par = k % 2
            qm = "qma" if par == 0 else "qmb"
            if gate is not None:
                P.wait("sync", "spo", gate)
            P.dma("sync", qm, lambda e, b=b, kk=kk, par=par: e.dma_start(
                out=smaskp[par][:, 0:D], in_=mr4[b, kk * 2]))
            P.dma("sync", qm, lambda e, b=b, kk=kk, par=par: e.dma_start(
                out=smaskp[par][:, D:], in_=mr4[b, kk * 2 + 1]))
            M[k]["qm"] = (qm, P.mark(qm))

        def dve_stats_ln1(k):
            par = k % 2
            P.wait("vector", M[k]["qx"][0], M[k]["qx"][1])
            for s in range(2):
                P.op("vector", lambda e, s=s, par=par: e.bn_stats(
                    sstat2[:, s * 12:s * 12 + 6], xkeep[par][:, s * D:s * D + 512]), drain=(s == 0))
                P.op("vector", lambda e, s=s, par=par: e.bn_stats(
                    sstat2[:, s * 12 + 6:s * 12 + 12], xkeep[par][:, s * D + 512:(s + 1) * D]), drain=False)
            for s in range(2):
                fn = lambda e, s=s: e.bn_aggr(saggr[:, s * 2:s * 2 + 2], sstat2[:, s * 12:s * 12 + 12])
                (P.opi("vector", "sdv", fn, drain=False) if s == 1 else P.op("vector", fn, drain=True))
            M[k]["stats"] = P.mark("sdv")

        def sc_ln1_rows(k):
            P.wait("scalar", "sdv", M[k]["stats"])
            for s in range(2):
                P.op("scalar", lambda e, s=s: e.activation(
                    out=srow[:, s:s + 1], in_=saggr[:, s * 2 + 1:s * 2 + 2], func=AF.Ln, bias=cbias[:, 0:1]))
                fn = lambda e, s=s: e.activation(
                    out=srow[:, s:s + 1], in_=srow[:, s:s + 1], func=AF.Exp, scale=-0.5)
                (P.opi("scalar", "ssc", fn) if s == 1 else P.op("scalar", fn))
            M[k]["ln1rows"] = P.mark("ssc")

        def dve_n1(k, trans_mark):
            par = k % 2
            P.wait("vector", "ssc", M[k]["ln1rows"])
            if trans_mark is not None:
                P.wait("vector", "spe", trans_mark)
            for s in range(2):
                P.op("vector", lambda e, s=s: e.tensor_tensor(
                    srow[:, 2 + s:3 + s], saggr[:, s * 2:s * 2 + 1], srow[:, s:s + 1], ALU.mult),
                    drain=(s == 0))
                fn = lambda e, s=s, par=par: e.tensor_scalar(
                    n1[:, s * D:(s + 1) * D], xkeep[par][:, s * D:(s + 1) * D],
                    srow[:, s:s + 1], srow[:, 2 + s:3 + s], ALU.mult, ALU.subtract)
                (P.opi("vector", "sdv", fn, drain=True) if s == 1 else P.op("vector", fn, drain=True))
            M[k]["n1"] = P.mark("sdv")

        def front(k, mid=None):
            b, kk = divmod(k, n_chunks)
            P.wait("tensor", "sdv", M[k]["n1"])
            for grp in range(2):
                if grp == 1:
                    P.wait("tensor", "sdv")
                for s in range(2):
                    for cc in range(4):
                        c = grp * 4 + cc
                        fn = lambda e, s=s, c=c, cc=cc: e.transpose(
                            PD[:, cc * 256 + s * 128: cc * 256 + s * 128 + 128],
                            n1[:, s * D + c * 128: s * D + (c + 1) * 128], identbf[:])
                        (P.opi("tensor", "spe", fn) if (s == 1 and cc == 3) else P.op("tensor", fn))
                P.wait("vector", "spe")
                for cc in range(4):
                    c = grp * 4 + cc
                    fn = lambda e, c=c, cc=cc, b=b: e.tensor_scalar(
                        htld[:, c * 256: (c + 1) * 256],
                        PD[:, cc * 256: (cc + 1) * 256],
                        s1a_col[:, b * 8 + c:b * 8 + c + 1],
                        sa_col[:, b * 8 + c:b * 8 + c + 1],
                        ALU.mult, ALU.add)
                    (P.opi("vector", "sdv", fn, drain=(cc == 0)) if cc == 3
                     else P.op("vector", fn, drain=(cc == 0)))
            M[k]["trans"] = P.mark("spe")
            M[k]["htld"] = P.mark("sdv")
            if mid is not None:
                mid()

            win_marks = {}
            for ph in range(2):
                for dd in range(8):
                    i = ph * 8 + dd
                    slot = i % 2
                    wsem = "qwa" if slot == 0 else "qwb"
                    if i >= 2:
                        P.wait("sync", "spe", win_marks[i - 2])
                    else:
                        P.wait("sync", "spe")
                    P.dma("sync", wsem, lambda e, dd=dd, slot=slot: e.dma_start(
                        out=winb[:, slot * 2048:(slot + 1) * 2048], in_=winbfr[dd]))
                    qmk = P.mark(wsem)
                    P.wait("tensor", wsem, qmk)
                    if i == 0:
                        P.wait("tensor", "sdv", M[k]["htld"])
                    if i == 8:
                        P.wait("tensor", "sdv")
                    for mm_ in range(8):
                        m = ph * 8 + mm_
                        pt = PA if mm_ < 4 else PB
                        fn = lambda e, slot=slot, m=m, mm_=mm_, dd=dd, pt=pt: e.matmul(
                            pt[:, (mm_ % 4) * 256:(mm_ % 4 + 1) * 256],
                            winb[:, slot * 2048 + m * 128: slot * 2048 + (m + 1) * 128],
                            tsl(htld, dd), start=(dd == 0 and mm_ % 2 == 0),
                            stop=(dd == 7), skip_group_check=True)
                        (P.opi("tensor", "spe", fn) if mm_ == 7 else P.op("tensor", fn))
                    win_marks[i] = P.mark("spe")
                P.wait("vector", "spe")
                for mm_ in range(8):
                    m = ph * 8 + mm_
                    dst = xh_r if m < 8 else xh_i
                    pt = PA if mm_ < 4 else PB
                    fn = lambda e, m=m, mm_=mm_, dst=dst, pt=pt: e.tensor_scalar(
                        dst[:, (m % 8) * 256:(m % 8 + 1) * 256],
                        pt[:, (mm_ % 4) * 256:(mm_ % 4 + 1) * 256],
                        bcol_in[:, m:m + 1], None, ALU.add, ALU.bypass)
                    (P.opi("vector", "sdv", fn, drain=(mm_ == 0)) if mm_ == 7
                     else P.op("vector", fn, drain=(mm_ == 0)))
            M[k]["xh"] = P.mark("sdv")

            P.wait("tensor", "sdv", M[k]["xh"])
            for rnd in range(4):
                gate = rnd // 2
                half = rnd % 2
                wg = wgx if gate == 0 else wga
                dstt = tnx if gate == 0 else tna
                bcolt = bgx2_col if gate == 0 else bga2_col
                if rnd > 0:
                    P.wait("tensor", "ssc")
                for nn in range(8):
                    n = half * 8 + nn
                    cch = n // 2
                    pbase = (n % 2) * 64
                    srcx = xh_r if n < 8 else xh_i
                    fn = lambda e, n=n, cch=cch, pbase=pbase, srcx=srcx, wg=wg: e.matmul(
                        PA[pbase:pbase + 64, (cch % 4) * 256:(cch % 4 + 1) * 256],
                        wg[:, n * 64:(n + 1) * 64],
                        srcx[:, (n % 8) * 256:(n % 8 + 1) * 256],
                        start=True, stop=True)
                    (P.opi("tensor", "spe", fn) if nn == 7 else P.op("tensor", fn))
                P.wait("scalar", "spe")
                for cc in range(4):
                    cch = half * 4 + cc
                    fn = lambda e, cc=cc, cch=cch, dstt=dstt, bcolt=bcolt: e.activation(
                        out=dstt[:, cch * 256:(cch + 1) * 256],
                        in_=PA[:, cc * 256:(cc + 1) * 256],
                        func=AF.Tanh, scale=0.5, bias=bcolt[:, cch:cch + 1])
                    (P.opi("scalar", "ssc", fn, drain=(cc == 0)) if cc == 3
                     else P.op("scalar", fn, drain=(cc == 0)))
            M[k]["tanh"] = P.mark("ssc")

        def back_pe_outproj(j, tanh_mark):
            P.wait("tensor", "ssc", tanh_mark)
            P.wait("tensor", "sdv", max(M[j]["h"], M[j - 1]["y16"] if j > 0 else 0))
            for si in range(4):
                dh = si % 2
                P.op("tensor", lambda e, si=si, dh=dh: e.matmul(
                    oslots[si], onesbf[:], bout_row[:, dh * 512:(dh + 1) * 512],
                    start=True, stop=False))
            wout_marks = {}
            for ch in range(16):
                slot = ch % 4
                vsem = "qva" if ch % 2 == 0 else "qvb"
                if ch >= 2:
                    P.wait("sync", "spe", wout_marks[ch - 2])
                else:
                    P.wait("sync", "spe")
                P.dma("sync", vsem, lambda e, ch=ch, slot=slot: e.dma_start(
                    out=woutb[:, slot * D:(slot + 1) * D], in_=woutbfr[ch]))
                qmk = P.mark(vsem)
                P.wait("tensor", vsem, qmk)
                hsrc = xtr if ch < 8 else xti
                for si in range(4):
                    s, dh = si // 2, si % 2
                    fn = lambda e, ch=ch, si=si, s=s, dh=dh, hsrc=hsrc: e.matmul(
                        oslots[si],
                        hsrc[:, (ch % 8) * 256 + s * 128: (ch % 8) * 256 + s * 128 + 128],
                        woutb[:, (ch % 4) * D + dh * 512: (ch % 4) * D + (dh + 1) * 512],
                        start=False, stop=(ch == 15))
                    (P.opi("tensor", "spe", fn) if si == 3 else P.op("tensor", fn))
                wout_marks[ch] = P.mark("spe")
            M[j]["outproj"] = P.mark("spe")

        def back_dve_early(j):
            b, kk = divmod(j, n_chunks)
            par = j % 2
            P.wait("vector", "spe", M[j]["outproj"])
            for si in range(4):
                s, dh = si // 2, si % 2
                P.op("vector", lambda e, si=si, s=s, dh=dh: e.tensor_tensor(
                    xmid[:, s * D + dh * 512: s * D + (dh + 1) * 512],
                    oslots[si], gab[:, dh * 512:(dh + 1) * 512], ALU.mult),
                    drain=(si == 0))
            for s in range(2):
                fn = lambda e, s=s, par=par: e.tensor_tensor(
                    xmid[:, s * D:(s + 1) * D], xmid[:, s * D:(s + 1) * D],
                    xkeep[par][:, s * D:(s + 1) * D], ALU.add)
                (P.opi("vector", "sdv", fn, drain=(s == 0)) if s == 1
                 else P.op("vector", fn, drain=(s == 0)))
            M[j]["xmid"] = P.mark("sdv")
            for s in range(2):
                P.op("vector", lambda e, s=s: e.bn_stats(
                    sstat[:, s * 12:s * 12 + 6], xmid[:, s * D:s * D + 512]), drain=(s == 0))
                P.op("vector", lambda e, s=s: e.bn_stats(
                    sstat[:, s * 12 + 6:s * 12 + 12], xmid[:, s * D + 512:(s + 1) * D]), drain=False)
            for s in range(2):
                fn = lambda e, s=s: e.bn_aggr(saggr[:, 4 + s * 2:6 + s * 2], sstat[:, s * 12:s * 12 + 12])
                (P.opi("vector", "sdv", fn, drain=False) if s == 1 else P.op("vector", fn, drain=True))
            M[j]["ln2stats"] = P.mark("sdv")

        def scan_prep(k):
            P.wait("vector", "ssc", M[k]["tanh"])
            P.wait("vector", "qo")
            P.op("vector", lambda e: e.tensor_scalar(gx[:], tnx[:], 0.5, 0.5, ALU.mult, ALU.add))
            for c in range(8):
                P.op("vector", lambda e, c=c: e.tensor_tensor_scan(
                    tsl(Gt, c), tsl(tna, c), tsl(tna, c), 0.0, ALU.add, ALU.bypass),
                    drain=(c == 0))
            P.op("vector", lambda e: e.tensor_tensor(Gt[:], Gt[:], rampbf[:], ALU.add))
            for c in range(8):
                P.op("vector", lambda e, c=c: e.tensor_scalar(
                    tsl(cl, c), tsl(Gt, c), Lh_col[:, c:c + 1], None, ALU.mult, ALU.bypass),
                    drain=(c == 0))
            for c in range(8):
                P.op("vector", lambda e, c=c: e.tensor_scalar(
                    tsl(Gt, c), tsl(Gt, c), aimh_col[:, c:c + 1], None, ALU.mult, ALU.bypass),
                    drain=(c == 0))
            for hf in range(2):
                gh = Gt[:, hf * 1024:(hf + 1) * 1024]
                xh_ = xsc[:, hf * 1024:(hf + 1) * 1024]
                xt_ = nrm[:, hf * 1024:(hf + 1) * 1024]
                P.op("vector", lambda e, gh=gh: e.tensor_scalar(
                    riw[:], gh, 1.0 / (2 * PI), None, ALU.mult, ALU.bypass))
                P.op("vector", lambda e, gh=gh: e.scalar_tensor_tensor(
                    gh, riw[:], -2 * PI, gh, ALU.mult, ALU.add))
                P.op("vector", lambda e, gh=gh, xh_=xh_: e.tensor_scalar(
                    xh_, gh, PI, None, ALU.is_gt, ALU.bypass))
                P.op("vector", lambda e, gh=gh, xt_=xt_: e.tensor_scalar(
                    xt_, gh, -PI, None, ALU.is_lt, ALU.bypass), drain=False)
                P.op("vector", lambda e, xh_=xh_, xt_=xt_: e.tensor_tensor(
                    xh_, xh_, xt_, ALU.subtract))
                fn = lambda e, gh=gh, xh_=xh_, hf=hf: e.scalar_tensor_tensor(
                    thr[:, hf * 1024:(hf + 1) * 1024], xh_, -2 * PI, gh, ALU.mult, ALU.add)
                (P.opi("vector", "sdv", fn) if hf == 1 else P.op("vector", fn))
            M[k]["thr"] = P.mark("sdv")

        def sc_ln2rows(j):
            P.wait("scalar", "sdv", M[j]["ln2stats"])
            for s in range(2):
                P.op("scalar", lambda e, s=s: e.activation(
                    out=srow[:, 4 + s:5 + s], in_=saggr[:, 5 + s * 2:6 + s * 2], func=AF.Ln, bias=cbias[:, 0:1]))
                fn = lambda e, s=s: e.activation(
                    out=srow[:, 4 + s:5 + s], in_=srow[:, 4 + s:5 + s], func=AF.Exp, scale=-0.5)
                (P.opi("scalar", "ssc", fn) if s == 1 else P.op("scalar", fn))
            M[j]["ln2rows"] = P.mark("ssc")

        def back_dve_n2(j):
            P.wait("vector", "ssc", M[j]["ln2rows"])
            for s in range(2):
                P.op("vector", lambda e, s=s: e.tensor_tensor(
                    srow[:, 6 + s:7 + s], saggr[:, 4 + s * 2:5 + s * 2], srow[:, 4 + s:5 + s], ALU.mult),
                    drain=(s == 0))
                fn = lambda e, s=s: e.tensor_scalar(
                    Srt[:, s * D:(s + 1) * D], xmid[:, s * D:(s + 1) * D],
                    srow[:, 4 + s:5 + s], srow[:, 6 + s:7 + s], ALU.mult, ALU.subtract)
                (P.opi("vector", "sdv", fn, drain=True) if s == 1 else P.op("vector", fn, drain=True))
            M[j]["n2"] = P.mark("sdv")

        def sc_exps(k):
            P.wait("scalar", "sdv", M[k]["thr"])
            P.op("scalar", lambda e: e.activation(out=epos[:], in_=cl[:], func=AF.Exp))
            P.opi("scalar", "ssc", lambda e: e.activation(out=eneg[:], in_=cl[:], func=AF.Exp, scale=-1.0),
                  drain=False)
            for c in range(8):
                fn = lambda e, c=c: e.activation(
                    out=tsl(cl, c), in_=tsl(tna, c), func=AF.Exp,
                    scale=L2_col[:, c:c + 1], bias=L2_col[:, c:c + 1])
                (P.opi("scalar", "ssc", fn, drain=(c == 0)) if c == 7
                 else P.op("scalar", fn, drain=(c == 0)))
            sc_m2 = P.mark("ssc")
            P.wait("vector", "ssc", sc_m2)
            P.op("vector", lambda e: e.tensor_scalar(cl[:], cl[:], -1.0, 1.0, ALU.mult, ALU.add))
            P.opi("vector", "sdv", lambda e: e.tensor_scalar(cl[:], cl[:], 1e-6, None, ALU.max, ALU.bypass))
            dv_w = P.mark("sdv")
            P.wait("scalar", "sdv", dv_w)
            P.op("scalar", lambda e: e.activation(out=Gt[:], in_=cl[:], func=AF.Ln))
            P.opi("scalar", "ssc", lambda e: e.activation(out=nrm[:], in_=Gt[:], func=AF.Exp, scale=0.5))
            M[k]["nrm"] = P.mark("ssc")

        def sc_trig(k):
            P.wait("scalar", "spo")
            P.op("scalar", lambda e: e.activation(out=y16[:], in_=thr[:], func=AF.Abs))
            P.op("scalar", lambda e: e.activation(out=tnx[:], in_=y16[:], func=AF.Sin, scale=-1.0, bias=cbias[:, 2:3]))
            P.opi("scalar", "ssc", lambda e: e.activation(out=tna[:], in_=thr[:], func=AF.Sin), drain=False)
            M[k]["trig"] = P.mark("ssc")

        def back_pe_trans(j):
            b, kk = divmod(j, n_chunks)
            P.wait("tensor", "sdv", M[j]["n2"])
            for grp in range(2):
                P.wait("tensor", "sdv")
                for s in range(2):
                    for cc in range(4):
                        c = grp * 4 + cc
                        fn = lambda e, s=s, c=c, cc=cc: e.transpose(
                            PD[:, cc * 256 + s * 128: cc * 256 + s * 128 + 128],
                            Srt[:, s * D + c * 128: s * D + (c + 1) * 128], identbf[:])
                        (P.opi("tensor", "spe", fn) if (s == 1 and cc == 3) else P.op("tensor", fn))
                P.wait("vector", "spe")
                for cc in range(4):
                    c = grp * 4 + cc
                    fn = lambda e, c=c, cc=cc, b=b: e.tensor_scalar(
                        h2q[:, c * 256: (c + 1) * 256],
                        PD[:, cc * 256: (cc + 1) * 256],
                        s1b_col[:, b * 8 + c:b * 8 + c + 1],
                        sbb_col[:, b * 8 + c:b * 8 + c + 1],
                        ALU.mult, ALU.add)
                    (P.opi("vector", "sdv", fn, drain=(cc == 0)) if cc == 3
                     else P.op("vector", fn, drain=(cc == 0)))
            M[j]["ln2tr"] = P.mark("spe")
            M[j]["h2q"] = P.mark("sdv")

        def back_pe_ffn(j):
            P.wait("tensor", "sdv", M[j]["h2q"])
            for si in range(4):
                dh = si % 2
                P.op("tensor", lambda e, si=si, dh=dh: e.matmul(
                    f2slots[si], onesbf[:], b2q_row[:, dh * 512:(dh + 1) * 512],
                    start=True, stop=False))
            relu_marks = {}
            cast_marks = {}
            for g in range(4):
                for pr in range(2):
                    r = g * 2 + pr
                    if r >= 1:
                        P.wait("tensor", "ssc", relu_marks[r - 1])
                    for q in range(4):
                        hh = r * 4 + q
                        for dd in range(8):
                            fn = lambda e, hh=hh, q=q, dd=dd: e.matmul(
                                PA[:, q * 256:(q + 1) * 256],
                                w1T[:, dd * H + hh * 128: dd * H + (hh + 1) * 128],
                                tsl(h2q, dd), start=(dd == 0 and q % 2 == 0),
                                stop=(dd == 7), skip_group_check=True)
                            (P.opi("tensor", "spe", fn) if (q == 3 and dd == 7) else P.op("tensor", fn))
                    P.wait("scalar", "spe")
                    for q in range(4):
                        hh = r * 4 + q
                        fn = lambda e, hh=hh, q=q: e.activation(
                            out=y16[:, (hh % 8) * 256:(hh % 8 + 1) * 256],
                            in_=PA[:, q * 256:(q + 1) * 256],
                            func=AF.Relu, bias=b1q_col[:, hh:hh + 1])
                        (P.opi("scalar", "ssc", fn, drain=(q == 0)) if q == 3
                         else P.op("scalar", fn, drain=(q == 0)))
                    relu_marks[r] = P.mark("ssc")
                    P.wait("gpsimd", "ssc", relu_marks[r])
                    P.wait("gpsimd", "spe")
                    hf = r % 2
                    P.opi("gpsimd", "spo", lambda e, hf=hf: e.tensor_copy(
                        r1q[:, hf * 1024:(hf + 1) * 1024],
                        y16[:, hf * 1024:(hf + 1) * 1024]), drain=True)
                    cast_marks[r] = P.mark("spo")
                for sub in range(2):
                    P.wait("tensor", "spo", cast_marks[g * 2 + sub])
                    for hh4 in range(4):
                        hh8 = sub * 4 + hh4
                        hh = g * 8 + hh8
                        for si in range(4):
                            s, dh = si // 2, si % 2
                            fn = lambda e, hh=hh, hh8=hh8, si=si, s=s, dh=dh: e.matmul(
                                f2slots[si],
                                r1q[:, hh8 * 256 + s * 128: hh8 * 256 + s * 128 + 128],
                                w2T[:, hh * D + dh * 512: hh * D + (dh + 1) * 512],
                                start=False, stop=(hh == 31))
                            (P.opi("tensor", "spe", fn) if (si == 3 and hh4 == 3) else P.op("tensor", fn))
            M[j]["ffn2"] = P.mark("spe")

        def scan_main(k):
            b, kk = divmod(k, n_chunks)
            P.wait("vector", "ssc", M[k]["trig"])
            if k >= 1:
                P.wait("vector", "spe", M[k - 1]["ln2tr"])
            P.op("vector", lambda e: e.tensor_tensor(gx[:], gx[:], nrm[:], ALU.mult))
            P.op("vector", lambda e: e.tensor_tensor(eneg[:], gx[:], eneg[:], ALU.mult))
            P.op("vector", lambda e: e.tensor_tensor(xtr[:], xh_r[:], tnx[:], ALU.mult))
            P.op("vector", lambda e: e.tensor_tensor(xti[:], xh_i[:], tna[:], ALU.mult))
            P.op("vector", lambda e: e.tensor_tensor(xtr[:], xtr[:], xti[:], ALU.add))
            P.op("vector", lambda e: e.tensor_tensor(xtr[:], xtr[:], eneg[:], ALU.mult))
            P.op("vector", lambda e: e.tensor_tensor(xti[:], xh_i[:], tnx[:], ALU.mult))
            P.op("vector", lambda e: e.tensor_tensor(xsc[:], xh_r[:], tna[:], ALU.mult))
            P.op("vector", lambda e: e.tensor_tensor(xti[:], xti[:], xsc[:], ALU.subtract))
            P.op("vector", lambda e: e.tensor_tensor(xti[:], xti[:], eneg[:], ALU.mult))
            for c in range(8):
                init_r = 0.0 if kk == 0 else hcarry[:, c:c + 1]
                init_i = 0.0 if kk == 0 else hcarry[:, 8 + c:9 + c]
                P.op("vector", lambda e, c=c, init_r=init_r: e.tensor_tensor_scan(
                    tsl(Srt, c), tsl(xtr, c), tsl(xtr, c), init_r, ALU.add, ALU.bypass),
                    drain=(c == 0))
                P.op("vector", lambda e, c=c, init_i=init_i: e.tensor_tensor_scan(
                    tsl(Sit, c), tsl(xti, c), tsl(xti, c), init_i, ALU.add, ALU.bypass),
                    drain=False)
            P.op("vector", lambda e: e.tensor_tensor(tnx[:], tnx[:], epos[:], ALU.mult))
            P.op("vector", lambda e: e.tensor_tensor(tna[:], tna[:], epos[:], ALU.mult))
            P.op("vector", lambda e: e.tensor_tensor(xtr[:], tnx[:], Srt[:], ALU.mult))
            P.op("vector", lambda e: e.tensor_tensor(xsc[:], tna[:], Sit[:], ALU.mult))
            P.op("vector", lambda e: e.tensor_tensor(xtr[:], xtr[:], xsc[:], ALU.subtract))
            P.op("vector", lambda e: e.tensor_tensor(xti[:], tnx[:], Sit[:], ALU.mult))
            P.op("vector", lambda e: e.tensor_tensor(xsc[:], tna[:], Srt[:], ALU.mult))
            P.op("vector", lambda e: e.tensor_tensor(xti[:], xti[:], xsc[:], ALU.add))
            P.op("vector", lambda e: e.tensor_copy(hcarry[:, 0:8], xtr[:, 255:2048:256]))
            P.opi("vector", "sdv", lambda e: e.tensor_copy(hcarry[:, 8:16], xti[:, 255:2048:256]), drain=False)
            M[k]["h"] = P.mark("sdv")

        def back_dve_tail_casts(j):
            P.wait("vector", "spe", M[j]["ffn2"])
            for si in range(2):
                dh = si % 2
                fn = lambda e, si=si, dh=dh: e.tensor_copy(
                    y16[:, dh * 512: (dh + 1) * 512], f2slots[si])
                (P.op("vector", fn, drain=True) if si == 0
                 else P.op("vector", fn, drain=False))

        def back_dve_tail_casts_pc(j):
            par = j % 2
            for si in range(2, 4):
                dh = si % 2
                fn = lambda e, si=si, dh=dh: e.tensor_copy(
                    y16[:, D + dh * 512: D + (dh + 1) * 512], f2slots[si])
                (P.opi("vector", "sdv", fn, drain=False) if si == 3
                 else P.op("vector", fn, drain=True))
            M[j]["y16"] = P.mark("sdv")
            P.wait("gpsimd", "sdv", M[j]["y16"])
            P.wait("gpsimd", M[j]["qm"][0], M[j]["qm"][1])
            P.op("gpsimd", lambda e, par=par: e.tensor_tensor(
                y16[:], y16[:], smaskp[par][:], ALU.mult), drain=True)
            P.op("gpsimd", lambda e: e.tensor_tensor(y16[:, 0:D], y16[:, 0:D], gbb[:], ALU.mult), drain=True)
            P.opi("gpsimd", "spo", lambda e: e.tensor_tensor(y16[:, D:], y16[:, D:], gbb[:], ALU.mult), drain=True)
            M[j]["po_y"] = P.mark("spo")

        def back_dve_tail_store(j):
            P.wait("vector", "spo", M[j]["po_y"])
            P.wait("vector", "qo")
            P.op("vector", lambda e: e.tensor_tensor(cl[:, 0:D], y16[:, 0:D], xmid[:, 0:D], ALU.add), drain=True)
            P.opi("vector", "sdv", lambda e: e.tensor_tensor(cl[:, D:], y16[:, D:], xmid[:, D:], ALU.add), drain=False)
            M[j]["out"] = P.mark("sdv")
            b, kk = divmod(j, n_chunks)
            P.wait("gpsimd", "sdv", M[j]["out"])
            P.dma("gpsimd", "qo", lambda e, b=b, kk=kk: e.dma_start(out=yr4[b, kk * 2], in_=cl[:, 0:D]))
            P.dma("gpsimd", "qo", lambda e, b=b, kk=kk: e.dma_start(out=yr4[b, kk * 2 + 1], in_=cl[:, D:]))

        pre()
        if NK > 0:
            batch_setup(0)
            dma_x(0)
            dma_mask(0, None)
            dve_stats_ln1(0)
            sc_ln1_rows(0)
            dve_n1(0, None)
            for kslot in range(NK + 2):
                k = kslot
                j = kslot - 1
                jj = kslot - 2
                if 0 <= jj < NK:
                    back_dve_tail_casts(jj)
                if k < NK:
                    front(k, mid=(lambda jj=jj: back_dve_tail_casts_pc(jj)) if 0 <= jj < NK else None)
                elif 0 <= jj < NK:
                    back_dve_tail_casts_pc(jj)
                if 0 <= jj < NK:
                    back_dve_tail_store(jj)
                if 1 <= k < NK:
                    dma_mask(k, M[k - 2]["po_y"] if k >= 2 else None)
                if k < NK:
                    bk, kkk = divmod(k, n_chunks)
                    if kkk == 1 and bk > 0:
                        batch_setup(bk)
                if 0 <= j < NK:
                    back_pe_outproj(j, M[min(k, NK - 1)]["tanh"])
                if k < NK:
                    scan_prep(k)
                if 0 <= j < NK:
                    back_dve_early(j)
                if k + 1 < NK:
                    dma_x(k + 1)
                    dve_stats_ln1(k + 1)
                if 0 <= j < NK:
                    sc_ln2rows(j)
                if k + 1 < NK:
                    sc_ln1_rows(k + 1)
                if 0 <= j < NK:
                    back_dve_n2(j)
                    back_pe_trans(j)
                if k + 1 < NK:
                    dve_n1(k + 1, M[k]["trans"])
                if k < NK:
                    sc_exps(k)
                    sc_trig(k)
                if 0 <= j < NK:
                    back_pe_ffn(j)
                if k < NK:
                    scan_main(k)
        P.wait("gpsimd", "qo")

        @block.sync
        def _(e):
            for f in P.ops["sync"]:
                f(e)

        @block.gpsimd
        def _(e):
            for f in P.ops["gpsimd"]:
                f(e)

        @block.vector
        def _(e):
            for f in P.ops["vector"]:
                f(e)

        @block.scalar
        def _(e):
            for f in P.ops["scalar"]:
                f(e)

        @block.tensor
        def _(e):
            for f in P.ops["tensor"]:
                f(e)

    return nc


_CACHE = {}


def _host_constants():
    if "mask" not in _CACHE:
        import jax
        cpu = jax.local_devices(backend="cpu")[0]
        with jax.default_device(cpu):
            keep = np.asarray(jax.random.bernoulli(
                jax.random.key(42, impl="threefry2x32"), 0.8, (32, S, D)))
        _CACHE["mask"] = (keep.astype(np.float32) * np.float32(1.25)).astype(
            ml_dtypes.float8_e4m3)
        _CACHE["ident"] = np.eye(128, dtype=np.float32)
        _CACHE["ramp"] = np.tile(np.arange(1, T + 1, dtype=np.float32), (128, 8))
    return _CACHE["mask"], _CACHE["ident"], _CACHE["ramp"]


def kernel(**inputs):
    mask, ident, ramp = _host_constants()
    if "nc" not in _CACHE:
        _CACHE["nc"] = build_program()
    nc = _CACHE["nc"]

    full = {
        "wmod": inputs["L0_wmod"], "W_in": inputs["L0_attn_W_in"],
        "b_in": inputs["L0_attn_b_in"], "Wg_x": inputs["L0_attn_Wg_x"],
        "bg_x": inputs["L0_attn_bg_x"], "Wg_a": inputs["L0_attn_Wg_a"],
        "bg_a": inputs["L0_attn_bg_a"], "a_real": inputs["L0_attn_a_real_param"],
        "a_imag": inputs["L0_attn_a_imag_param"], "W_out": inputs["L0_attn_W_out"],
        "b_out": inputs["L0_attn_b_out"], "w1": inputs["L0_ffn_w1"],
        "b1": inputs["L0_ffn_b1"], "w2": inputs["L0_ffn_w2"],
        "b2": inputs["L0_ffn_b2"], "ident": ident, "ramp": ramp,
    }
    full = {k: np.ascontiguousarray(np.asarray(v, dtype=np.float32))
            for k, v in full.items()}
    x = np.asarray(inputs["L0_attn_x"], dtype=np.float32)
    cv = np.asarray(inputs["L0_c"], dtype=np.float32)

    in_maps = []
    for core in range(NCORES):
        m = dict(full)
        m["x"] = np.ascontiguousarray(x[core * BC:(core + 1) * BC])
        m["cvec"] = np.ascontiguousarray(cv[core * BC:(core + 1) * BC])
        m["mask"] = np.ascontiguousarray(mask[core * BC:(core + 1) * BC])
        in_maps.append(m)

    res = run_bass_kernel_spmd(nc, in_maps, core_ids=list(range(NCORES)))
    return np.concatenate([r["y"] for r in res.results], axis=0)


if __name__ == "__main__":
    build_program()
    print("program built ok")


# version 14 (final)
# speedup vs baseline: 1.0465x; 1.0023x over previous
import contextlib
import numpy as np
import ml_dtypes
import concourse.bass as bass
import concourse.mybir as mybir
from concourse.bass_utils import run_bass_kernel_spmd

F32 = mybir.dt.float32
BF16 = mybir.dt.bfloat16
F16 = mybir.dt.float16
F8 = mybir.dt.float8e4
I32 = mybir.dt.int32
AF = mybir.ActivationFunctionType
ALU = mybir.AluOpType

NCORES = 8
BC = 4
S = 2048
D = 1024
DH = 2048
DI = 1024
NB = 16
H = 4096
DMOD = 6144
T = 256
NCH = S // T
EPS = 1e-5
PI = float(np.pi)


class Prog:
    ENGS = ("sync", "gpsimd", "vector", "scalar", "tensor")

    def __init__(self, sems):
        self.ops = {e: [] for e in self.ENGS}
        self.sems = sems
        self.cnt = {k: 0 for k in sems}

    AUTODRAIN = ("vector", "scalar")

    def op(self, eng, fn, drain=None):
        if drain is None:
            drain = eng in self.AUTODRAIN
        if drain:
            self.ops[eng].append(lambda e: e.drain())
        self.ops[eng].append(fn)

    def opi(self, eng, sem, fn, n=1, drain=None):
        h = self.sems[sem]
        self.cnt[sem] += n
        if drain is None:
            drain = eng in self.AUTODRAIN
        if drain:
            self.ops[eng].append(lambda e: e.drain())

        def wrapped(e, fn=fn, h=h, n=n):
            fn(e).then_inc(h, n)

        self.ops[eng].append(wrapped)

    def dma(self, eng, sem, fn):
        self.opi(eng, sem, fn, 16, drain=False)

    def wait(self, eng, sem, val=None):
        h = self.sems[sem]
        v = self.cnt[sem] if val is None else val
        self.ops[eng].append(lambda e, h=h, v=v: e.wait_ge(h, v))

    def mark(self, sem):
        return self.cnt[sem]


def build_program(n_seq=BC, n_chunks=NCH, pre_stage=99):
    nc = bass.Bass("TRN2", target_bir_lowering=False)

    dx = nc.dram_tensor("x", [BC, S, D], F32, kind="ExternalInput").ap()
    dmask = nc.dram_tensor("mask", [BC, S, D], F8, kind="ExternalInput").ap()
    dc = nc.dram_tensor("cvec", [BC, D], F32, kind="ExternalInput").ap()
    dwmod = nc.dram_tensor("wmod", [D, DMOD], F32, kind="ExternalInput").ap()
    dwin = nc.dram_tensor("W_in", [D, DH], F32, kind="ExternalInput").ap()
    dbin = nc.dram_tensor("b_in", [DH], F32, kind="ExternalInput").ap()
    dwgx = nc.dram_tensor("Wg_x", [NB, 128, 64], F32, kind="ExternalInput").ap()
    dbgx = nc.dram_tensor("bg_x", [DI], F32, kind="ExternalInput").ap()
    dwga = nc.dram_tensor("Wg_a", [NB, 128, 64], F32, kind="ExternalInput").ap()
    dbga = nc.dram_tensor("bg_a", [DI], F32, kind="ExternalInput").ap()
    darp = nc.dram_tensor("a_real", [DI], F32, kind="ExternalInput").ap()
    daip = nc.dram_tensor("a_imag", [DI], F32, kind="ExternalInput").ap()
    dwout = nc.dram_tensor("W_out", [DH, D], F32, kind="ExternalInput").ap()
    dbout = nc.dram_tensor("b_out", [D], F32, kind="ExternalInput").ap()
    dw1 = nc.dram_tensor("w1", [H, D], F32, kind="ExternalInput").ap()
    db1 = nc.dram_tensor("b1", [H], F32, kind="ExternalInput").ap()
    dw2 = nc.dram_tensor("w2", [D, H], F32, kind="ExternalInput").ap()
    db2 = nc.dram_tensor("b2", [D], F32, kind="ExternalInput").ap()
    dident = nc.dram_tensor("ident", [128, 128], F32, kind="ExternalInput").ap()
    dramp = nc.dram_tensor("ramp", [128, 2048], F32, kind="ExternalInput").ap()
    dy = nc.dram_tensor("y", [BC, S, D], F32, kind="ExternalOutput").ap()

    dwin_bf = nc.dram_tensor("win_bf", [D, DH], BF16, kind="Internal").ap()
    dx_bf = nc.dram_tensor("x_bf", [BC, S, D], BF16, kind="Internal").ap()
    dwout_bf = nc.dram_tensor("wout_bf", [DH, D], BF16, kind="Internal").ap()

    xr4 = dx.rearrange("b (n p) d -> b n p d", p=128)
    xbf4 = dx_bf.rearrange("b (n p) d -> b n p d", p=128)
    mr4 = dmask.rearrange("b (n p) d -> b n p d", p=128)
    yr4 = dy.rearrange("b (n p) d -> b n p d", p=128)
    winr = dwin.rearrange("(c p) m -> c p m", p=128)
    winbfr = dwin_bf.rearrange("(c p) m -> c p m", p=128)
    woutr = dwout.rearrange("(c p) m -> c p m", p=128)
    woutbfr = dwout_bf.rearrange("(c p) m -> c p m", p=128)
    w1r = dw1.rearrange("(c p) m -> c p m", p=128)
    w2r = dw2.rearrange("(c p) m -> c p m", p=128)
    wmodr = dwmod.rearrange("(c p) m -> c p m", p=128)

    with contextlib.ExitStack() as st:
        def sbt(name, shape, dt):
            return st.enter_context(nc.sbuf_tensor(name, shape, dt))

        def pst(name, shape, dt):
            return st.enter_context(nc.psum_tensor(name, shape, dt))

        w1T = sbt("w1T", [128, 8 * H], F8)
        w2T = sbt("w2T", [128, 32 * D], F8)
        wgx = sbt("wgx", [128, NB * 64], BF16)
        wga = sbt("wga", [128, NB * 64], BF16)
        identbf = sbt("identbf", [128, 128], BF16)
        identf = sbt("identf", [128, 128], F32)
        ones128 = sbt("ones128", [128, 128], BF16)
        rampbf = sbt("rampbf", [128, 2048], BF16)
        onesbf = sbt("onesbf", [1, 128], BF16)

        bcol_in = sbt("bcol_in", [128, 16], F32)
        b1q_col = sbt("b1q_col", [128, 32], F32)
        Lh_col = sbt("Lh_col", [128, 8], F32)
        L2_col = sbt("L2_col", [128, 8], F32)
        aimh_col = sbt("aimh_col", [128, 8], F32)
        bgx2_col = sbt("bgx2_col", [128, 8], F32)
        bga2_col = sbt("bga2_col", [128, 8], F32)
        b2q_row = sbt("b2q_row", [1, D], BF16)
        bout_row = sbt("bout_row", [1, D], BF16)

        mod_cols = sbt("mod_cols", [128, 192], F32)
        s1a_col = sbt("s1a_col", [128, 32], F32)
        sa_col = sbt("sa_col", [128, 32], F32)
        s1b_col = sbt("s1b_col", [128, 32], F32)
        sbb_col = sbt("sbb_col", [128, 32], F32)
        gacl_col = sbt("gacl_col", [128, 32], F32)
        gbcl_col = sbt("gbcl_col", [128, 32], F32)
        gab = sbt("gab", [128, D], BF16)
        gbb = sbt("gbb", [128, D], BF16)
        cfm = sbt("cfm", [128, 32], BF16)
        msc = sbt("msc", [128, 8], F32)
        cbias = sbt("cbias", [128, 4], F32)

        winb = sbt("winb", [128, 2 * DH], BF16)
        woutb = sbt("woutb", [128, 4 * D], BF16)

        xkeep = [sbt("xkeep0", [128, 2 * D], BF16), sbt("xkeep1", [128, 2 * D], BF16)]
        smaskp = [sbt("smask0", [128, 2 * D], F8), sbt("smask1", [128, 2 * D], F8)]
        sstat = sbt("sstat", [128, 24], F32)
        sstat2 = sbt("sstat2", [128, 24], F32)
        saggr = sbt("saggr", [128, 8], F32)
        srow = sbt("srow", [128, 8], F32)
        n1 = sbt("n1", [128, 2 * D], BF16)
        htld = sbt("htld", [128, DH], BF16)
        xh_r = sbt("xh_r", [128, 2048], BF16)
        xh_i = sbt("xh_i", [128, 2048], BF16)
        tnx = sbt("tnx", [128, 2048], BF16)
        tna = sbt("tna", [128, 2048], BF16)
        gx = sbt("gx", [128, 2048], BF16)
        Gt = sbt("Gt", [128, 2048], F32)
        cl = sbt("cl", [128, 2048], F32)
        thr = sbt("thr", [128, 2048], F16)
        riw = sbt("riw", [128, 1024], I32)
        nrm = sbt("nrm", [128, 2048], BF16)
        epos = sbt("epos", [128, 2048], BF16)
        eneg = sbt("eneg", [128, 2048], BF16)
        xtr = sbt("xtr", [128, 2048], BF16)
        xti = sbt("xti", [128, 2048], BF16)
        xsc = sbt("xsc", [128, 2048], BF16)
        Srt = sbt("Srt", [128, 2048], BF16)
        Sit = sbt("Sit", [128, 2048], BF16)
        hcarry = sbt("hcarry", [128, 16], BF16)
        xmid = sbt("xmid", [128, 2 * D], BF16)
        h2q = sbt("h2q", [128, 2048], F8)
        r1q = sbt("r1q", [128, 2048], F8)
        y16 = sbt("y16", [128, 2 * D], F16)

        PA = pst("PA", [128, 1024], F32)
        PB = pst("PB", [128, 1024], F32)
        PC = pst("PC", [128, 1024], F32)
        PE2 = pst("PE2", [128, 512], F32)
        PD = pst("PD", [128, 1024], BF16)

        sems = {}
        for name in ("qxa", "qxb", "qma", "qmb", "qo", "qs", "qwa", "qwb", "qva", "qvb", "spe", "sdv", "ssc", "spo"):
            sems[name] = st.enter_context(nc.semaphore(name))
        block = st.enter_context(nc.Block())
        P = Prog(sems)

        def tsl(tile, c, w=256):
            return tile[:, c * w:(c + 1) * w]

        def pre():
            P.dma("gpsimd", "qo", lambda e: e.dma_start(out=identf[:], in_=dident))
            P.dma("gpsimd", "qo", lambda e: e.dma_start(out=Gt[:], in_=dramp))
            P.op("vector", lambda e: e.memset(ones128[:], 1.0))
            P.op("vector", lambda e: e.memset(onesbf[:], 1.0))
            P.op("vector", lambda e: e.memset(cbias[:, 0:1], EPS))
            P.op("vector", lambda e: e.memset(cbias[:, 1:2], 1.0))
            P.op("vector", lambda e: e.memset(cbias[:, 2:3], PI / 2))

            P.wait("vector", "qo")
            P.op("vector", lambda e: e.tensor_copy(identbf[:], identf[:]))
            P.opi("vector", "sdv", lambda e: e.tensor_copy(rampbf[:], Gt[:]))
            if pre_stage < 1:
                return
            P.wait("gpsimd", "sdv")
            P.dma("gpsimd", "qo", lambda e: e.dma_start(
                out=cl[0:16, 1024:1152], in_=dbin.rearrange("(a p) -> a p", p=128)))
            P.dma("gpsimd", "qo", lambda e: e.dma_start(
                out=cl[32:64, 1024:1152], in_=db1.rearrange("(a p) -> a p", p=128)))
            P.dma("gpsimd", "qo", lambda e: e.dma_start(
                out=cl[64:72, 1024:1152], in_=daip.rearrange("(a p) -> a p", p=128)))
            P.dma("gpsimd", "qo", lambda e: e.dma_start(
                out=Gt[0:8, 1024:1152], in_=dbgx.rearrange("(a p) -> a p", p=128)))
            P.dma("gpsimd", "qo", lambda e: e.dma_start(
                out=Gt[32:40, 1024:1152], in_=dbga.rearrange("(a p) -> a p", p=128)))
            P.dma("gpsimd", "qo", lambda e: e.dma_start(
                out=Gt[64:72, 1024:1152], in_=darp.rearrange("(a p) -> a p", p=128)))
            P.dma("gpsimd", "qo", lambda e: e.dma_start(
                out=cl[0:1, 0:1024], in_=db2.rearrange("(o d) -> o d", o=1)))

            P.wait("vector", "qo")
            P.op("vector", lambda e: e.tensor_copy(r1q[32:64, 0:128], cl[32:64, 1024:1152]))
            P.op("vector", lambda e: e.tensor_copy(cl[32:64, 1024:1152], r1q[32:64, 0:128]))
            P.op("vector", lambda e: e.tensor_copy(r1q[0:1, 0:1024], cl[0:1, 0:1024]))
            P.opi("vector", "sdv", lambda e: e.tensor_copy(b2q_row[:], r1q[0:1, 0:1024]))
            P.wait("gpsimd", "sdv")
            P.dma("gpsimd", "qo", lambda e: e.dma_start(
                out=cl[0:1, 0:1024], in_=dbout.rearrange("(o d) -> o d", o=1)))
            P.wait("vector", "qo")
            P.opi("vector", "sdv", lambda e: e.tensor_copy(bout_row[:], cl[0:1, 0:1024]))
            P.opi("vector", "sdv", lambda e: e.tensor_scalar(
                cl[64:72, 1152:1280], cl[64:72, 1024:1152], 0.5 * PI, None, ALU.mult, ALU.bypass))
            P.wait("scalar", "sdv")
            P.op("scalar", lambda e: e.activation(out=Gt[64:72, 1152:1280], in_=Gt[64:72, 1024:1152], func=AF.Abs))
            P.op("scalar", lambda e: e.activation(out=Gt[64:72, 1280:1408], in_=Gt[64:72, 1152:1280], func=AF.Exp, scale=-1.0))
            P.op("scalar", lambda e: e.activation(out=Gt[64:72, 1152:1280], in_=Gt[64:72, 1280:1408], func=AF.Ln, bias=cbias[64:72, 1:2]))
            P.opi("scalar", "ssc", lambda e: e.activation(out=Gt[64:72, 1280:1408], in_=Gt[64:72, 1024:1152], func=AF.Relu))
            P.wait("vector", "ssc")
            P.opi("vector", "sdv", lambda e: e.tensor_tensor(
                Gt[64:72, 1408:1536], Gt[64:72, 1152:1280], Gt[64:72, 1280:1408], ALU.add))
            P.wait("scalar", "sdv")
            P.opi("scalar", "ssc", lambda e: e.activation(out=Gt[64:72, 1152:1280], in_=Gt[64:72, 1408:1536], func=AF.Ln))
            P.wait("vector", "ssc")
            P.op("vector", lambda e: e.tensor_scalar(
                Gt[64:72, 1152:1280], Gt[64:72, 1152:1280], 0.5, None, ALU.mult, ALU.bypass))
            P.op("vector", lambda e: e.tensor_scalar(
                Gt[0:8, 1152:1280], Gt[0:8, 1024:1152], 0.5, None, ALU.mult, ALU.bypass))
            P.opi("vector", "sdv", lambda e: e.tensor_scalar(
                Gt[32:40, 1152:1280], Gt[32:40, 1024:1152], 0.5, None, ALU.mult, ALU.bypass))
            P.wait("tensor", "sdv")
            P.op("tensor", lambda e: e.transpose(PA[:, 0:128], cl[:, 1024:1152], identf[:]))
            P.op("tensor", lambda e: e.transpose(PA[:, 128:256], cl[:, 1152:1280], identf[:]))
            P.opi("tensor", "spe", lambda e: e.transpose(PA[:, 256:384], Gt[:, 1152:1280], identf[:]))
            P.wait("vector", "spe")
            P.op("vector", lambda e: e.tensor_copy(bcol_in[:], PA[:, 0:16]))
            P.op("vector", lambda e: e.tensor_copy(b1q_col[:], PA[:, 32:64]))
            P.op("vector", lambda e: e.tensor_copy(aimh_col[:], PA[:, 192:200]))
            P.op("vector", lambda e: e.tensor_copy(bgx2_col[:], PA[:, 256:264]))
            P.op("vector", lambda e: e.tensor_copy(bga2_col[:], PA[:, 288:296]))
            P.op("vector", lambda e: e.tensor_copy(Lh_col[:], PA[:, 320:328]))
            P.opi("vector", "sdv", lambda e: e.tensor_scalar(L2_col[:], PA[:, 320:328], 2.0, None, ALU.mult, ALU.bypass))

            if pre_stage < 2:
                return
            P.wait("gpsimd", "sdv")
            P.dma("gpsimd", "qo", lambda e: e.dma_start(
                out=cl[:, 0:1024].rearrange("p (n j) -> p n j", j=64),
                in_=dwgx.rearrange("n h j -> h n j")))
            P.dma("gpsimd", "qo", lambda e: e.dma_start(
                out=Gt[:, 0:1024].rearrange("p (n j) -> p n j", j=64),
                in_=dwga.rearrange("n h j -> h n j")))
            P.wait("vector", "qo")
            P.op("vector", lambda e: e.tensor_copy(wgx[:], cl[:, 0:1024]))
            P.opi("vector", "sdv", lambda e: e.tensor_copy(wga[:], Gt[:, 0:1024]))

            if pre_stage < 3:
                return
            for c in range(8):
                P.wait("gpsimd", "sdv")
                P.dma("gpsimd", "qo", lambda e, c=c: e.dma_start(out=Gt[:], in_=winr[c]))
                P.wait("vector", "qo")
                P.opi("vector", "sdv", lambda e: e.tensor_copy(winb[:, 0:2048], Gt[:]))
                P.wait("gpsimd", "sdv")
                P.dma("gpsimd", "qs", lambda e, c=c: e.dma_start(out=winbfr[c], in_=winb[:, 0:2048]))
                P.wait("vector", "qs")
            if pre_stage < 4:
                return
            for c in range(16):
                P.wait("gpsimd", "sdv")
                P.dma("gpsimd", "qo", lambda e, c=c: e.dma_start(out=Gt[:, 0:1024], in_=woutr[c]))
                P.wait("vector", "qo")
                P.opi("vector", "sdv", lambda e: e.tensor_copy(winb[:, 0:1024], Gt[:, 0:1024]))
                P.wait("gpsimd", "sdv")
                P.dma("gpsimd", "qs", lambda e, c=c: e.dma_start(out=woutbfr[c], in_=winb[:, 0:1024]))
                P.wait("vector", "qs")

            if pre_stage < 5:
                return
            for hh in range(32):
                P.wait("gpsimd", "sdv")
                P.dma("gpsimd", "qo", lambda e, hh=hh: e.dma_start(out=Gt[:, 0:1024], in_=w1r[hh]))
                P.wait("vector", "qo")
                P.op("vector", lambda e: e.tensor_copy(r1q[:, 0:1024], Gt[:, 0:1024]))
                P.opi("vector", "sdv", lambda e: e.tensor_copy(xh_r[:, 0:1024], r1q[:, 0:1024]))
                P.wait("tensor", "sdv")
                for c in range(8):
                    fn = lambda e, c=c: e.transpose(
                        PD[:, c * 128:(c + 1) * 128], xh_r[:, c * 128:(c + 1) * 128], identbf[:])
                    (P.opi("tensor", "spe", fn) if c == 7 else P.op("tensor", fn))
                P.wait("vector", "spe")
                for c in range(8):
                    fn = lambda e, c=c, hh=hh: e.tensor_copy(
                        w1T[:, c * H + hh * 128: c * H + (hh + 1) * 128], PD[:, c * 128:(c + 1) * 128])
                    (P.opi("vector", "sdv", fn) if c == 7 else P.op("vector", fn))
                P.wait("tensor", "sdv")

            if pre_stage < 6:
                return
            for c in range(8):
                P.wait("gpsimd", "sdv")
                P.dma("gpsimd", "qo", lambda e, c=c: e.dma_start(out=Gt[:], in_=w2r[c][:, 0:2048]))
                P.dma("gpsimd", "qo", lambda e, c=c: e.dma_start(out=cl[:], in_=w2r[c][:, 2048:4096]))
                P.wait("vector", "qo")
                for half in range(2):
                    src = Gt if half == 0 else cl
                    P.op("vector", lambda e, src=src: e.tensor_copy(r1q[:], src[:]))
                    P.opi("vector", "sdv", lambda e: e.tensor_copy(xh_r[:], r1q[:]))
                    P.wait("tensor", "sdv")
                    for j2 in range(2):
                        for jj in range(8):
                            j = j2 * 8 + jj
                            fn = lambda e, j=j, jj=jj: e.transpose(
                                PD[:, jj * 128:(jj + 1) * 128], xh_r[:, j * 128:(j + 1) * 128], identbf[:])
                            (P.opi("tensor", "spe", fn) if jj == 7 else P.op("tensor", fn))
                        P.wait("vector", "spe")
                        for jj in range(8):
                            hh = half * 16 + j2 * 8 + jj
                            fn = lambda e, jj=jj, hh=hh, c=c: e.tensor_copy(
                                w2T[:, hh * D + c * 128: hh * D + (c + 1) * 128], PD[:, jj * 128:(jj + 1) * 128])
                            (P.opi("vector", "sdv", fn) if jj == 7 else P.op("vector", fn))
                        P.wait("tensor", "sdv")

            if pre_stage < 7:
                return
            for i in range(32):
                bb, nn_ = i // 8, (i % 8) * 2
                P.wait("gpsimd", "sdv")
                P.dma("gpsimd", "qo", lambda e, bb=bb, nn_=nn_: e.dma_start(out=Gt[:, 0:1024], in_=xr4[bb, nn_]))
                P.dma("gpsimd", "qo", lambda e, bb=bb, nn_=nn_: e.dma_start(out=cl[:, 0:1024], in_=xr4[bb, nn_ + 1]))
                P.wait("vector", "qo")
                P.op("vector", lambda e: e.tensor_copy(winb[:, 0:1024], Gt[:, 0:1024]))
                P.opi("vector", "sdv", lambda e: e.tensor_copy(winb[:, 1024:2048], cl[:, 0:1024]), drain=False)
                P.wait("gpsimd", "sdv")
                P.dma("gpsimd", "qs", lambda e, bb=bb, nn_=nn_: e.dma_start(out=xbf4[bb, nn_], in_=winb[:, 0:1024]))
                P.dma("gpsimd", "qs", lambda e, bb=bb, nn_=nn_: e.dma_start(out=xbf4[bb, nn_ + 1], in_=winb[:, 1024:2048]))
                P.wait("vector", "qs")

            P.wait("gpsimd", "spe")
            P.wait("gpsimd", "sdv")
            P.dma("gpsimd", "qo", lambda e: e.dma_start(out=cl[0:4, 0:1024], in_=dc))
            P.wait("scalar", "qo")
            P.opi("scalar", "ssc", lambda e: e.activation(out=Gt[0:4, 0:1024], in_=cl[0:4, 0:1024], func=AF.Silu))
            P.wait("tensor", "ssc")
            for c8 in range(8):
                fn = lambda e, c8=c8: e.transpose(
                    PB[:, c8 * 128:(c8 + 1) * 128], Gt[:, c8 * 128:(c8 + 1) * 128], identf[:])
                (P.opi("tensor", "spe", fn) if c8 == 7 else P.op("tensor", fn))
            P.wait("vector", "spe")
            for c8 in range(8):
                fn = lambda e, c8=c8: e.tensor_copy(
                    cfm[:, c8 * 4:(c8 + 1) * 4], PB[:, c8 * 128: c8 * 128 + 4])
                (P.opi("vector", "sdv", fn) if c8 == 7 else P.op("vector", fn))
            for d8 in range(8):
                for j3 in range(3):
                    P.wait("gpsimd", "sdv")
                    P.dma("gpsimd", "qo", lambda e, d8=d8, j3=j3: e.dma_start(
                        out=Gt[:], in_=wmodr[d8][:, j3 * 2048:(j3 + 1) * 2048]))
                    P.wait("vector", "qo")
                    P.wait("vector", "spe")
                    P.opi("vector", "sdv", lambda e: e.tensor_copy(xh_r[:], Gt[:]))
                    P.wait("tensor", "sdv")
                    for mcl in range(16):
                        gmc = j3 * 16 + mcl
                        fn = lambda e, gmc=gmc, mcl=mcl, d8=d8, j3=j3: e.matmul(
                            PE2[:, gmc * 4:(gmc + 1) * 4],
                            xh_r[:, mcl * 128:(mcl + 1) * 128],
                            cfm[:, d8 * 4:(d8 + 1) * 4],
                            start=(d8 == 0 and j3 == 0 and mcl == 0),
                            stop=(d8 == 7), skip_group_check=True)
                        (P.opi("tensor", "spe", fn) if mcl == 15 else P.op("tensor", fn))
            P.wait("vector", "spe")
            P.opi("vector", "sdv", lambda e: e.tensor_copy(mod_cols[:], PE2[:, 0:192]))

            for b in range(BC):
                P.op("vector", lambda e, b=b: e.tensor_scalar(
                    s1a_col[:, b * 8:(b + 1) * 8], mod_cols[:, 32 + b:64:4], 1.0, None, ALU.add, ALU.bypass))
                P.op("vector", lambda e, b=b: e.tensor_copy(
                    sa_col[:, b * 8:(b + 1) * 8], mod_cols[:, 0 + b:32:4]))
                P.op("vector", lambda e, b=b: e.tensor_scalar(
                    s1b_col[:, b * 8:(b + 1) * 8], mod_cols[:, 128 + b:160:4], 1.0, None, ALU.add, ALU.bypass))
                P.op("vector", lambda e, b=b: e.tensor_copy(
                    sbb_col[:, b * 8:(b + 1) * 8], mod_cols[:, 96 + b:128:4]))
                for (src0, dst) in ((64, gacl_col), (160, gbcl_col)):
                    P.op("vector", lambda e, b=b, src0=src0: e.tensor_scalar(
                        msc[:], mod_cols[:, src0 + b:src0 + 32:4], 0.0, None, ALU.is_ge, ALU.bypass))
                    P.op("vector", lambda e: e.tensor_scalar(
                        msc[:], msc[:], 2.0, -1.0, ALU.mult, ALU.add))
                    P.op("vector", lambda e, b=b, src0=src0: e.tensor_scalar(
                        srow[:], mod_cols[:, src0 + b:src0 + 32:4], -1.0, None, ALU.mult, ALU.bypass))
                    P.op("vector", lambda e, b=b, src0=src0, dst=dst: e.tensor_tensor(
                        dst[:, b * 8:(b + 1) * 8], mod_cols[:, src0 + b:src0 + 32:4],
                        srow[:], ALU.max))
                    P.op("vector", lambda e, b=b, dst=dst: e.tensor_scalar(
                        dst[:, b * 8:(b + 1) * 8], dst[:, b * 8:(b + 1) * 8],
                        0.25, None, ALU.max, ALU.bypass))
                    P.opi("vector", "sdv", lambda e, b=b, dst=dst: e.tensor_tensor(
                        dst[:, b * 8:(b + 1) * 8], dst[:, b * 8:(b + 1) * 8], msc[:], ALU.mult))

        def batch_setup(b):
            P.wait("vector", "spo")
            for (colt, dst) in ((gacl_col, gab), (gbcl_col, gbb)):
                for c8 in range(8):
                    fn = lambda e, colt=colt, b=b, c8=c8: e.tensor_scalar(
                        xsc[:, c8 * 128:(c8 + 1) * 128], ones128[:],
                        colt[:, b * 8 + c8:b * 8 + c8 + 1], None, ALU.mult, ALU.bypass)
                    (P.opi("vector", "sdv", fn, drain=(c8 == 0)) if c8 == 7
                     else P.op("vector", fn, drain=(c8 == 0)))
                P.wait("tensor", "sdv")
                for c8 in range(8):
                    fn = lambda e, c8=c8: e.transpose(
                        PD[:, c8 * 128:(c8 + 1) * 128], xsc[:, c8 * 128:(c8 + 1) * 128], identbf[:])
                    (P.opi("tensor", "spe", fn) if c8 == 7 else P.op("tensor", fn))
                P.wait("vector", "spe")
                P.opi("vector", "sdv", lambda e, dst=dst: e.tensor_copy(dst[:], PD[:]), drain=True)
                P.wait("tensor", "sdv")

        NK = n_seq * n_chunks
        M = [dict() for _ in range(NK + 2)]
        oslots = (PC[:, 0:512], PC[:, 512:1024], PE2[:], PA[:, 0:512])
        f2slots = (PB[:, 0:512], PB[:, 512:1024], PC[:, 0:512], PC[:, 512:1024])

        def dma_x(k):
            b, kk = divmod(k, n_chunks)
            par = k % 2
            qx = "qxa" if par == 0 else "qxb"
            P.wait("sync", "sdv")
            P.dma("sync", qx, lambda e, b=b, kk=kk, par=par: e.dma_start(
                out=xkeep[par][:, 0:D], in_=xbf4[b, kk * 2]))
            P.dma("sync", qx, lambda e, b=b, kk=kk, par=par: e.dma_start(
                out=xkeep[par][:, D:], in_=xbf4[b, kk * 2 + 1]))
            M[k]["qx"] = (qx, P.mark(qx))

        def dma_mask(k, gate):
            b, kk = divmod(k, n_chunks)
            par = k % 2
            qm = "qma" if par == 0 else "qmb"
            if gate is not None:
                P.wait("sync", "spo", gate)
            P.dma("sync", qm, lambda e, b=b, kk=kk, par=par: e.dma_start(
                out=smaskp[par][:, 0:D], in_=mr4[b, kk * 2]))
            P.dma("sync", qm, lambda e, b=b, kk=kk, par=par: e.dma_start(
                out=smaskp[par][:, D:], in_=mr4[b, kk * 2 + 1]))
            M[k]["qm"] = (qm, P.mark(qm))

        def dve_stats_ln1(k):
            par = k % 2
            P.wait("vector", M[k]["qx"][0], M[k]["qx"][1])
            for s in range(2):
                P.op("vector", lambda e, s=s, par=par: e.bn_stats(
                    sstat2[:, s * 12:s * 12 + 6], xkeep[par][:, s * D:s * D + 512]), drain=(s == 0))
                P.op("vector", lambda e, s=s, par=par: e.bn_stats(
                    sstat2[:, s * 12 + 6:s * 12 + 12], xkeep[par][:, s * D + 512:(s + 1) * D]), drain=False)
            for s in range(2):
                fn = lambda e, s=s: e.bn_aggr(saggr[:, s * 2:s * 2 + 2], sstat2[:, s * 12:s * 12 + 12])
                (P.opi("vector", "sdv", fn, drain=False) if s == 1 else P.op("vector", fn, drain=True))
            M[k]["stats"] = P.mark("sdv")

        def sc_ln1_rows(k):
            P.wait("scalar", "sdv", M[k]["stats"])
            for s in range(2):
                P.op("scalar", lambda e, s=s: e.activation(
                    out=srow[:, s:s + 1], in_=saggr[:, s * 2 + 1:s * 2 + 2], func=AF.Ln, bias=cbias[:, 0:1]))
                fn = lambda e, s=s: e.activation(
                    out=srow[:, s:s + 1], in_=srow[:, s:s + 1], func=AF.Exp, scale=-0.5)
                (P.opi("scalar", "ssc", fn) if s == 1 else P.op("scalar", fn))
            M[k]["ln1rows"] = P.mark("ssc")

        def dve_n1(k, trans_mark):
            par = k % 2
            P.wait("vector", "ssc", M[k]["ln1rows"])
            if trans_mark is not None:
                P.wait("vector", "spe", trans_mark)
            for s in range(2):
                P.op("vector", lambda e, s=s: e.tensor_tensor(
                    srow[:, 2 + s:3 + s], saggr[:, s * 2:s * 2 + 1], srow[:, s:s + 1], ALU.mult),
                    drain=(s == 0))
            for s in range(2):
                fn = lambda e, s=s, par=par: e.tensor_scalar(
                    n1[:, s * D:(s + 1) * D], xkeep[par][:, s * D:(s + 1) * D],
                    srow[:, s:s + 1], srow[:, 2 + s:3 + s], ALU.mult, ALU.subtract)
                (P.opi("vector", "sdv", fn, drain=False) if s == 1 else P.op("vector", fn, drain=True))
            M[k]["n1"] = P.mark("sdv")

        def front(k, mid=None):
            b, kk = divmod(k, n_chunks)
            P.wait("tensor", "sdv", M[k]["n1"])
            for grp in range(2):
                if grp == 1:
                    P.wait("tensor", "sdv")
                for s in range(2):
                    for cc in range(4):
                        c = grp * 4 + cc
                        fn = lambda e, s=s, c=c, cc=cc: e.transpose(
                            PD[:, cc * 256 + s * 128: cc * 256 + s * 128 + 128],
                            n1[:, s * D + c * 128: s * D + (c + 1) * 128], identbf[:])
                        (P.opi("tensor", "spe", fn) if (s == 1 and cc == 3) else P.op("tensor", fn))
                P.wait("vector", "spe")
                for cc in range(4):
                    c = grp * 4 + cc
                    fn = lambda e, c=c, cc=cc, b=b: e.tensor_scalar(
                        htld[:, c * 256: (c + 1) * 256],
                        PD[:, cc * 256: (cc + 1) * 256],
                        s1a_col[:, b * 8 + c:b * 8 + c + 1],
                        sa_col[:, b * 8 + c:b * 8 + c + 1],
                        ALU.mult, ALU.add)
                    (P.opi("vector", "sdv", fn, drain=(cc == 0)) if cc == 3
                     else P.op("vector", fn, drain=(cc == 0)))
            M[k]["trans"] = P.mark("spe")
            M[k]["htld"] = P.mark("sdv")
            if mid is not None:
                mid()

            win_marks = {}
            for ph in range(2):
                for dd in range(8):
                    i = ph * 8 + dd
                    slot = i % 2
                    wsem = "qwa" if slot == 0 else "qwb"
                    if i >= 2:
                        P.wait("sync", "spe", win_marks[i - 2])
                    else:
                        P.wait("sync", "spe")
                    P.dma("sync", wsem, lambda e, dd=dd, slot=slot: e.dma_start(
                        out=winb[:, slot * 2048:(slot + 1) * 2048], in_=winbfr[dd]))
                    qmk = P.mark(wsem)
                    P.wait("tensor", wsem, qmk)
                    if i == 0:
                        P.wait("tensor", "sdv", M[k]["htld"])
                    if i == 8:
                        P.wait("tensor", "sdv")
                    for mm_ in range(8):
                        m = ph * 8 + mm_
                        pt = PA if mm_ < 4 else PB
                        fn = lambda e, slot=slot, m=m, mm_=mm_, dd=dd, pt=pt: e.matmul(
                            pt[:, (mm_ % 4) * 256:(mm_ % 4 + 1) * 256],
                            winb[:, slot * 2048 + m * 128: slot * 2048 + (m + 1) * 128],
                            tsl(htld, dd), start=(dd == 0 and mm_ % 2 == 0),
                            stop=(dd == 7), skip_group_check=True)
                        (P.opi("tensor", "spe", fn) if mm_ == 7 else P.op("tensor", fn))
                    win_marks[i] = P.mark("spe")
                P.wait("vector", "spe")
                for mm_ in range(8):
                    m = ph * 8 + mm_
                    dst = xh_r if m < 8 else xh_i
                    pt = PA if mm_ < 4 else PB
                    fn = lambda e, m=m, mm_=mm_, dst=dst, pt=pt: e.tensor_scalar(
                        dst[:, (m % 8) * 256:(m % 8 + 1) * 256],
                        pt[:, (mm_ % 4) * 256:(mm_ % 4 + 1) * 256],
                        bcol_in[:, m:m + 1], None, ALU.add, ALU.bypass)
                    (P.opi("vector", "sdv", fn, drain=(mm_ == 0)) if mm_ == 7
                     else P.op("vector", fn, drain=(mm_ == 0)))
            M[k]["xh"] = P.mark("sdv")

            P.wait("tensor", "sdv", M[k]["xh"])
            for rnd in range(4):
                gate = rnd // 2
                half = rnd % 2
                wg = wgx if gate == 0 else wga
                dstt = tnx if gate == 0 else tna
                bcolt = bgx2_col if gate == 0 else bga2_col
                if rnd > 0:
                    P.wait("tensor", "ssc")
                for nn in range(8):
                    n = half * 8 + nn
                    cch = n // 2
                    pbase = (n % 2) * 64
                    srcx = xh_r if n < 8 else xh_i
                    fn = lambda e, n=n, cch=cch, pbase=pbase, srcx=srcx, wg=wg: e.matmul(
                        PA[pbase:pbase + 64, (cch % 4) * 256:(cch % 4 + 1) * 256],
                        wg[:, n * 64:(n + 1) * 64],
                        srcx[:, (n % 8) * 256:(n % 8 + 1) * 256],
                        start=True, stop=True)
                    (P.opi("tensor", "spe", fn) if nn == 7 else P.op("tensor", fn))
                P.wait("scalar", "spe")
                for cc in range(4):
                    cch = half * 4 + cc
                    fn = lambda e, cc=cc, cch=cch, dstt=dstt, bcolt=bcolt: e.activation(
                        out=dstt[:, cch * 256:(cch + 1) * 256],
                        in_=PA[:, cc * 256:(cc + 1) * 256],
                        func=AF.Tanh, scale=0.5, bias=bcolt[:, cch:cch + 1])
                    (P.opi("scalar", "ssc", fn, drain=(cc == 0)) if cc == 3
                     else P.op("scalar", fn, drain=(cc == 0)))
            M[k]["tanh"] = P.mark("ssc")

        def back_pe_outproj(j, tanh_mark):
            P.wait("tensor", "ssc", tanh_mark)
            P.wait("tensor", "sdv", max(M[j]["h"], M[j - 1]["y16"] if j > 0 else 0))
            for si in range(4):
                dh = si % 2
                P.op("tensor", lambda e, si=si, dh=dh: e.matmul(
                    oslots[si], onesbf[:], bout_row[:, dh * 512:(dh + 1) * 512],
                    start=True, stop=False))
            wout_marks = {}
            for ch in range(16):
                slot = ch % 4
                vsem = "qva" if ch % 2 == 0 else "qvb"
                if ch >= 2:
                    P.wait("sync", "spe", wout_marks[ch - 2])
                else:
                    P.wait("sync", "spe")
                P.dma("sync", vsem, lambda e, ch=ch, slot=slot: e.dma_start(
                    out=woutb[:, slot * D:(slot + 1) * D], in_=woutbfr[ch]))
                qmk = P.mark(vsem)
                P.wait("tensor", vsem, qmk)
                hsrc = xtr if ch < 8 else xti
                for si in range(4):
                    s, dh = si // 2, si % 2
                    fn = lambda e, ch=ch, si=si, s=s, dh=dh, hsrc=hsrc: e.matmul(
                        oslots[si],
                        hsrc[:, (ch % 8) * 256 + s * 128: (ch % 8) * 256 + s * 128 + 128],
                        woutb[:, (ch % 4) * D + dh * 512: (ch % 4) * D + (dh + 1) * 512],
                        start=False, stop=(ch == 15))
                    (P.opi("tensor", "spe", fn) if si == 3 else P.op("tensor", fn))
                wout_marks[ch] = P.mark("spe")
            M[j]["outproj"] = P.mark("spe")

        def back_dve_early(j):
            b, kk = divmod(j, n_chunks)
            par = j % 2
            P.wait("vector", "spe", M[j]["outproj"])
            for si in range(4):
                s, dh = si // 2, si % 2
                P.op("vector", lambda e, si=si, s=s, dh=dh: e.tensor_tensor(
                    xmid[:, s * D + dh * 512: s * D + (dh + 1) * 512],
                    oslots[si], gab[:, dh * 512:(dh + 1) * 512], ALU.mult),
                    drain=(si == 0))
            for s in range(2):
                fn = lambda e, s=s, par=par: e.tensor_tensor(
                    xmid[:, s * D:(s + 1) * D], xmid[:, s * D:(s + 1) * D],
                    xkeep[par][:, s * D:(s + 1) * D], ALU.add)
                (P.opi("vector", "sdv", fn, drain=(s == 0)) if s == 1
                 else P.op("vector", fn, drain=(s == 0)))
            M[j]["xmid"] = P.mark("sdv")
            for s in range(2):
                P.op("vector", lambda e, s=s: e.bn_stats(
                    sstat[:, s * 12:s * 12 + 6], xmid[:, s * D:s * D + 512]), drain=(s == 0))
                P.op("vector", lambda e, s=s: e.bn_stats(
                    sstat[:, s * 12 + 6:s * 12 + 12], xmid[:, s * D + 512:(s + 1) * D]), drain=False)
            for s in range(2):
                fn = lambda e, s=s: e.bn_aggr(saggr[:, 4 + s * 2:6 + s * 2], sstat[:, s * 12:s * 12 + 12])
                (P.opi("vector", "sdv", fn, drain=False) if s == 1 else P.op("vector", fn, drain=True))
            M[j]["ln2stats"] = P.mark("sdv")

        def scan_prep(k):
            P.wait("vector", "ssc", M[k]["tanh"])
            P.wait("vector", "qo")
            P.op("vector", lambda e: e.tensor_scalar(gx[:], tnx[:], 0.5, 0.5, ALU.mult, ALU.add))
            for c in range(8):
                P.op("vector", lambda e, c=c: e.tensor_tensor_scan(
                    tsl(Gt, c), tsl(tna, c), tsl(tna, c), 0.0, ALU.add, ALU.bypass),
                    drain=(c == 0))
            P.op("vector", lambda e: e.tensor_tensor(Gt[:], Gt[:], rampbf[:], ALU.add))
            for c in range(8):
                P.op("vector", lambda e, c=c: e.tensor_scalar(
                    tsl(cl, c), tsl(Gt, c), Lh_col[:, c:c + 1], None, ALU.mult, ALU.bypass),
                    drain=(c == 0))
            for c in range(8):
                P.op("vector", lambda e, c=c: e.tensor_scalar(
                    tsl(Gt, c), tsl(Gt, c), aimh_col[:, c:c + 1], None, ALU.mult, ALU.bypass),
                    drain=(c == 0))
            for hf in range(2):
                gh = Gt[:, hf * 1024:(hf + 1) * 1024]
                xh_ = xsc[:, hf * 1024:(hf + 1) * 1024]
                xt_ = nrm[:, hf * 1024:(hf + 1) * 1024]
                P.op("vector", lambda e, gh=gh: e.tensor_scalar(
                    riw[:], gh, 1.0 / (2 * PI), None, ALU.mult, ALU.bypass))
                P.op("vector", lambda e, gh=gh: e.scalar_tensor_tensor(
                    gh, riw[:], -2 * PI, gh, ALU.mult, ALU.add))
                P.op("vector", lambda e, gh=gh, xh_=xh_: e.tensor_scalar(
                    xh_, gh, PI, None, ALU.is_gt, ALU.bypass))
                P.op("vector", lambda e, gh=gh, xt_=xt_: e.tensor_scalar(
                    xt_, gh, -PI, None, ALU.is_lt, ALU.bypass), drain=False)
                P.op("vector", lambda e, xh_=xh_, xt_=xt_: e.tensor_tensor(
                    xh_, xh_, xt_, ALU.subtract))
                fn = lambda e, gh=gh, xh_=xh_, hf=hf: e.scalar_tensor_tensor(
                    thr[:, hf * 1024:(hf + 1) * 1024], xh_, -2 * PI, gh, ALU.mult, ALU.add)
                (P.opi("vector", "sdv", fn) if hf == 1 else P.op("vector", fn))
            M[k]["thr"] = P.mark("sdv")

        def sc_ln2rows(j):
            P.wait("scalar", "sdv", M[j]["ln2stats"])
            for s in range(2):
                P.op("scalar", lambda e, s=s: e.activation(
                    out=srow[:, 4 + s:5 + s], in_=saggr[:, 5 + s * 2:6 + s * 2], func=AF.Ln, bias=cbias[:, 0:1]),
                    drain=(s == 0))
            for s in range(2):
                fn = lambda e, s=s: e.activation(
                    out=srow[:, 4 + s:5 + s], in_=srow[:, 4 + s:5 + s], func=AF.Exp, scale=-0.5)
                (P.opi("scalar", "ssc", fn, drain=False) if s == 1 else P.op("scalar", fn, drain=True))
            M[j]["ln2rows"] = P.mark("ssc")

        def back_dve_n2(j):
            P.wait("vector", "ssc", M[j]["ln2rows"])
            for s in range(2):
                P.op("vector", lambda e, s=s: e.tensor_tensor(
                    srow[:, 6 + s:7 + s], saggr[:, 4 + s * 2:5 + s * 2], srow[:, 4 + s:5 + s], ALU.mult),
                    drain=(s == 0))
            for s in range(2):
                fn = lambda e, s=s: e.tensor_scalar(
                    Srt[:, s * D:(s + 1) * D], xmid[:, s * D:(s + 1) * D],
                    srow[:, 4 + s:5 + s], srow[:, 6 + s:7 + s], ALU.mult, ALU.subtract)
                (P.opi("vector", "sdv", fn, drain=False) if s == 1 else P.op("vector", fn, drain=True))
            M[j]["n2"] = P.mark("sdv")

        def sc_exps(k):
            P.wait("scalar", "sdv", M[k]["thr"])
            P.op("scalar", lambda e: e.activation(out=epos[:], in_=cl[:], func=AF.Exp))
            P.opi("scalar", "ssc", lambda e: e.activation(out=eneg[:], in_=cl[:], func=AF.Exp, scale=-1.0),
                  drain=False)
            for c in range(8):
                fn = lambda e, c=c: e.activation(
                    out=tsl(cl, c), in_=tsl(tna, c), func=AF.Exp,
                    scale=L2_col[:, c:c + 1], bias=L2_col[:, c:c + 1])
                (P.opi("scalar", "ssc", fn, drain=(c == 0)) if c == 7
                 else P.op("scalar", fn, drain=(c == 0)))
            sc_m2 = P.mark("ssc")
            P.wait("vector", "ssc", sc_m2)
            P.op("vector", lambda e: e.tensor_scalar(cl[:], cl[:], -1.0, 1.0, ALU.mult, ALU.add))
            P.opi("vector", "sdv", lambda e: e.tensor_scalar(cl[:], cl[:], 1e-6, None, ALU.max, ALU.bypass))
            dv_w = P.mark("sdv")
            P.wait("scalar", "sdv", dv_w)
            P.op("scalar", lambda e: e.activation(out=Gt[:], in_=cl[:], func=AF.Ln))
            P.opi("scalar", "ssc", lambda e: e.activation(out=nrm[:], in_=Gt[:], func=AF.Exp, scale=0.5))
            M[k]["nrm"] = P.mark("ssc")

        def sc_trig(k):
            P.wait("scalar", "spo")
            P.op("scalar", lambda e: e.activation(out=y16[:], in_=thr[:], func=AF.Abs))
            P.op("scalar", lambda e: e.activation(out=tnx[:], in_=y16[:], func=AF.Sin, scale=-1.0, bias=cbias[:, 2:3]))
            P.opi("scalar", "ssc", lambda e: e.activation(out=tna[:], in_=thr[:], func=AF.Sin), drain=False)
            M[k]["trig"] = P.mark("ssc")

        def back_pe_trans(j):
            b, kk = divmod(j, n_chunks)
            P.wait("tensor", "sdv", M[j]["n2"])
            for grp in range(2):
                P.wait("tensor", "sdv")
                for s in range(2):
                    for cc in range(4):
                        c = grp * 4 + cc
                        fn = lambda e, s=s, c=c, cc=cc: e.transpose(
                            PD[:, cc * 256 + s * 128: cc * 256 + s * 128 + 128],
                            Srt[:, s * D + c * 128: s * D + (c + 1) * 128], identbf[:])
                        (P.opi("tensor", "spe", fn) if (s == 1 and cc == 3) else P.op("tensor", fn))
                P.wait("vector", "spe")
                for cc in range(4):
                    c = grp * 4 + cc
                    fn = lambda e, c=c, cc=cc, b=b: e.tensor_scalar(
                        h2q[:, c * 256: (c + 1) * 256],
                        PD[:, cc * 256: (cc + 1) * 256],
                        s1b_col[:, b * 8 + c:b * 8 + c + 1],
                        sbb_col[:, b * 8 + c:b * 8 + c + 1],
                        ALU.mult, ALU.add)
                    (P.opi("vector", "sdv", fn, drain=(cc == 0)) if cc == 3
                     else P.op("vector", fn, drain=(cc == 0)))
            M[j]["ln2tr"] = P.mark("spe")
            M[j]["h2q"] = P.mark("sdv")

        def back_pe_ffn(j):
            P.wait("tensor", "sdv", M[j]["h2q"])
            for si in range(4):
                dh = si % 2
                P.op("tensor", lambda e, si=si, dh=dh: e.matmul(
                    f2slots[si], onesbf[:], b2q_row[:, dh * 512:(dh + 1) * 512],
                    start=True, stop=False))
            relu_marks = {}
            cast_marks = {}
            for g in range(4):
                for pr in range(2):
                    r = g * 2 + pr
                    if r >= 1:
                        P.wait("tensor", "ssc", relu_marks[r - 1])
                    for q in range(4):
                        hh = r * 4 + q
                        for dd in range(8):
                            fn = lambda e, hh=hh, q=q, dd=dd: e.matmul(
                                PA[:, q * 256:(q + 1) * 256],
                                w1T[:, dd * H + hh * 128: dd * H + (hh + 1) * 128],
                                tsl(h2q, dd), start=(dd == 0 and q % 2 == 0),
                                stop=(dd == 7), skip_group_check=True)
                            (P.opi("tensor", "spe", fn) if (q == 3 and dd == 7) else P.op("tensor", fn))
                    P.wait("scalar", "spe")
                    for q in range(4):
                        hh = r * 4 + q
                        fn = lambda e, hh=hh, q=q: e.activation(
                            out=y16[:, (hh % 8) * 256:(hh % 8 + 1) * 256],
                            in_=PA[:, q * 256:(q + 1) * 256],
                            func=AF.Relu, bias=b1q_col[:, hh:hh + 1])
                        (P.opi("scalar", "ssc", fn, drain=(q == 0)) if q == 3
                         else P.op("scalar", fn, drain=(q == 0)))
                    relu_marks[r] = P.mark("ssc")
                    P.wait("gpsimd", "ssc", relu_marks[r])
                    P.wait("gpsimd", "spe")
                    hf = r % 2
                    P.opi("gpsimd", "spo", lambda e, hf=hf: e.tensor_copy(
                        r1q[:, hf * 1024:(hf + 1) * 1024],
                        y16[:, hf * 1024:(hf + 1) * 1024]), drain=True)
                    cast_marks[r] = P.mark("spo")
                for sub in range(2):
                    P.wait("tensor", "spo", cast_marks[g * 2 + sub])
                    for hh4 in range(4):
                        hh8 = sub * 4 + hh4
                        hh = g * 8 + hh8
                        for si in range(4):
                            s, dh = si // 2, si % 2
                            fn = lambda e, hh=hh, hh8=hh8, si=si, s=s, dh=dh: e.matmul(
                                f2slots[si],
                                r1q[:, hh8 * 256 + s * 128: hh8 * 256 + s * 128 + 128],
                                w2T[:, hh * D + dh * 512: hh * D + (dh + 1) * 512],
                                start=False, stop=(hh == 31))
                            (P.opi("tensor", "spe", fn) if (si == 3 and hh4 == 3) else P.op("tensor", fn))
            M[j]["ffn2"] = P.mark("spe")

        def scan_main(k):
            b, kk = divmod(k, n_chunks)
            P.wait("vector", "ssc", M[k]["trig"])
            if k >= 1:
                P.wait("vector", "spe", M[k - 1]["ln2tr"])
            P.op("vector", lambda e: e.tensor_tensor(gx[:], gx[:], nrm[:], ALU.mult))
            P.op("vector", lambda e: e.tensor_tensor(eneg[:], gx[:], eneg[:], ALU.mult))
            P.op("vector", lambda e: e.tensor_tensor(xtr[:], xh_r[:], tnx[:], ALU.mult))
            P.op("vector", lambda e: e.tensor_tensor(xti[:], xh_i[:], tna[:], ALU.mult))
            P.op("vector", lambda e: e.tensor_tensor(xtr[:], xtr[:], xti[:], ALU.add))
            P.op("vector", lambda e: e.tensor_tensor(xtr[:], xtr[:], eneg[:], ALU.mult))
            P.op("vector", lambda e: e.tensor_tensor(xti[:], xh_i[:], tnx[:], ALU.mult))
            P.op("vector", lambda e: e.tensor_tensor(xsc[:], xh_r[:], tna[:], ALU.mult))
            P.op("vector", lambda e: e.tensor_tensor(xti[:], xti[:], xsc[:], ALU.subtract))
            P.op("vector", lambda e: e.tensor_tensor(xti[:], xti[:], eneg[:], ALU.mult))
            for c in range(8):
                init_r = 0.0 if kk == 0 else hcarry[:, c:c + 1]
                init_i = 0.0 if kk == 0 else hcarry[:, 8 + c:9 + c]
                P.op("vector", lambda e, c=c, init_r=init_r: e.tensor_tensor_scan(
                    tsl(Srt, c), tsl(xtr, c), tsl(xtr, c), init_r, ALU.add, ALU.bypass),
                    drain=(c == 0))
                P.op("vector", lambda e, c=c, init_i=init_i: e.tensor_tensor_scan(
                    tsl(Sit, c), tsl(xti, c), tsl(xti, c), init_i, ALU.add, ALU.bypass),
                    drain=False)
            P.op("vector", lambda e: e.tensor_tensor(tnx[:], tnx[:], epos[:], ALU.mult))
            P.op("vector", lambda e: e.tensor_tensor(tna[:], tna[:], epos[:], ALU.mult))
            P.op("vector", lambda e: e.tensor_tensor(xtr[:], tnx[:], Srt[:], ALU.mult))
            P.op("vector", lambda e: e.tensor_tensor(xsc[:], tna[:], Sit[:], ALU.mult))
            P.op("vector", lambda e: e.tensor_tensor(xtr[:], xtr[:], xsc[:], ALU.subtract))
            P.op("vector", lambda e: e.tensor_tensor(xti[:], tnx[:], Sit[:], ALU.mult))
            P.op("vector", lambda e: e.tensor_tensor(xsc[:], tna[:], Srt[:], ALU.mult))
            P.op("vector", lambda e: e.tensor_tensor(xti[:], xti[:], xsc[:], ALU.add))
            P.op("vector", lambda e: e.tensor_copy(hcarry[:, 0:8], xtr[:, 255:2048:256]))
            P.opi("vector", "sdv", lambda e: e.tensor_copy(hcarry[:, 8:16], xti[:, 255:2048:256]), drain=False)
            M[k]["h"] = P.mark("sdv")

        def back_dve_tail_casts(j):
            P.wait("vector", "spe", M[j]["ffn2"])
            for si in range(2):
                dh = si % 2
                fn = lambda e, si=si, dh=dh: e.tensor_copy(
                    y16[:, dh * 512: (dh + 1) * 512], f2slots[si])
                (P.op("vector", fn, drain=True) if si == 0
                 else P.op("vector", fn, drain=False))

        def back_dve_tail_casts_pc(j):
            par = j % 2
            for si in range(2, 4):
                dh = si % 2
                fn = lambda e, si=si, dh=dh: e.tensor_copy(
                    y16[:, D + dh * 512: D + (dh + 1) * 512], f2slots[si])
                (P.opi("vector", "sdv", fn, drain=False) if si == 3
                 else P.op("vector", fn, drain=True))
            M[j]["y16"] = P.mark("sdv")
            P.wait("gpsimd", "sdv", M[j]["y16"])
            P.wait("gpsimd", M[j]["qm"][0], M[j]["qm"][1])
            P.op("gpsimd", lambda e, par=par: e.tensor_tensor(
                y16[:], y16[:], smaskp[par][:], ALU.mult), drain=True)
            P.op("gpsimd", lambda e: e.tensor_tensor(y16[:, 0:D], y16[:, 0:D], gbb[:], ALU.mult), drain=True)
            P.opi("gpsimd", "spo", lambda e: e.tensor_tensor(y16[:, D:], y16[:, D:], gbb[:], ALU.mult), drain=True)
            M[j]["po_y"] = P.mark("spo")

        def back_dve_tail_store(j):
            P.wait("vector", "spo", M[j]["po_y"])
            P.wait("vector", "qo")
            P.op("vector", lambda e: e.tensor_tensor(cl[:, 0:D], y16[:, 0:D], xmid[:, 0:D], ALU.add), drain=True)
            P.opi("vector", "sdv", lambda e: e.tensor_tensor(cl[:, D:], y16[:, D:], xmid[:, D:], ALU.add), drain=False)
            M[j]["out"] = P.mark("sdv")
            b, kk = divmod(j, n_chunks)
            P.wait("gpsimd", "sdv", M[j]["out"])
            P.dma("gpsimd", "qo", lambda e, b=b, kk=kk: e.dma_start(out=yr4[b, kk * 2], in_=cl[:, 0:D]))
            P.dma("gpsimd", "qo", lambda e, b=b, kk=kk: e.dma_start(out=yr4[b, kk * 2 + 1], in_=cl[:, D:]))

        pre()
        if NK > 0:
            batch_setup(0)
            dma_x(0)
            dma_mask(0, None)
            dve_stats_ln1(0)
            sc_ln1_rows(0)
            dve_n1(0, None)
            for kslot in range(NK + 2):
                k = kslot
                j = kslot - 1
                jj = kslot - 2
                if 0 <= jj < NK:
                    back_dve_tail_casts(jj)
                if k < NK:
                    front(k, mid=(lambda jj=jj: back_dve_tail_casts_pc(jj)) if 0 <= jj < NK else None)
                elif 0 <= jj < NK:
                    back_dve_tail_casts_pc(jj)
                if 0 <= jj < NK:
                    back_dve_tail_store(jj)
                if 1 <= k < NK:
                    dma_mask(k, M[k - 2]["po_y"] if k >= 2 else None)
                if k < NK:
                    bk, kkk = divmod(k, n_chunks)
                    if kkk == 1 and bk > 0:
                        batch_setup(bk)
                if 0 <= j < NK:
                    back_pe_outproj(j, M[min(k, NK - 1)]["tanh"])
                if k < NK:
                    scan_prep(k)
                if 0 <= j < NK:
                    back_dve_early(j)
                if k + 1 < NK:
                    dma_x(k + 1)
                    dve_stats_ln1(k + 1)
                if 0 <= j < NK:
                    sc_ln2rows(j)
                if k + 1 < NK:
                    sc_ln1_rows(k + 1)
                if 0 <= j < NK:
                    back_dve_n2(j)
                    back_pe_trans(j)
                if k + 1 < NK:
                    dve_n1(k + 1, M[k]["trans"])
                if k < NK:
                    sc_exps(k)
                    sc_trig(k)
                if 0 <= j < NK:
                    back_pe_ffn(j)
                if k < NK:
                    scan_main(k)
        P.wait("gpsimd", "qo")

        @block.sync
        def _(e):
            for f in P.ops["sync"]:
                f(e)

        @block.gpsimd
        def _(e):
            for f in P.ops["gpsimd"]:
                f(e)

        @block.vector
        def _(e):
            for f in P.ops["vector"]:
                f(e)

        @block.scalar
        def _(e):
            for f in P.ops["scalar"]:
                f(e)

        @block.tensor
        def _(e):
            for f in P.ops["tensor"]:
                f(e)

    return nc


_CACHE = {}


def _host_constants():
    if "mask" not in _CACHE:
        import jax
        cpu = jax.local_devices(backend="cpu")[0]
        with jax.default_device(cpu):
            keep = np.asarray(jax.random.bernoulli(
                jax.random.key(42, impl="threefry2x32"), 0.8, (32, S, D)))
        _CACHE["mask"] = (keep.astype(np.float32) * np.float32(1.25)).astype(
            ml_dtypes.float8_e4m3)
        _CACHE["ident"] = np.eye(128, dtype=np.float32)
        _CACHE["ramp"] = np.tile(np.arange(1, T + 1, dtype=np.float32), (128, 8))
    return _CACHE["mask"], _CACHE["ident"], _CACHE["ramp"]


def kernel(**inputs):
    mask, ident, ramp = _host_constants()
    if "nc" not in _CACHE:
        _CACHE["nc"] = build_program()
    nc = _CACHE["nc"]

    full = {
        "wmod": inputs["L0_wmod"], "W_in": inputs["L0_attn_W_in"],
        "b_in": inputs["L0_attn_b_in"], "Wg_x": inputs["L0_attn_Wg_x"],
        "bg_x": inputs["L0_attn_bg_x"], "Wg_a": inputs["L0_attn_Wg_a"],
        "bg_a": inputs["L0_attn_bg_a"], "a_real": inputs["L0_attn_a_real_param"],
        "a_imag": inputs["L0_attn_a_imag_param"], "W_out": inputs["L0_attn_W_out"],
        "b_out": inputs["L0_attn_b_out"], "w1": inputs["L0_ffn_w1"],
        "b1": inputs["L0_ffn_b1"], "w2": inputs["L0_ffn_w2"],
        "b2": inputs["L0_ffn_b2"], "ident": ident, "ramp": ramp,
    }
    full = {k: np.ascontiguousarray(np.asarray(v, dtype=np.float32))
            for k, v in full.items()}
    x = np.asarray(inputs["L0_attn_x"], dtype=np.float32)
    cv = np.asarray(inputs["L0_c"], dtype=np.float32)

    in_maps = []
    for core in range(NCORES):
        m = dict(full)
        m["x"] = np.ascontiguousarray(x[core * BC:(core + 1) * BC])
        m["cvec"] = np.ascontiguousarray(cv[core * BC:(core + 1) * BC])
        m["mask"] = np.ascontiguousarray(mask[core * BC:(core + 1) * BC])
        in_maps.append(m)

    res = run_bass_kernel_spmd(nc, in_maps, core_ids=list(range(NCORES)))
    return np.concatenate([r["y"] for r in res.results], axis=0)


if __name__ == "__main__":
    build_program()
    print("program built ok")
